# version 10
# baseline (speedup 1.0000x reference)
import numpy as np
import ml_dtypes
from contextlib import ExitStack

import concourse.bass as bass
import concourse.mybir as mybir
from concourse.bass_utils import run_bass_kernel_spmd
from concourse.alu_op_type import AluOpType as OP

F32 = mybir.dt.float32
BF16 = mybir.dt.bfloat16
I32 = mybir.dt.int32
AF = mybir.ActivationFunctionType
AX = mybir.AxisListType

T = 4096
D = 1024
NCORE = 8
EPS = 1e-5

D_IN = 2048
D_ST = 128
NH = 32
HD = 64
CH = 256
NCH = T // CH
CONVD = D_IN + 2 * D_ST
D_PROJ = 2 * D_IN + 2 * D_ST + NH

AH = 16
AHD = 64
LAT = 32

STAGES = ("A", "B", "C", "D")
DEBUG = False
FFN_CHUNKS = None
_DBG = {}

BIG = 1e30
_ROT = [(i + 1 if i % 2 == 0 else i - 1) for i in range(32)]


ENGINES = ("sync", "tensor", "vector", "scalar", "gpsimd")


PARANOID = False
SCOPES = False
SCHED = True
DMA_PIN = ("x", "ax", "cp")


class Prog:
    def __init__(self, nc):
        self.nc = nc
        self.ops = []
        self.last_w = {}
        self.readers = {}
        self._last_q = {}
        self._open = set()
        self._pin = None
        self.scope = None

    def add(self, eng, fn, reads=(), writes=(), step=None):
        idx = len(self.ops)
        deps = set()
        for k in reads:
            j = self.last_w.get(k, self._pin)
            if j is not None:
                deps.add(j)
        for k in writes:
            j = self.last_w.get(k, self._pin)
            if j is not None:
                deps.add(j)
            for r in self.readers.get(k, ()):
                deps.add(r)
        if step is None:
            step = 16 if eng in ("sync", "gpsimd") else 1
        self.ops.append(dict(idx=idx, eng=eng, fn=fn, deps=deps,
                             step=step, scope=self.scope, w=tuple(writes)))
        self._open.difference_update(deps)
        self._open.add(idx)
        for k in reads:
            self.readers.setdefault(k, []).append(idx)
        for k in writes:
            self.last_w[k] = idx
            self.readers[k] = []
        return idx

    def dma(self, out, in_, reads=(), writes=(), eng="sync"):
        return self.add(eng, lambda e, o=out, i=in_: e.dma_start(out=o, in_=i),
                        reads=reads, writes=writes, step=16)

    def barrier(self):
        deps = set(self._open)
        pin = None
        for e in ENGINES:
            self.add(e, lambda eng: eng.nop(), reads=(), writes=(), step=1)
            self.ops[-1]["deps"] = deps - {self.ops[-1]["idx"]}
            self._open.difference_update(deps)
            if e == "sync":
                pin = self.ops[-1]["idx"]
        for k in list(self.last_w):
            self.last_w[k] = pin
            self.readers[k] = []
        self._pin = pin

    def emit(self, block, sems):
        ops = self.ops
        import heapq
        n = len(ops)
        succ = [[] for _ in range(n)]
        for op in ops:
            for dd in op["deps"]:
                succ[dd].append(op["idx"])

        def est(op):
            if op["step"] == 16:
                return 1400
            return {"tensor": 350, "vector": 500, "scalar": 700,
                    "sync": 1400, "gpsimd": 1400}[op["eng"]]
        prio = [0] * n
        for op in reversed(ops):
            i = op["idx"]
            mx = 0
            for j in succ[i]:
                if prio[j] > mx:
                    mx = prio[j]
            prio[i] = est(op) + mx
        if SCHED:
            if DMA_PIN:
                for op in ops:
                    if (op["step"] == 16 and
                            any(k.startswith(p) for p in DMA_PIN
                                for k in (op["w"] or ("?",)))):
                        prio[op["idx"]] = -op["idx"]
            indeg = [len(op["deps"]) for op in ops]
            ready = [(-prio[op["idx"]], op["idx"]) for op in ops
                     if indeg[op["idx"]] == 0]
            heapq.heapify(ready)
            order = []
            while ready:
                _, i = heapq.heappop(ready)
                order.append(i)
                for j in succ[i]:
                    indeg[j] -= 1
                    if indeg[j] == 0:
                        heapq.heappush(ready, (-prio[j], j))
            assert len(order) == n
        else:
            order = list(range(n))
        lastq = {}
        for i in order:
            e = ops[i]["eng"]
            if e in ("sync", "gpsimd"):
                j = lastq.get(e)
                if j is not None:
                    ops[i]["deps"].add(j)
                lastq[e] = i
        byidx = ops
        ops = [ops[i] for i in order]
        need_inc = set()
        for op in ops:
            if op["step"] == 16:
                need_inc.add(op["idx"])
            for d in op["deps"]:
                if (byidx[d]["eng"] != op["eng"] or byidx[d]["step"] == 16
                        or op["eng"] != "tensor" or PARANOID):
                    need_inc.add(d)
        inc_val = {}
        cum = {e: 0 for e in ENGINES}
        for op in ops:
            if op["idx"] in need_inc:
                cum[op["eng"]] += op["step"]
                inc_val[op["idx"]] = cum[op["eng"]]
        per_eng = {e: [] for e in ENGINES}
        for op in ops:
            per_eng[op["eng"]].append(op)

        def make(e):
            eng_ops = per_eng[e]

            def body(eng):
                hw = {p: 0 for p in ENGINES}
                cur = [None, None]

                def set_scope(sc):
                    if not SCOPES or sc == cur[0]:
                        return
                    if cur[0] is not None:
                        self.nc.leave_named_scope(cur[0], cur[1], False)
                    cur[0], cur[1] = sc, None
                    if sc is not None:
                        cur[1], _ = self.nc.enter_named_scope(sc, False)
                for op in eng_ops:
                    set_scope(op["scope"])
                    for d in sorted(op["deps"]):
                        dop = byidx[d]
                        if (dop["eng"] == e and dop["step"] != 16
                                and e == "tensor" and not PARANOID):
                            continue
                        v = inc_val[d]
                        if v > hw[dop["eng"]]:
                            eng.wait_ge(sems[dop["eng"]], v)
                            hw[dop["eng"]] = v
                    inst = op["fn"](eng)
                    if op["idx"] in need_inc:
                        inst.then_inc(sems[op["eng"]], op["step"])
                set_scope(None)
            return body

        for e in ENGINES:
            if per_eng[e]:
                getattr(block, e)(make(e))


def _rope_tables():
    inv_freq = 1.0 / 10000.0 ** (np.arange(0, LAT, 2, dtype=np.float32) / LAT)
    t = np.arange(T, dtype=np.float32)
    freqs = np.outer(t, inv_freq)
    cos = np.repeat(np.cos(freqs), 2, axis=-1)[:, :LAT]
    sin = np.repeat(np.sin(freqs), 2, axis=-1)[:, :LAT]
    sinp = sin.copy()
    sinp[:, 0::2] = -sin[:, 0::2]
    cos4 = np.tile(cos.T, (4, 1))
    sin4 = np.tile(sinp.T, (4, 1))
    return (cos4.astype(ml_dtypes.bfloat16), sin4.astype(ml_dtypes.bfloat16))


def host_prep(I):
    f32 = np.float32
    sh = {}
    W_inT = np.ascontiguousarray(I["L0_attn_a_W_in"].T)
    sh["WzdtT"] = np.ascontiguousarray(
        np.concatenate([W_inT[:, :D_IN], W_inT[:, D_IN + CONVD:]], axis=1))
    sh["WxbcT"] = np.ascontiguousarray(W_inT[:, D_IN:D_IN + CONVD])
    sh["WoutT"] = np.ascontiguousarray(I["L0_attn_a_W_out"].T)
    cw = I["L0_attn_a_conv_w"]
    diag = np.zeros((18 * 4, 128, 128), f32)
    idx = np.arange(128)
    for n in range(18):
        for k in range(4):
            diag[n * 4 + k, idx, idx] = cw[n * 128:(n + 1) * 128, k]
    sh["convdiag"] = diag
    sh["convbcols"] = np.ascontiguousarray(I["L0_attn_a_conv_b"].reshape(18, 128).T)
    sh["dtbias_col"] = I["L0_attn_a_dt_bias"].reshape(NH, 1).copy()
    sh["Alog_col"] = I["L0_attn_a_A_log"].reshape(NH, 1).copy()
    sh["D_col"] = I["L0_attn_a_D"].reshape(NH, 1).copy()
    sh["normw_cols"] = np.ascontiguousarray(I["L0_attn_a_norm_w"].reshape(16, 128).T)
    sh["fc1_q"] = I["L0_ffn_fc1_q"]
    sh["fc2_q"] = I["L0_ffn_fc2_q"]
    sh["fc1_scales"] = I["L0_ffn_fc1_scales"]
    sh["fc2_scales"] = I["L0_ffn_fc2_scales"]
    sh["b1cols"] = np.ascontiguousarray(I["L0_ffn_fc1_bias"].reshape(32, 128).T)
    sh["b2row"] = I["L0_ffn_fc2_bias"].reshape(1, D).copy()
    sh["gcols"] = np.ascontiguousarray(I["L0_ffn_ln_gamma"].reshape(8, 128).T)
    sh["bcols"] = np.ascontiguousarray(I["L0_ffn_ln_beta"].reshape(8, 128).T)
    sh["c_attn_w"] = I["L0_attn_b_c_attn_w"]
    sh["cab_cols"] = np.ascontiguousarray(I["L0_attn_b_c_attn_b"].reshape(24, 128).T)
    q2l = I["L0_attn_b_q2l_w"]
    kv2l = I["L0_attn_b_kv2l_w"]
    bd2 = np.zeros((128, 64), f32)
    bd2[0:64, 0:32] = q2l.T
    bd2[64:128, 32:64] = q2l.T
    sh["q2l_bd2"] = bd2
    bd2k = np.zeros((128, 64), f32)
    bd2k[0:64, 0:32] = kv2l.T
    bd2k[64:128, 32:64] = kv2l.T
    sh["kv2l_bd2"] = bd2k
    wqk = I["L0_attn_b_wqk_w"]
    bd4 = np.zeros((128, 128), f32)
    for i in range(4):
        bd4[32 * i:32 * i + 32, 32 * i:32 * i + 32] = wqk.T
    sh["wqk_bd4"] = bd4
    sh["qlb_t"] = np.tile(I["L0_attn_b_q2l_b"], 4).reshape(128, 1).copy()
    sh["kvlb_t"] = np.tile(I["L0_attn_b_kv2l_b"], 4).reshape(128, 1).copy()
    sh["wqkb_t"] = np.tile(I["L0_attn_b_wqk_b"], 4).reshape(128, 1).copy()
    l2o = I["L0_attn_b_l2o_w"]
    sh["cprojT"] = np.ascontiguousarray(I["L0_attn_b_cproj_w"].T)
    sh["cprojb_row"] = I["L0_attn_b_cproj_b"].reshape(1, D).copy()
    sh["ident"] = np.eye(128, dtype=f32)
    sh["identb"] = np.eye(128, dtype=f32).astype(ml_dtypes.bfloat16)
    step = (np.arange(128)[:, None] < np.arange(128)[None, :]).astype(f32)
    sh["stepb"] = step.astype(ml_dtypes.bfloat16)
    sh["negbigeye"] = (-BIG * np.eye(128)).astype(ml_dtypes.bfloat16)
    sel = np.zeros((32, 32 * 256), f32)
    for h in range(32):
        sel[h, 256 * h:256 * (h + 1)] = 1.0
    sh["selb"] = sel.astype(ml_dtypes.bfloat16)
    sh["ones_row"] = np.ones((1, 128), f32)
    selc0 = np.zeros((128, 8), f32)
    for k in range(24):
        selc0[k, k % 8] = 1.0
    sh["selc0_h"] = selc0.astype(ml_dtypes.bfloat16)
    selbc = np.zeros((128, 1024), f32)
    for k in range(24):
        hl = k % 8
        selbc[k, 128 * hl:128 * (hl + 1)] = -1.0
    sh["selbc_h"] = selbc.astype(ml_dtypes.bfloat16)
    sh["negones_rowb"] = (-np.ones((1, 128))).astype(ml_dtypes.bfloat16)
    cos4, sin4 = _rope_tables()
    sh["rope_cos"] = cos4
    sh["rope_sin"] = sin4
    qm = np.zeros((4, 128, 512), f32)
    kk = np.arange(128)[:, None]; qq = np.arange(512)[None, :]
    for j in range(4):
        qm[j] = (kk > qq - 128 * j)
    sh["qmask"] = qm.astype(ml_dtypes.bfloat16)
    sh["negones_sq"] = (-np.ones((128, 128))).astype(ml_dtypes.bfloat16)
    l2o = I["L0_attn_b_l2o_w"]
    l2ob = I["L0_attn_b_l2o_b"].reshape(1, 64)
    sh["l2o_extA"] = np.concatenate([l2o.T, l2ob], axis=0).copy()
    sh["l2o_extB"] = np.concatenate([l2ob, l2o.T], axis=0).copy()
    per_core = []
    for b in range(NCORE):
        per_core.append({"x_in": np.ascontiguousarray(I["L0_attn_a_inputs"][b])})
    return sh, per_core


def build_nc():
    nc = bass.Bass("TRN2", target_bir_lowering=False)
    dr = {}

    def din(name, shape, dt=F32):
        dr[name] = nc.dram_tensor(name, list(shape), dt, kind="ExternalInput").ap()

    din("x_in", (T, D))
    din("WzdtT", (D, 2080)); din("WxbcT", (D, CONVD)); din("WoutT", (D_IN, D))
    din("convdiag", (72, 128, 128)); din("convbcols", (128, 18))
    din("dtbias_col", (NH, 1)); din("Alog_col", (NH, 1)); din("D_col", (NH, 1))
    din("normw_cols", (128, 16))
    din("fc1_q", (D, 4096), I32); din("fc2_q", (4096, D), I32)
    din("fc1_scales", (8, 4096)); din("fc2_scales", (32, D))
    din("b1cols", (128, 32)); din("b2row", (1, D))
    din("gcols", (128, 8)); din("bcols", (128, 8))
    din("c_attn_w", (3072, D)); din("cab_cols", (128, 24))
    din("q2l_bd2", (128, 64)); din("kv2l_bd2", (128, 64)); din("wqk_bd4", (128, 128))
    din("qlb_t", (128, 1)); din("kvlb_t", (128, 1)); din("wqkb_t", (128, 1))
    din("l2o_extA", (33, 64)); din("l2o_extB", (33, 64))
    din("cprojT", (D, D)); din("cprojb_row", (1, D))
    din("qmask", (4, 128, 512), BF16); din("negones_sq", (128, 128), BF16)
    din("ident", (128, 128)); din("identb", (128, 128), BF16)
    din("stepb", (128, 128), BF16); din("negbigeye", (128, 128), BF16)
    din("selb", (32, 32 * 256), BF16)
    din("ones_row", (1, 128)); din("negones_rowb", (1, 128), BF16)
    din("selc0_h", (128, 8), BF16); din("selbc_h", (128, 1024), BF16)
    din("rope_cos", (128, T), BF16); din("rope_sin", (128, T), BF16)

    dr["xio"] = nc.dram_tensor("xio", [T, D], F32, kind="ExternalOutput").ap()
    dr["w1bf"] = nc.dram_tensor("w1bf", [128, 8 * 4096], BF16, kind="Internal").ap()
    dr["w2bf"] = nc.dram_tensor("w2bf", [128, 32 * D], BF16, kind="Internal").ap()

    prog = Prog(nc)
    with ExitStack() as ctx:
        sems = {e: ctx.enter_context(nc.semaphore(f"sem_{e}")) for e in ENGINES}
        g = {}

        def gsb(name, shape, dt=F32):
            t_ = ctx.enter_context(nc.sbuf_tensor("g_" + name, list(shape), dt))
            g[name] = t_
            return t_

        gsb("ident", (128, 128)); gsb("identb", (128, 128), BF16)
        gsb("ones_row", (1, 128))
        gsb("onesb_row", (1, 128), BF16)
        gsb("negones_rowb", (1, 128), BF16)
        gsb("ones_col", (128, 1))
        for n in ("ident", "identb", "ones_row"):
            prog.dma(g[n][:, :], dr[n][:, :], writes=(n,))
        prog.add("vector", lambda e: e.memset(g["ones_col"][:, :], 1.0),
                 writes=("ones_col",))
        prog.add("vector", lambda e: e.tensor_copy(out=g["onesb_row"][:, :],
                                                   in_=g["ones_row"][:, :]),
                 reads=("ones_row",), writes=("onesb_row",))
        prog.dma(g["negones_rowb"][:, :], dr["negones_rowb"][:, :],
                 writes=("negones_rowb",))

        if "A" in STAGES:
            prog.scope = "stA"
            with ExitStack() as sctx:
                stage_a(nc, prog, dr, g, sctx)
        else:
            stage_copy(nc, prog, dr, g, ctx)
        prog.barrier()
        if "B" in STAGES:
            prog.scope = "stB"
            with ExitStack() as sctx:
                stage_ffn(nc, prog, dr, g, sctx, spill=("D" in STAGES))
            prog.barrier()
        if "C" in STAGES:
            prog.scope = "stC"
            with ExitStack() as sctx:
                stage_attn(nc, prog, dr, g, sctx)
            prog.barrier()
        if "D" in STAGES:
            prog.scope = "stD"
            with ExitStack() as sctx:
                stage_ffn(nc, prog, dr, g, sctx, reload=True)
            prog.barrier()

        blk = ctx.enter_context(nc.Block())
        prog.emit(blk, sems)
    return nc


def stage_copy(nc, prog, dr, g, ctx):
    buf = ctx.enter_context(nc.sbuf_tensor("cp", [128, 2 * D], F32))
    for i in range(T // 128):
        p = i % 2
        key = f"cp{p}"
        prog.dma(buf[:, p * D:(p + 1) * D], dr["x_in"][i * 128:(i + 1) * 128, :],
                 writes=(key,))
        prog.dma(dr["xio"][i * 128:(i + 1) * 128, :], buf[:, p * D:(p + 1) * D],
                 reads=(key,), writes=(), eng="gpsimd")


def dbg_dump(nc, prog, dr, name, ap, shape, dt, keys):
    if not DEBUG or name in _DBG:
        return
    _DBG[name] = True
    t = nc.dram_tensor("dbg_" + name, list(shape), dt, kind="ExternalOutput").ap()
    prog.dma(t[:, :] if len(shape) == 2 else t[:], ap, reads=tuple(keys), writes=())


def emit_ln_stats(prog, st6, agg, comb, xts, xkeys, kp, fused):
    K6, KA, KC = kp + "st6", kp + "agg", kp + "comb"
    for tt in range(4):
        prog.add("vector", lambda e, tt=tt: e.bn_stats(
            out=st6[:, 16 * tt:16 * tt + 6], in_=xts[tt][:, 0:512]),
            reads=(xkeys[tt],), writes=(K6,))
        prog.add("vector", lambda e, tt=tt: e.bn_stats(
            out=st6[:, 16 * tt + 8:16 * tt + 14], in_=xts[tt][:, 512:1024]),
            reads=(xkeys[tt],), writes=(K6,))
    for tt in range(4):
        prog.add("vector", lambda e, tt=tt: e.bn_aggr(
            out=agg[:, 8 * tt:8 * tt + 2], in_=st6[:, 16 * tt:16 * tt + 6]),
            reads=(K6,), writes=(KA,))
        prog.add("vector", lambda e, tt=tt: e.bn_aggr(
            out=agg[:, 8 * tt + 4:8 * tt + 6], in_=st6[:, 16 * tt + 8:16 * tt + 14]),
            reads=(K6,), writes=(KA,))
    a4 = agg[:, 0:32].rearrange("p (t k) -> p t k", k=8)
    m0 = a4[:, :, 0:1].squeeze(2); v0 = a4[:, :, 1:2].squeeze(2)
    m1 = a4[:, :, 4:5].squeeze(2); v1 = a4[:, :, 5:6].squeeze(2)
    prog.add("vector", lambda e: e.tensor_tensor(out=comb[:, 0:4], in0=m0, in1=m1,
                                                 op=OP.add),
             reads=(KA,), writes=(KC,))
    prog.add("vector", lambda e: e.tensor_scalar(out=comb[:, 0:4], in0=comb[:, 0:4],
                                                 scalar1=0.5, scalar2=None, op0=OP.mult),
             reads=(KC,), writes=(KC,))
    prog.add("vector", lambda e: e.tensor_tensor(out=comb[:, 4:8], in0=v0, in1=v1,
                                                 op=OP.add),
             reads=(KA,), writes=(KC,))
    prog.add("vector", lambda e: e.tensor_scalar(out=comb[:, 4:8], in0=comb[:, 4:8],
                                                 scalar1=0.5, scalar2=None, op0=OP.mult),
             reads=(KC,), writes=(KC,))
    prog.add("vector", lambda e: e.tensor_tensor(out=comb[:, 12:16], in0=m0, in1=m1,
                                                 op=OP.subtract),
             reads=(KA,), writes=(KC,))
    prog.add("vector", lambda e: e.tensor_tensor(out=comb[:, 8:12], in0=comb[:, 12:16],
                                                 in1=comb[:, 12:16], op=OP.mult),
             reads=(KC,), writes=(KC,))
    prog.add("vector", lambda e: e.scalar_tensor_tensor(
        out=comb[:, 16:20], in0=comb[:, 8:12], scalar=0.25, in1=comb[:, 4:8],
        op0=OP.mult, op1=OP.add),
        reads=(KC,), writes=(KC,))
    if fused:
        prog.add("vector", lambda e: e.tensor_scalar(
            out=comb[:, 12:16], in0=comb[:, 16:20],
            scalar1=EPS, scalar2=None, op0=OP.add),
            reads=(KC,), writes=(KC,))
        prog.add("vector", lambda e: e.reciprocal(
            out=comb[:, 12:16], in_=comb[:, 12:16]),
            reads=(KC,), writes=(KC,))
        prog.add("scalar", lambda e: e.activation(
            out=comb[:, 8:12], in_=comb[:, 12:16], func=AF.Sqrt),
            reads=(KC,), writes=(KC,))
        prog.add("vector", lambda e: e.tensor_tensor(
            out=comb[:, 4:8], in0=comb[:, 0:4], in1=comb[:, 8:12],
            op=OP.mult), reads=(KC,), writes=(KC,))
        prog.add("vector", lambda e: e.tensor_scalar(
            out=comb[:, 12:16], in0=comb[:, 8:12],
            scalar1=1.0, scalar2=None, op0=OP.add),
            reads=(KC,), writes=(KC,))
        prog.add("vector", lambda e: e.tensor_scalar(
            out=comb[:, 16:20], in0=comb[:, 16:20],
            scalar1=1.0 + EPS, scalar2=EPS * EPS, op0=OP.mult, op1=OP.add),
            reads=(KC,), writes=(KC,))
    else:
        prog.add("vector", lambda e: e.tensor_scalar(
            out=comb[:, 16:20], in0=comb[:, 16:20],
            scalar1=EPS, scalar2=None, op0=OP.add),
            reads=(KC,), writes=(KC,))
    prog.add("vector", lambda e: e.reciprocal(out=comb[:, 16:20], in_=comb[:, 16:20]),
             reads=(KC,), writes=(KC,))
    prog.add("scalar", lambda e: e.activation(out=comb[:, 20:24], in_=comb[:, 16:20],
                                              func=AF.Sqrt),
             reads=(KC,), writes=(KC,))


def stage_ffn(nc, prog, dr, g, ctx, spill=False, reload=False):
    sb = lambda n, s, d=F32: ctx.enter_context(nc.sbuf_tensor(n, list(s), d))
    ps = lambda n, s, d=F32: ctx.enter_context(nc.psum_tensor(n, list(s), d))
    tag = "ffnD" if reload else "ffnB"

    w1 = sb(tag + "w1", (128, 8 * 4096), BF16)
    w2 = sb(tag + "w2", (128, 32 * D), BF16)
    b1c = sb(tag + "b1c", (128, 32))
    b2r = sb(tag + "b2r", (1, D))
    gc = sb(tag + "gc", (128, 8))
    bc = sb(tag + "bc", (128, 8))
    prog.dma(b1c[:, :], dr["b1cols"][:, :], writes=("b1c",))
    prog.dma(b2r[:, :], dr["b2row"][:, :], writes=("b2r",))
    prog.dma(gc[:, :], dr["gcols"][:, :], writes=("gc",))
    prog.dma(bc[:, :], dr["bcols"][:, :], writes=("bc",))

    if reload:
        prog.dma(w1[:, :], dr["w1bf"][:, :], writes=("w1",))
        prog.dma(w2[:, :], dr["w2bf"][:, :], writes=("w2",))
    else:
      with ExitStack() as dctx:
        sbq = lambda n, s, d=F32: dctx.enter_context(nc.sbuf_tensor(n, list(s), d))
        psq = lambda n, s, d=F32: dctx.enter_context(nc.psum_tensor(n, list(s), d))
        qb = sbq(tag + "qb", (128, 2 * 4096), I32)
        srow = sbq(tag + "srow", (1, 4096))
        psE = psq(tag + "psE", (128, 2048))
        for gi in range(8):
            pq = gi % 2
            qk = f"qb{pq}"
            prog.dma(qb[:, pq * 4096:(pq + 1) * 4096],
                     dr["fc1_q"][gi * 128:(gi + 1) * 128, :],
                     writes=(qk,))
            prog.dma(srow[:, :], dr["fc1_scales"][gi:gi + 1, :],
                     writes=("srow",))
            for half in range(2):
                for j in range(4):
                    c0 = half * 2048 + j * 512
                    prog.add("tensor",
                             lambda e, c0=c0, j=j: e.matmul(
                                 out=psE[:, j * 512:(j + 1) * 512],
                                 lhsT=g["ones_row"][:, :],
                                 rhs=srow[:, c0:c0 + 512], start=True, stop=True),
                             reads=("srow", "ones_row"), writes=("psE",))
                prog.add("vector",
                         lambda e, gi=gi, pq=pq, half=half: e.scalar_tensor_tensor(
                             out=w1[:, gi * 4096 + half * 2048: gi * 4096 + half * 2048 + 2048],
                             in0=qb[:, pq * 4096 + half * 2048: pq * 4096 + half * 2048 + 2048],
                             scalar=-128.0, in1=psE[:, :],
                             op0=OP.add, op1=OP.mult),
                         reads=(qk, "psE", "srow"), writes=("w1",))
        srow2 = sbq(tag + "srow2", (1, D))
        for gi in range(32):
            pq = gi % 2
            qk = f"qb{pq}"
            prog.dma(qb[:, pq * 4096:pq * 4096 + D],
                     dr["fc2_q"][gi * 128:(gi + 1) * 128, :],
                     writes=(qk,))
            prog.dma(srow2[:, :], dr["fc2_scales"][gi:gi + 1, :],
                     writes=("srow2",))
            for j in range(2):
                prog.add("tensor",
                         lambda e, j=j: e.matmul(
                             out=psE[:, j * 512:(j + 1) * 512],
                             lhsT=g["ones_row"][:, :],
                             rhs=srow2[:, j * 512:(j + 1) * 512], start=True, stop=True),
                         reads=("srow2", "ones_row"), writes=("psE",))
            prog.add("vector",
                     lambda e, gi=gi, pq=pq: e.scalar_tensor_tensor(
                         out=w2[:, gi * D:(gi + 1) * D],
                         in0=qb[:, pq * 4096:pq * 4096 + D],
                         scalar=-128.0, in1=psE[:, 0:D],
                         op0=OP.add, op1=OP.mult),
                     reads=(qk, "psE", "srow2"), writes=("w2",))
        if spill:
            prog.dma(dr["w1bf"][:, :], w1[:, :], reads=("w1",),
                     writes=("w1sp",), eng="gpsimd")
            prog.dma(dr["w2bf"][:, :], w2[:, :], reads=("w2",),
                     writes=("w2sp",), eng="gpsimd")
        prog.barrier()

    xb = sb(tag + "xb", (128, 4 * D))
    lnt = sb(tag + "lnt", (128, 8 * 512), BF16)
    silu = sb(tag + "silu", (128, 32 * 512), BF16)
    st6 = sb(tag + "st6", (128, 64))
    agg = sb(tag + "agg", (128, 32))
    comb = sb(tag + "comb", (128, 24))
    lnw = sb(tag + "lnw", (128, D))
    xnew = sb(tag + "xnew", (128, D))
    psA = ps(tag + "psA", (128, 512))
    psB = ps(tag + "psB", (128, 512))
    psC = ps(tag + "psC", (128, 512))
    psD = ps(tag + "psD", (128, 2 * 1024))

    nchunks = FFN_CHUNKS or (T // 512)
    for c in range(nchunks):
        cp = c % 2
        for tt in range(4):
            i = c * 4 + tt
            xk = f"x{tt}"
            prog.dma(xb[:, tt * D:(tt + 1) * D],
                     dr["xio"][i * 128:(i + 1) * 128, :],
                     writes=(xk,))
        xts = [xb[:, tt * D:(tt + 1) * D] for tt in range(4)]
        emit_ln_stats(prog, st6, agg, comb, xts, [f"x{t_}" for t_ in range(4)],
                      tag, fused=True)
        for tt in range(4):
            xk = f"x{tt}"
            prog.add("vector", lambda e, tt=tt: e.tensor_scalar(
                out=lnw[:, :], in0=xts[tt],
                scalar1=comb[:, tt:tt + 1],
                scalar2=comb[:, 20 + tt:21 + tt],
                op0=OP.subtract, op1=OP.mult),
                reads=(xk, tag + "comb"), writes=("lnw",))
            prog.add("vector", lambda e, tt=tt: e.tensor_scalar(
                out=xts[tt], in0=xts[tt],
                scalar1=comb[:, 12 + tt:13 + tt],
                scalar2=comb[:, 4 + tt:5 + tt],
                op0=OP.mult, op1=OP.subtract),
                reads=(xk, tag + "comb"), writes=(xk,))
            for half in range(2):
                pst = psA if half == 0 else psB
                pk = "psA" if half == 0 else "psB"
                for kk in range(4):
                    k = half * 4 + kk
                    prog.add("tensor", lambda e, k=k, kk=kk, pst=pst: e.transpose(
                        out=pst[:, kk * 128:(kk + 1) * 128],
                        in_=lnw[:, k * 128:(k + 1) * 128],
                        identity=g["ident"][:, :]),
                        reads=("lnw", "ident"), writes=(pk,))
                for kk in range(4):
                    k = half * 4 + kk
                    prog.add("vector", lambda e, k=k, kk=kk, tt=tt, pst=pst: e.tensor_scalar(
                        out=lnt[:, k * 512 + tt * 128: k * 512 + (tt + 1) * 128],
                        in0=pst[:, kk * 128:(kk + 1) * 128],
                        scalar1=gc[:, k:k + 1], scalar2=bc[:, k:k + 1],
                        op0=OP.mult, op1=OP.add),
                        reads=(pk, "gc", "bc"), writes=("lnt",))
        if c == 0:
            dbg_dump(nc, prog, dr, "lnt", lnt[:, :], (128, 8 * 512), BF16, ("lnt",))
            dbg_dump(nc, prog, dr, "w1", w1[:, :], (128, 8 * 4096), BF16, ("w1",))
            dbg_dump(nc, prog, dr, "stats", comb[:, :], (128, 24), F32, (tag + "comb",))
            dbg_dump(nc, prog, dr, "agg", agg[:, :], (128, 32), F32, (tag + "agg",))
            dbg_dump(nc, prog, dr, "st6", st6[:, :], (128, 64), F32, (tag + "st6",))
            dbg_dump(nc, prog, dr, "x0", xb[:, 0:D], (128, D), F32, ("x0",))
            dbg_dump(nc, prog, dr, "lnw", lnw[:, :], (128, D), F32, ("lnw",))
        for n in range(32):
            pst, pk = (psB, "psB") if n % 2 == 0 else (psC, "psC")
            for k in range(8):
                prog.add("tensor", lambda e, n=n, k=k, pst=pst: e.matmul(
                    out=pst[:, :],
                    lhsT=w1[:, k * 4096 + n * 128: k * 4096 + (n + 1) * 128],
                    rhs=lnt[:, k * 512:(k + 1) * 512],
                    start=(k == 0), stop=(k == 7)),
                    reads=("w1", "lnt"), writes=(pk,))
            prog.add("scalar", lambda e, n=n, pst=pst: e.activation(
                out=silu[:, n * 512:(n + 1) * 512], in_=pst[:, :],
                func=AF.Silu, bias=b1c[:, n:n + 1]),
                reads=(pk, "b1c"), writes=("silu",))
        if c == 0:
            dbg_dump(nc, prog, dr, "silu", silu[:, :], (128, 32 * 512), BF16, ("silu",))
            dbg_dump(nc, prog, dr, "w2", w2[:, :], (128, 32 * D), BF16, ("w2",))
        for tt in range(4):
            dp = tt % 2
            dk = f"psD{dp}"
            for F in range(32):
                for j in range(2):
                    prog.add("tensor", lambda e, F=F, j=j, tt=tt, dp=dp: e.matmul(
                        out=psD[:, dp * 1024 + j * 512: dp * 1024 + (j + 1) * 512],
                        lhsT=silu[:, F * 512 + tt * 128: F * 512 + (tt + 1) * 128],
                        rhs=w2[:, F * D + j * 512: F * D + (j + 1) * 512],
                        start=(F == 0), stop=False),
                        reads=("silu", "w2"), writes=(dk,))
            for j in range(2):
                prog.add("tensor", lambda e, j=j, dp=dp: e.matmul(
                    out=psD[:, dp * 1024 + j * 512: dp * 1024 + (j + 1) * 512],
                    lhsT=g["ones_row"][:, :],
                    rhs=b2r[:, j * 512:(j + 1) * 512],
                    start=False, stop=True),
                    reads=("b2r", "ones_row"), writes=(dk,))
            xk = f"x{tt}"
            xt = xb[:, tt * D:(tt + 1) * D]
            prog.add("vector", lambda e, xt=xt, dp=dp: e.scalar_tensor_tensor(
                out=xnew[:, :],
                in0=psD[:, dp * 1024: dp * 1024 + 1024],
                scalar=0.5, in1=xt, op0=OP.mult, op1=OP.add),
                reads=(dk, xk), writes=("xnew",))
            i = c * 4 + tt
            prog.dma(dr["xio"][i * 128:(i + 1) * 128, :],
                     xnew[:, :],
                     reads=("xnew",), writes=(), eng="gpsimd")


A_PARTS = 3


def stage_a(nc, prog, dr, g, ctx):
    sb = lambda n, s_, d=F32: ctx.enter_context(nc.sbuf_tensor(n, list(s_), d))
    ps = lambda n, s_, d=F32: ctx.enter_context(nc.psum_tensor(n, list(s_), d))
    pA = ps("a_pA", (128, 2048))
    pB = ps("a_pB", (128, 1024))
    pC = ps("a_pC", (128, 2048), BF16)

    wz = sb("a_wz", (128, 8 * 2080), BF16)
    wxbc = sb("a_wxbc", (128, 8 * 2304), BF16)
    wout = sb("a_wout", (128, 16 * 1024), BF16)
    convd = sb("a_convd", (128, 72 * 128), BF16)
    convb = sb("a_convb", (128, 18))
    selc0 = sb("a_selc0", (128, 8), BF16)
    selbc = sb("a_selbc", (128, 1024), BF16)
    stepb = sb("a_stepb", (128, 128), BF16)
    nbe = sb("a_nbe", (128, 128), BF16)
    dtbc = sb("a_dtbc", (32, 1))
    Acol = sb("a_Acol", (32, 1))
    Dcol = sb("a_Dcol", (32, 1))
    lnst = sb("a_lnst", (128, 64))
    prev = sb("a_prev", (128, 2048))
    xa = sb("a_xa", (128, 2 * D))
    ln0t = sb("a_ln0t", (128, 8 * 256), BF16)
    xbcfm = sb("a_xbcfm", (128, 18 * 260), BF16)
    siluz = sb("a_siluz", (128, 2 * 2048), BF16)
    dtc = sb("a_dtc", (128, 64))
    xssm = sb("a_xssm", (128, 16 * 256), BF16)
    Bfm = sb("a_Bfm", (128, 256), BF16)
    Cfm = sb("a_Cfm", (128, 256), BF16)
    Btm = sb("a_Btm", (128, 256), BF16)
    fm32 = sb("a_fm32", (32, 1024))
    hml = sb("a_hml", (128, 256), BF16)
    grp = sb("a_grp", (128, 256), BF16)
    one256 = sb("a_one256", (32, 256), BF16)
    cdrow = sb("a_cdrow", (1, 32), BF16)
    cstm = sb("a_cstm", (128, 64))
    ecs = sb("a_ecs", (128, 64))
    dtptm = sb("a_dtptm", (128, 64), BF16)
    dectm = sb("a_dectm", (128, 64), BF16)
    Ddtm = sb("a_Ddtm", (128, 64), BF16)
    cdb = sb("a_cdb", (128, 32))
    CBT0 = sb("a_CBT0", (128, 256), BF16)
    CBT1 = sb("a_CBT1", (128, 128), BF16)
    mtg = sb("a_mtg", (128, 2048), BF16)
    mt1g = sb("a_mt1g", (128, 1024), BF16)
    xd = sb("a_xd", (128, 4096), BF16)
    xdds = sb("a_xdds", (128, 512), BF16)
    tg = sb("a_tg", (128, 2048), BF16)
    yscr = sb("a_yscr", (128, 512))
    x4s = sb("a_x4s", (128, 512), BF16)
    rms = sb("a_rms", (128, 16))
    ynt = sb("a_ynt", (128, 2048), BF16)
    xnew = sb("a_xnew", (128, D))

    for name, buf in (("selc0_h", selc0), ("selbc_h", selbc),
                      ("stepb", stepb), ("negbigeye", nbe)):
        prog.dma(buf[:, :], dr[name][:, :], writes=(name,))
    prog.dma(convb[:, :], dr["convbcols"][:, :], writes=("convb",))
    prog.dma(dtbc[:, :], dr["dtbias_col"][:, :], writes=("dtbc",))
    prog.dma(Dcol[:, :], dr["D_col"][:, :], writes=("Dcol",))
    prog.dma(Acol[:, :], dr["Alog_col"][:, :], writes=("Acol",))
    prog.add("scalar", lambda e: e.activation(out=Acol[:, :], in_=Acol[:, :],
                                              func=AF.Exp),
             reads=("Acol",), writes=("Acol",))
    prog.add("vector", lambda e: e.tensor_scalar(out=Acol[:, :], in0=Acol[:, :],
                                                 scalar1=-1.0, scalar2=None,
                                                 op0=OP.mult),
             reads=("Acol",), writes=("Acol",))
    prog.add("vector", lambda e: e.memset(xbcfm[:, :], 0.0), writes=("xbcfm",))
    prog.add("vector", lambda e: e.memset(grp[:, :], 0.0), writes=("grp",))
    prog.add("vector", lambda e: e.memset(hml[:, :], 0.0), writes=("hml",))
    prog.add("vector", lambda e: e.memset(one256[:, :], 1.0), writes=("one256",))

    xd_f = xd.bitcast(F32)
    tg_f = tg.bitcast(F32)
    mtg_f = mtg.bitcast(F32)
    stgs = [xd_f[:, 0:1024], xd_f[:, 1024:2048]]
    stgk = ["xd", "xd2"]

    def stream_cast(dst, dst_key, dram, rows, cols, fold_col=None, nwc=None):
        for (r0, c0, n, dcol) in _pieces(rows, cols):
            si = stream_cast.si % 2
            stream_cast.si += 1
            prog.dma(stgs[si][:, 0:n], dram[r0:r0 + 128, c0:c0 + n],
                     writes=(stgk[si],))
            if fold_col is None:
                prog.add("vector", lambda e, si=si, n=n, dcol=dcol: e.tensor_copy(
                    out=dst[:, dcol:dcol + n], in_=stgs[si][:, 0:n]),
                    reads=(stgk[si],), writes=(dst_key,))
            else:
                prog.add("vector", lambda e, si=si, n=n, dcol=dcol: e.tensor_scalar(
                    out=dst[:, dcol:dcol + n], in0=stgs[si][:, 0:n],
                    scalar1=nwc[:, fold_col:fold_col + 1], scalar2=None,
                    op0=OP.mult),
                    reads=(stgk[si], "nwc"), writes=(dst_key,))
    stream_cast.si = 0

    def _pieces(r0, spec):
        c0, tot, d0 = spec
        out = []
        off = 0
        while off < tot:
            n = min(1024, tot - off)
            out.append((r0, c0 + off, n, d0 + off))
            off += n
        return out

    with ExitStack() as wctx:
        nwc = wctx.enter_context(nc.sbuf_tensor("a_nwc", [128, 16], F32))
        pvf = prev
        st6 = pvf[:, 0:64]; agg = pvf[:, 64:96]; comb = pvf[:, 96:120]
        prog.dma(nwc[:, :], dr["normw_cols"][:, :], writes=("nwc",))
        for k in range(8):
            stream_cast(wz, "wz", dr["WzdtT"], k * 128, (0, 2080, k * 2080))
        for k in range(8):
            stream_cast(wxbc, "wxbc", dr["WxbcT"], k * 128, (0, 2304, k * 2304))
        for k in range(16):
            stream_cast(wout, "wout", dr["WoutT"], k * 128, (0, 1024, k * 1024),
                        fold_col=k, nwc=nwc)
        for q in range(9):
            si = stream_cast.si % 2
            stream_cast.si += 1
            prog.dma(stgs[si][:, 0:1024].rearrange("p (n q) -> p n q", n=8),
                     dr["convdiag"][8 * q:8 * (q + 1), :, :]
                     .rearrange("n p q -> p n q"),
                     writes=(stgk[si],))
            prog.add("vector", lambda e, si=si, q=q: e.tensor_copy(
                out=convd[:, q * 1024:(q + 1) * 1024], in_=stgs[si][:, 0:1024]),
                reads=(stgk[si],), writes=("convd",))
        pstg = [xd_f[:, 0:1024], xd_f[:, 1024:2048], tg_f[:, 0:1024],
                mtg_f[:, 0:1024]]
        pkey = ["xd", "xd2", "tg", "mtg"]
        for c8 in range(8):
            for tt in range(4):
                i = c8 * 4 + tt
                prog.dma(pstg[tt][:, :], dr["x_in"][i * 128:(i + 1) * 128, :],
                         writes=(pkey[tt],))
            emit_ln_stats(prog, st6, agg, comb, pstg, pkey, "a0", fused=False)
            prog.add("vector", lambda e, c8=c8: e.tensor_copy(
                out=lnst[:, c8 * 4:(c8 + 1) * 4], in_=comb[:, 0:4]),
                reads=("a0comb",), writes=("lnst",))
            prog.add("vector", lambda e, c8=c8: e.tensor_copy(
                out=lnst[:, 32 + c8 * 4:32 + (c8 + 1) * 4], in_=comb[:, 20:24]),
                reads=("a0comb",), writes=("lnst",))
    prog.add("vector", lambda e: e.memset(prev[:, :], 0.0),
             reads=("a0st6", "a0agg", "a0comb", "lnst"), writes=("prev",))
    prog.barrier()

    for c in range(NCH):
        stage_a_chunk(nc, prog, dr, g, c, locals())


def stage_a_chunk(nc, prog, dr, g, c, B):
    pA, pB, pC = B["pA"], B["pB"], B["pC"]
    wz, wxbc, wout = B["wz"], B["wxbc"], B["wout"]
    convd, convb = B["convd"], B["convb"]
    xa, ln0t, lnst = B["xa"], B["ln0t"], B["lnst"]
    lnw = B["ynt"].bitcast(F32)
    xbcfm, siluz, dtc = B["xbcfm"], B["siluz"], B["dtc"]
    xssm, Bfm, Cfm, Btm = B["xssm"], B["Bfm"], B["Cfm"], B["Btm"]
    fm32, hml, grp = B["fm32"], B["hml"], B["grp"]
    one256, cdrow, selbc = B["one256"], B["cdrow"], B["selbc"]
    cstm, ecs, dtptm, dectm, Ddtm, cdb = (B["cstm"], B["ecs"], B["dtptm"],
                                          B["dectm"], B["Ddtm"], B["cdb"])
    CBT0, CBT1, mtg, mt1g = B["CBT0"], B["CBT1"], B["mtg"], B["mt1g"]
    xd, xdds, tg = B["xd"], B["xdds"], B["tg"]
    yscr, x4s, rms, ynt, xnew = (
        B["yscr"], B["x4s"], B["rms"], B["ynt"], B["xnew"])
    selc0, stepb, nbe = (B["selc0"], B["stepb"], B["nbe"])
    dtbc, Acol, Dcol = B["dtbc"], B["Acol"], B["Dcol"]
    prev = B["prev"]

    for th in range(2):
        i = c * 2 + th
        prog.dma(xa[:, th * D:(th + 1) * D], dr["x_in"][i * 128:(i + 1) * 128, :],
                 writes=(f"ax{th}",))
    for th in range(2):
        i = c * 2 + th
        prog.add("vector", lambda e, th=th, i=i: e.tensor_scalar(
            out=lnw[:, :], in0=xa[:, th * D:(th + 1) * D],
            scalar1=lnst[:, i:i + 1], scalar2=lnst[:, 32 + i:33 + i],
            op0=OP.subtract, op1=OP.mult),
            reads=(f"ax{th}", "lnst"), writes=("ynt",))
        for half in range(2):
            for kk in range(4):
                k = half * 4 + kk
                prog.add("tensor", lambda e, k=k, kk=kk: e.transpose(
                    out=pB[:, kk * 128:(kk + 1) * 128],
                    in_=lnw[:, k * 128:(k + 1) * 128],
                    identity=g["ident"][:, :]),
                    reads=("ynt", "ident"), writes=("a_pB0", "a_pB1"))
            prog.add("vector", lambda e, half=half, th=th: e.tensor_copy(
                out=ln0t[:, :].rearrange("p (k t) -> p k t", k=8)
                    [:, half * 4:(half + 1) * 4, th * 128:(th + 1) * 128],
                in_=pB[:, 0:512].rearrange("p (k t) -> p k t", k=4)),
                reads=("a_pB0", "a_pB1"), writes=("ln0t",))
    for th in range(2):
        for ncol in range(4):
            for k in range(8):
                prog.add("tensor", lambda e, th=th, ncol=ncol, k=k: e.matmul(
                    out=pA[:, ncol * 512:(ncol + 1) * 512],
                    lhsT=ln0t[:, k * 256 + th * 128: k * 256 + (th + 1) * 128],
                    rhs=wz[:, k * 2080 + ncol * 512: k * 2080 + (ncol + 1) * 512],
                    start=(k == 0), stop=(k == 7)),
                    reads=("ln0t", "wz"), writes=(f"a_pA{ncol}",))
        for k in range(8):
            prog.add("tensor", lambda e, th=th, k=k: e.matmul(
                out=pB[:, th * 512:th * 512 + 32],
                lhsT=ln0t[:, k * 256 + th * 128: k * 256 + (th + 1) * 128],
                rhs=wz[:, k * 2080 + 2048: k * 2080 + 2080],
                start=(k == 0), stop=(k == 7)),
                reads=("ln0t", "wz"), writes=(f"a_pB{th}",))
        prog.add("scalar", lambda e, th=th: e.activation(
            out=siluz[:, th * 2048:(th + 1) * 2048], in_=pA[:, :], func=AF.Silu),
            reads=("a_pA0", "a_pA1", "a_pA2", "a_pA3"), writes=("siluz",))
        prog.add("vector", lambda e, th=th: e.tensor_copy(
            out=dtc[:, th * 32:(th + 1) * 32], in_=pB[:, th * 512:th * 512 + 32]),
            reads=(f"a_pB{th}",), writes=("dtc",))
    for n in range(18):
        pk = "a_pB0" if n % 2 == 0 else "a_pB1"
        off = 0 if n % 2 == 0 else 512
        for k in range(8):
            prog.add("tensor", lambda e, n=n, k=k, off=off: e.matmul(
                out=pB[:, off:off + 256],
                lhsT=wxbc[:, k * 2304 + n * 128: k * 2304 + (n + 1) * 128],
                rhs=ln0t[:, k * 256:(k + 1) * 256],
                start=(k == 0), stop=(k == 7)),
                reads=("wxbc", "ln0t"), writes=(pk,))
        prog.add("vector", lambda e, n=n, off=off: e.tensor_copy(
            out=xbcfm[:, n * 260 + 3: n * 260 + 259], in_=pB[:, off:off + 256]),
            reads=(pk,), writes=("xbcfm", "halo")) 
        cpk = "a_pB1" if n % 2 == 0 else "a_pB0"
        coff = 512 if n % 2 == 0 else 0
        for tap in range(4):
            prog.add("tensor", lambda e, n=n, tap=tap, coff=coff: e.matmul(
                out=pB[:, coff:coff + 256],
                lhsT=convd[:, (4 * n + tap) * 128:(4 * n + tap + 1) * 128],
                rhs=xbcfm[:, n * 260 + tap: n * 260 + tap + 256],
                start=(tap == 0), stop=(tap == 3)),
                reads=("convd", "xbcfm"), writes=(cpk,))
        if n < 16:
            dst = lambda e, n=n, coff=coff: e.activation(
                out=xssm[:, n * 256:(n + 1) * 256], in_=pB[:, coff:coff + 256],
                func=AF.Silu, bias=convb[:, n:n + 1])
            wkey = "xssm"
        elif n == 16:
            dst = lambda e, coff=coff: e.activation(
                out=Bfm[:, :], in_=pB[:, coff:coff + 256],
                func=AF.Silu, bias=convb[:, 16:17])
            wkey = "Bfm"
        else:
            dst = lambda e, coff=coff: e.activation(
                out=Cfm[:, :], in_=pB[:, coff:coff + 256],
                func=AF.Silu, bias=convb[:, 17:18])
            wkey = "Cfm"
        prog.add("scalar", dst, reads=(cpk, "convb"), writes=(wkey,))
    for hseg in range(3):
        v = xbcfm[:, hseg * 6 * 260:(hseg + 1) * 6 * 260].rearrange(
            "p (n q) -> p n q", n=6)
        prog.add("vector", lambda e, v=v: e.tensor_copy(
            out=v[:, :, 0:3], in_=v[:, :, 256:259]),
            reads=("xbcfm",), writes=("xbcfm",))
    for half in range(2):
        prog.add("tensor", lambda e, half=half: e.transpose(
            out=pC[:, half * 128:(half + 1) * 128],
            in_=Bfm[:, half * 128:(half + 1) * 128], identity=g["identb"][:, :]),
            reads=("Bfm", "identb"), writes=("a_pC0",))
    prog.add("vector", lambda e: e.tensor_copy(out=Btm[:, :], in_=pC[:, 0:256]),
             reads=("a_pC0",), writes=("Btm",))
    if c == 0:
        dbg_dump(nc, prog, dr, "a_siluz", B["siluz"][:, :], (128, 4096), BF16, ("siluz",))
        dbg_dump(nc, prog, dr, "a_xssm", B["xssm"][:, :], (128, 4096), BF16, ("xssm",))
        dbg_dump(nc, prog, dr, "a_Bfm", B["Bfm"][:, :], (128, 256), BF16, ("Bfm",))
        dbg_dump(nc, prog, dr, "a_Cfm", B["Cfm"][:, :], (128, 256), BF16, ("Cfm",))
        dbg_dump(nc, prog, dr, "a_dtc", B["dtc"][:, :], (128, 64), F32, ("dtc",))
        dbg_dump(nc, prog, dr, "a_ln0t", B["ln0t"][:, :], (128, 2048), BF16, ("ln0t",))
        dbg_dump(nc, prog, dr, "a_xbcfm", B["xbcfm"][:, :], (128, 18 * 260), BF16, ("xbcfm",))
        dbg_dump(nc, prog, dr, "a_convd", B["convd"][:, :], (128, 72 * 128), BF16, ("convd",))
    if A_PARTS < 2:
        return
    stage_a_chunk_p2(nc, prog, dr, g, c, B)


def stage_a_chunk_p2(nc, prog, dr, g, c, B):
    pA, pB, pC = B["pA"], B["pB"], B["pC"]
    fm32, hml, grp = B["fm32"], B["hml"], B["grp"]
    one256, cdrow, selbc = B["one256"], B["cdrow"], B["selbc"]
    selc0, stepb, nbe = B["selc0"], B["stepb"], B["nbe"]
    dtbc, Acol, Dcol = B["dtbc"], B["Acol"], B["Dcol"]
    dtc, xssm, Bfm, Cfm, Btm = B["dtc"], B["xssm"], B["Bfm"], B["Cfm"], B["Btm"]
    cstm, ecs, dtptm, dectm, Ddtm, cdb = (B["cstm"], B["ecs"], B["dtptm"],
                                          B["dectm"], B["Ddtm"], B["cdb"])
    CBT0, CBT1, mtg, mt1g = B["CBT0"], B["CBT1"], B["mtg"], B["mt1g"]
    xd, xdds, siluz = B["xd"], B["xdds"], B["siluz"]
    yscr, x4s, rms, ynt, xnew = (B["yscr"], B["x4s"], B["rms"], B["ynt"],
                                 B["xnew"])
    prev, xa, wout = B["prev"], B["xa"], B["wout"]
    f0, f1, f2, f3 = (fm32[:, 0:256], fm32[:, 256:512], fm32[:, 512:768],
                      fm32[:, 768:1024])
    FK = "a_fm"
    V, S, Tn = "vector", "scalar", "tensor"

    def add(eng, fn, r, w):
        prog.add(eng, fn, reads=r, writes=w)

    for th in range(2):
        add(Tn, lambda e, th=th: e.transpose(
            out=pB[0:32, th * 128:(th + 1) * 128],
            in_=dtc[:, th * 32:(th + 1) * 32], identity=g["ident"][:, :]),
            ("dtc", "ident"), ("a_pB0",))
    add(V, lambda e: e.tensor_copy(out=f0[:, :], in_=pB[0:32, 0:256]),
        ("a_pB0",), (FK,))
    add(V, lambda e: e.tensor_scalar(out=f1[:, :], in0=f0[:, :],
                                     scalar1=dtbc[:, 0:1], scalar2=None,
                                     op0=OP.add), (FK, "dtbc"), (FK,))
    add(V, lambda e: e.tensor_scalar(out=f3[:, :], in0=f1[:, :], scalar1=-1.0,
                                     scalar2=None, op0=OP.mult), (FK,), (FK,))
    add(V, lambda e: e.tensor_tensor(out=f2[:, :], in0=f1[:, :], in1=f3[:, :],
                                     op=OP.max), (FK,), (FK,))
    add(S, lambda e: e.activation(out=f3[:, :], in_=f2[:, :], func=AF.Exp,
                                  scale=-1.0), (FK,), (FK,))
    add(S, lambda e: e.activation(out=f2[:, :], in_=f3[:, :], func=AF.Ln,
                                  bias=g["ones_col"][0:32, 0:1]),
        (FK, "ones_col"), (FK,))
    add(S, lambda e: e.activation(out=f0[:, :], in_=f2[:, :], func=AF.Exp,
                                  scale=-1.0), (FK,), (FK,))
    add(V, lambda e: e.scalar_tensor_tensor(out=f0[:, :], in0=f3[:, :],
                                            scalar=1.0, in1=f0[:, :],
                                            op0=OP.add, op1=OP.mult),
        (FK,), (FK,))
    add(V, lambda e: e.scalar_tensor_tensor(out=f2[:, :], in0=f0[:, :],
                                            scalar=-1.0, in1=f2[:, :],
                                            op0=OP.add, op1=OP.add),
        (FK,), (FK,))
    add(V, lambda e: e.tensor_scalar(out=f3[:, :], in0=f1[:, :], scalar1=0.0,
                                     scalar2=None, op0=OP.max), (FK,), (FK,))
    add(V, lambda e: e.tensor_tensor(out=f1[:, :], in0=f2[:, :], in1=f3[:, :],
                                     op=OP.add), (FK,), (FK,))
    add(V, lambda e: e.tensor_scalar(out=f0[:, :], in0=f1[:, :],
                                     scalar1=Acol[:, 0:1], scalar2=None,
                                     op0=OP.mult), (FK, "Acol"), (FK,))
    add(V, lambda e: e.tensor_tensor_scan(out=f2[:, :], data0=one256[:, :],
                                          data1=f0[:, :], initial=0.0,
                                          op0=OP.mult, op1=OP.add),
        (FK, "one256"), (FK,))
    add(S, lambda e: e.activation(out=f3[:, :], in_=f2[:, :], func=AF.Exp,
                                  scale=-1.0, bias=f2[:, 255:256]),
        (FK,), (FK,))
    add(S, lambda e: e.activation(out=f0[:, 0:1], in_=f2[:, 255:256],
                                  func=AF.Exp), (FK,), (FK,))
    add(Tn, lambda e: e.transpose(out=pB[0:1, 0:32], in_=f0[:, 0:1],
                                  identity=g["ident"][0:32, 0:32]),
        (FK, "ident"), ("a_pB0",))
    add(V, lambda e: e.tensor_copy(out=cdrow[:, :], in_=pB[0:1, 0:32]),
        ("a_pB0",), ("cdrow",))
    add(Tn, lambda e: e.matmul(out=pB[:, 0:32], lhsT=g["onesb_row"][:, :],
                               rhs=cdrow[:, :], start=True, stop=True),
        ("cdrow", "onesb_row"), ("a_pB0",))
    add(V, lambda e: e.tensor_copy(out=cdb[:, :], in_=pB[:, 0:32]),
        ("a_pB0",), ("cdb",))
    for th in range(2):
        add(Tn, lambda e, th=th: e.transpose(
            out=pB[:, th * 32:(th + 1) * 32],
            in_=f2[:, th * 128:(th + 1) * 128],
            identity=g["ident"][0:32, 0:32]), (FK, "ident"), ("a_pB0",))
        add(Tn, lambda e, th=th: e.transpose(
            out=pB[:, 64 + th * 32: 64 + (th + 1) * 32],
            in_=f1[:, th * 128:(th + 1) * 128],
            identity=g["ident"][0:32, 0:32]), (FK, "ident"), ("a_pB0",))
        add(Tn, lambda e, th=th: e.transpose(
            out=pB[:, 128 + th * 32: 128 + (th + 1) * 32],
            in_=f3[:, th * 128:(th + 1) * 128],
            identity=g["ident"][0:32, 0:32]), (FK, "ident"), ("a_pB0",))
    add(V, lambda e: e.tensor_copy(out=cstm[:, :], in_=pB[:, 0:64]),
        ("a_pB0",), ("cstm",))
    add(V, lambda e: e.tensor_copy(out=dtptm[:, :], in_=pB[:, 64:128]),
        ("a_pB0",), ("dtptm",))
    add(V, lambda e: e.tensor_copy(out=dectm[:, :], in_=pB[:, 128:192]),
        ("a_pB0",), ("dectm",))
    add(S, lambda e: e.activation(out=ecs[:, :], in_=cstm[:, :], func=AF.Exp),
        ("cstm",), ("ecs",))
    add(V, lambda e: e.reciprocal(out=f3[:, :], in_=f1[:, :]), (FK,), (FK,))
    add(V, lambda e: e.tensor_scalar(out=f3[:, :], in0=f3[:, :],
                                     scalar1=Dcol[:, 0:1], scalar2=None,
                                     op0=OP.mult), (FK, "Dcol"), (FK,))
    for th in range(2):
        add(Tn, lambda e, th=th: e.transpose(
            out=pB[:, 192 + th * 32: 192 + (th + 1) * 32],
            in_=f3[:, th * 128:(th + 1) * 128],
            identity=g["ident"][0:32, 0:32]), (FK, "ident"), ("a_pB0",))
    add(V, lambda e: e.tensor_copy(out=Ddtm[:, :], in_=pB[:, 192:256]),
        ("a_pB0",), ("Ddtm",))
    hscr = B["x4s"][0:32, 0:256].bitcast(BF16)[:, 0:256]
    add(V, lambda e: e.tensor_scalar(out=f0[:, :], in0=f2[:, :], scalar1=-1.0,
                                     scalar2=None, op0=OP.mult), (FK,), (FK,))
    add(V, lambda e, hscr=hscr: e.tensor_copy(out=hscr, in_=f0[:, :]),
        (FK,), ("x4s",))
    add(V, lambda e, hscr=hscr: e.tensor_copy(out=hml[0:32, :], in_=hscr),
        ("x4s",), ("hml",))
    add(V, lambda e, hscr=hscr: e.tensor_tensor(out=f1[:, :], in0=f0[:, :],
                                                in1=hscr, op=OP.subtract),
        (FK, "x4s"), (FK,))
    add(V, lambda e, hscr=hscr: e.tensor_copy(out=hscr, in_=f1[:, :]),
        (FK,), ("x4s",))
    add(V, lambda e, hscr=hscr: e.tensor_copy(out=hml[32:64, :], in_=hscr),
        ("x4s",), ("hml",))
    add(V, lambda e, hscr=hscr: e.tensor_tensor(out=f3[:, :], in0=f1[:, :],
                                                in1=hscr, op=OP.subtract),
        (FK, "x4s"), (FK,))
    add(V, lambda e: e.tensor_copy(out=hml[64:96, :], in_=f3[:, :]),
        (FK,), ("hml",))
    add(Tn, lambda e: e.matmul(out=pB[:, 0:256], lhsT=Bfm[:, 0:128],
                               rhs=Cfm[:, :], start=True, stop=True),
        ("Bfm", "Cfm"), ("a_pB0",))
    add(Tn, lambda e: e.matmul(out=pB[:, 512:640], lhsT=Bfm[:, 128:256],
                               rhs=Cfm[:, 128:256], start=True, stop=True),
        ("Bfm", "Cfm"), ("a_pB1",))
    add(V, lambda e: e.tensor_copy(out=CBT0[:, :], in_=pB[:, 0:256]),
        ("a_pB0",), ("CBT0",))
    add(V, lambda e: e.tensor_copy(out=CBT1[:, :], in_=pB[:, 512:640]),
        ("a_pB1",), ("CBT1",))
    for th in range(2):
        for i in range(16):
            add(Tn, lambda e, th=th, i=i: e.transpose(
                out=pC[:, i * 128:(i + 1) * 128],
                in_=xssm[:, i * 256 + th * 128: i * 256 + (th + 1) * 128],
                identity=g["identb"][:, :]),
                ("xssm", "identb"), ("a_pC0", "a_pC1"))
        add(V, lambda e, th=th: e.tensor_tensor(
            out=xd[:, th * 2048:(th + 1) * 2048].rearrange(
                "p (h q) -> p h q", h=32),
            in0=pC[:, :].rearrange("p (h q) -> p h q", h=32),
            in1=dtptm[:, th * 32:(th + 1) * 32].unsqueeze(2)
                .broadcast_to((128, 32, 64)),
            op=OP.mult), ("a_pC0", "a_pC1", "dtptm"), ("xd",))
    for gi in range(4):
        GK = "grp"
        for lvl in range(3):
            prog.dma(grp[8 * lvl:8 * lvl + 8, :],
                     hml[32 * lvl + 8 * gi: 32 * lvl + 8 * gi + 8, :],
                     reads=("hml",), writes=(GK,))
        for hl in range(8):
            add(Tn, lambda e, hl=hl: e.matmul(
                out=pA[:, hl * 256:(hl + 1) * 256],
                lhsT=selbc[:, hl * 128:(hl + 1) * 128],
                rhs=grp[:, 0:256], start=True, stop=False),
                (GK, "selbc"), ("a_pA0", "a_pA1", "a_pA2", "a_pA3"))
            add(Tn, lambda e, hl=hl: e.matmul(
                out=pA[:, hl * 256:(hl + 1) * 256],
                lhsT=grp[:, 0:128],
                rhs=selc0[:, hl:hl + 1].broadcast_to((128, 256)),
                start=False, stop=False),
                (GK, "selc0"), ("a_pA0", "a_pA1", "a_pA2", "a_pA3"))
            add(Tn, lambda e, hl=hl: e.matmul(
                out=pA[:, hl * 256: hl * 256 + 128],
                lhsT=stepb[:, :], rhs=nbe[:, :], start=False, stop=True),
                ("stepb", "negbigeye"), ("a_pA0", "a_pA1", "a_pA2", "a_pA3"))
        add(S, lambda e: e.activation(out=mtg[:, :], in_=pA[:, :], func=AF.Exp),
            ("a_pA0", "a_pA1", "a_pA2", "a_pA3"), ("mtg",))
        for hl in range(8):
            add(Tn, lambda e, hl=hl: e.matmul(
                out=pB[:, hl * 128:(hl + 1) * 128],
                lhsT=selbc[:, hl * 128:(hl + 1) * 128],
                rhs=grp[:, 128:256], start=True, stop=False),
                (GK, "selbc"), ("a_pB0", "a_pB1"))
            add(Tn, lambda e, hl=hl: e.matmul(
                out=pB[:, hl * 128:(hl + 1) * 128],
                lhsT=grp[:, 128:256],
                rhs=selc0[:, hl:hl + 1].broadcast_to((128, 128)),
                start=False, stop=False),
                (GK, "selc0"), ("a_pB0", "a_pB1"))
            add(Tn, lambda e, hl=hl: e.matmul(
                out=pB[:, hl * 128:(hl + 1) * 128],
                lhsT=stepb[:, :], rhs=nbe[:, :], start=False, stop=True),
                ("stepb", "negbigeye"), ("a_pB0", "a_pB1"))
        add(S, lambda e: e.activation(out=mt1g[:, :], in_=pB[:, :], func=AF.Exp),
            ("a_pB0", "a_pB1"), ("mt1g",))
        add(V, lambda e: e.tensor_tensor(
            out=mtg[:, :].rearrange("p (h q) -> p h q", h=8),
            in0=mtg[:, :].rearrange("p (h q) -> p h q", h=8),
            in1=CBT0[:, :].unsqueeze(1).broadcast_to((128, 8, 256)),
            op=OP.mult), ("mtg", "CBT0"), ("mtg",))
        add(V, lambda e: e.tensor_tensor(
            out=mt1g[:, :].rearrange("p (h q) -> p h q", h=8),
            in0=mt1g[:, :].rearrange("p (h q) -> p h q", h=8),
            in1=CBT1[:, :].unsqueeze(1).broadcast_to((128, 8, 128)),
            op=OP.mult), ("mt1g", "CBT1"), ("mt1g",))
        add(V, lambda e, gi=gi: e.tensor_copy(
            out=xdds[:, :], in_=prev[:, gi * 512:(gi + 1) * 512]),
            ("prev",), ("xdds",))
        for lt in range(2):
            add(V, lambda e, lt=lt, gi=gi: e.tensor_tensor(
                out=x4s[:, :].rearrange("p (h q) -> p h q", h=8),
                in0=xd[:, lt * 2048 + gi * 512: lt * 2048 + (gi + 1) * 512]
                    .rearrange("p (h q) -> p h q", h=8),
                in1=Ddtm[:, lt * 32 + gi * 8: lt * 32 + (gi + 1) * 8]
                    .unsqueeze(2).broadcast_to((128, 8, 64)),
                op=OP.mult), ("xd", "Ddtm"), ("x4s",))
            for hl in range(8):
                if lt == 0:
                    add(Tn, lambda e, hl=hl, gi=gi: e.matmul(
                        out=pA[:, hl * 64:(hl + 1) * 64],
                        lhsT=mtg[:, hl * 256: hl * 256 + 128],
                        rhs=xd[:, gi * 512 + hl * 64: gi * 512 + (hl + 1) * 64],
                        start=True, stop=True),
                        ("mtg", "xd"), ("a_pA0",))
                else:
                    add(Tn, lambda e, hl=hl, gi=gi: e.matmul(
                        out=pA[:, hl * 64:(hl + 1) * 64],
                        lhsT=mtg[:, hl * 256 + 128: hl * 256 + 256],
                        rhs=xd[:, gi * 512 + hl * 64: gi * 512 + (hl + 1) * 64],
                        start=True, stop=False),
                        ("mtg", "xd"), ("a_pA0",))
                    add(Tn, lambda e, hl=hl, gi=gi: e.matmul(
                        out=pA[:, hl * 64:(hl + 1) * 64],
                        lhsT=mt1g[:, hl * 128:(hl + 1) * 128],
                        rhs=xd[:, 2048 + gi * 512 + hl * 64:
                                2048 + gi * 512 + (hl + 1) * 64],
                        start=False, stop=True),
                        ("mt1g", "xd"), ("a_pA0",))
                add(Tn, lambda e, hl=hl, lt=lt: e.matmul(
                    out=pA[:, 512 + hl * 64: 512 + (hl + 1) * 64],
                    lhsT=Cfm[:, lt * 128:(lt + 1) * 128],
                    rhs=xdds[:, hl * 64:(hl + 1) * 64],
                    start=True, stop=True),
                    ("Cfm", "xdds"), ("a_pA1",))
            add(V, lambda e, lt=lt, gi=gi: e.tensor_tensor(
                out=yscr[:, :].rearrange("p (h q) -> p h q", h=8),
                in0=pA[:, 512:1024].rearrange("p (h q) -> p h q", h=8),
                in1=ecs[:, lt * 32 + gi * 8: lt * 32 + (gi + 1) * 8]
                    .unsqueeze(2).broadcast_to((128, 8, 64)),
                op=OP.mult), ("a_pA1", "ecs"), ("yscr",))
            add(V, lambda e: e.tensor_tensor(out=yscr[:, :], in0=yscr[:, :],
                                             in1=x4s[:, :], op=OP.add),
                ("yscr", "x4s"), ("yscr",))
            add(V, lambda e: e.scalar_tensor_tensor(
                out=yscr[:, :], in0=yscr[:, :], scalar=0.0, in1=pA[:, 0:512],
                op0=OP.add, op1=OP.add), ("a_pA0", "yscr"), ("yscr",))
            add(V, lambda e, lt=lt, gi=gi: e.tensor_tensor(
                out=xssm[:, lt * 2048 + gi * 512: lt * 2048 + (gi + 1) * 512],
                in0=yscr[:, :],
                in1=siluz[:, lt * 2048 + gi * 512: lt * 2048 + (gi + 1) * 512],
                op=OP.mult), ("yscr", "siluz"), ("xssm",))
            add(V, lambda e, lt=lt, gi=gi: e.scalar_tensor_tensor(
                out=x4s[:, :],
                in0=xssm[:, lt * 2048 + gi * 512: lt * 2048 + (gi + 1) * 512],
                scalar=1.0,
                in1=xssm[:, lt * 2048 + gi * 512: lt * 2048 + (gi + 1) * 512],
                op0=OP.mult, op1=OP.mult,
                accum_out=rms[:, lt * 4 + gi: lt * 4 + gi + 1]),
                ("xssm",), ("x4s", "rms"))
        for lh in range(2):
            add(V, lambda e, lh=lh, gi=gi: e.tensor_tensor(
                out=xdds[:, :].rearrange("p (h q) -> p h q", h=8),
                in0=xd[:, lh * 2048 + gi * 512: lh * 2048 + (gi + 1) * 512]
                    .rearrange("p (h q) -> p h q", h=8),
                in1=dectm[:, lh * 32 + gi * 8: lh * 32 + (gi + 1) * 8]
                    .unsqueeze(2).broadcast_to((128, 8, 64)),
                op=OP.mult), ("xd", "dectm"), ("xdds",)) 
            if c == 0 and gi == 0:
                dbg_dump(nc, prog, dr, f"a_xdds{lh}", xdds[:, :], (128, 512),
                         BF16, ("xdds",))
            for hl in range(8):
                add(Tn, lambda e, hl=hl, lh=lh: e.matmul(
                    out=pA[:, 1536 + hl * 64: 1536 + (hl + 1) * 64],
                    lhsT=Btm[:, lh * 128:(lh + 1) * 128],
                    rhs=xdds[:, hl * 64:(hl + 1) * 64],
                    start=(lh == 0 and hl == 0), stop=(lh == 1 and hl == 7)),
                    ("Btm", "xdds"), ("a_pA3",))
            if lh == 0:
                pass
        add(V, lambda e, gi=gi: e.tensor_tensor(
            out=yscr[:, :].rearrange("p (h q) -> p h q", h=8),
            in0=prev[:, gi * 512:(gi + 1) * 512]
                .rearrange("p (h q) -> p h q", h=8),
            in1=cdb[:, gi * 8:(gi + 1) * 8].unsqueeze(2)
                .broadcast_to((128, 8, 64)),
            op=OP.mult), ("prev", "cdb"), ("yscr",))
        add(V, lambda e, gi=gi: e.scalar_tensor_tensor(
            out=prev[:, gi * 512:(gi + 1) * 512],
            in0=pA[:, 1536:2048], scalar=0.0, in1=yscr[:, :],
            op0=OP.add, op1=OP.add), ("a_pA3", "yscr"), ("prev",))
    for lt in range(2):
        add(V, lambda e, lt=lt: e.tensor_reduce(
            out=rms[:, 8 + lt: 9 + lt], in_=rms[:, lt * 4:(lt + 1) * 4],
            axis=AX.X, op=OP.add), ("rms",), ("rms",))
        add(V, lambda e, lt=lt: e.tensor_scalar(
            out=rms[:, 10 + lt: 11 + lt], in0=rms[:, 8 + lt: 9 + lt],
            scalar1=1.0 / 2048.0, scalar2=EPS, op0=OP.mult, op1=OP.add),
            ("rms",), ("rms",))
        add(S, lambda e, lt=lt: e.activation(
            out=rms[:, 12 + lt: 13 + lt], in_=rms[:, 10 + lt: 11 + lt],
            func=AF.Ln), ("rms",), ("rms",))
        add(S, lambda e, lt=lt: e.activation(
            out=rms[:, 14 + lt: 15 + lt], in_=rms[:, 12 + lt: 13 + lt],
            func=AF.Exp, scale=-0.5), ("rms",), ("rms",))
        add(V, lambda e, lt=lt: e.tensor_scalar(
            out=xssm[:, lt * 2048:(lt + 1) * 2048],
            in0=xssm[:, lt * 2048:(lt + 1) * 2048],
            scalar1=rms[:, 14 + lt: 15 + lt], scalar2=None, op0=OP.mult),
            ("xssm", "rms"), ("xssm",))
        for i in range(16):
            add(Tn, lambda e, i=i, lt=lt: e.transpose(
                out=pC[:, i * 128:(i + 1) * 128],
                in_=xssm[:, lt * 2048 + i * 128: lt * 2048 + (i + 1) * 128],
                identity=g["identb"][:, :]),
                ("xssm", "identb"), ("a_pC0", "a_pC1"))
        add(V, lambda e: e.tensor_copy(out=ynt[:, 0:2048],
                                       in_=pC[:, :]),
            ("a_pC0", "a_pC1"), ("ynt",))
        for k in range(16):
            for j in range(2):
                add(Tn, lambda e, k=k, j=j: e.matmul(
                    out=pB[:, j * 512:(j + 1) * 512],
                    lhsT=ynt[:, k * 128:(k + 1) * 128],
                    rhs=wout[:, k * 1024 + j * 512: k * 1024 + (j + 1) * 512],
                    start=(k == 0), stop=(k == 15)),
                    ("ynt", "wout"), (f"a_pB{j}",))
        add(V, lambda e, lt=lt: e.tensor_tensor(
            out=xnew[:, :], in0=pB[:, 0:1024],
            in1=xa[:, lt * D:(lt + 1) * D], op=OP.add),
            ("a_pB0", "a_pB1", f"ax{lt}"), ("xnew",))
        i = c * 2 + lt
        prog.dma(dr["xio"][i * 128:(i + 1) * 128, :], xnew[:, :],
                 reads=("xnew",), writes=(), eng="gpsimd")
    if c == 1:
        dbg_dump(nc, prog, dr, "a_prev1", B["prev"][:, :], (128, 2048), F32, ("prev",))
        dbg_dump(nc, prog, dr, "a_cdb1", B["cdb"][:, :], (128, 32), F32, ("cdb",))
        dbg_dump(nc, prog, dr, "a_yn1", B["xssm"][:, :], (128, 4096), BF16, ("xssm",))
    if c == 0:
        dbg_dump(nc, prog, dr, "a_yn", B["xssm"][:, :], (128, 4096), BF16, ("xssm",))
        dbg_dump(nc, prog, dr, "a_mtg", B["mtg"][:, :], (128, 2048), BF16, ("mtg",))
        dbg_dump(nc, prog, dr, "a_prevb", B["prev"][:, :], (128, 2048), F32, ("prev",))
        dbg_dump(nc, prog, dr, "a_cstm", B["cstm"][:, :], (128, 64), F32, ("cstm",))
        dbg_dump(nc, prog, dr, "a_dectm", B["dectm"][:, :], (128, 64), BF16, ("dectm",))
        dbg_dump(nc, prog, dr, "a_dtptm", B["dtptm"][:, :], (128, 64), BF16, ("dtptm",))
        dbg_dump(nc, prog, dr, "a_xd", B["xd"][:, :], (128, 4096), BF16, ("xd",))
        dbg_dump(nc, prog, dr, "a_Btm", B["Btm"][:, :], (128, 256), BF16, ("Btm",))



def stage_attn(nc, prog, dr, g, ctx):
    sb = lambda n, s_, d=F32: ctx.enter_context(nc.sbuf_tensor(n, list(s_), d))
    ps = lambda n, s_, d=F32: ctx.enter_context(nc.psum_tensor(n, list(s_), d))
    V, S, Tn = "vector", "scalar", "tensor"

    def add(eng, fn, r, w):
        prog.add(eng, fn, reads=r, writes=w)

    pA = ps("c_pA", (128, 2048))
    pB = ps("c_pB", (128, 1024))
    pC = ps("c_pC", (128, 2048), BF16)

    merg = sb("c_merg", (128, 12 * 8 * 128), BF16)
    ln2t = sb("c_ln2t", (128, 8 * 2048), BF16)
    krope = sb("c_krope", (128, 4 * 4096), BF16)
    vT = sb("c_vT", (128, 32 * 528), BF16)
    cpw = sb("c_cpw", (128, 8 * 1024), BF16)
    qlat = sb("c_qlat", (128, 2048), BF16)
    shuf = sb("c_shuf", (128, 2048), BF16)
    qfk = sb("c_qfk", (128, 4 * 2048), BF16)
    pts = qlat[:, :]
    Yq = sb("c_Yq", (128, 8 * 512), BF16)
    ylat = sb("c_ylat", (128, 512), BF16)

    invd = sb("c_invd", (1, 512))
    cosb = sb("c_cosb", (128, 2048), BF16)
    sinb = sb("c_sinb", (128, 2048), BF16)
    xa = sb("c_xa", (128, D))

    qsq = pts[:, :].bitcast(F32)
    latb = sb("c_latb", (128, 12))
    lnw = sb("c_lnw", (128, D))
    invdbc = qlat[:, 0:512]
    xnew = lnw
    st6 = sb("c_st6", (128, 64)); agg = sb("c_agg", (128, 32))
    comb = sb("c_comb", (128, 24))
    qm = sb("c_qm", (128, 4 * 512), BF16)
    nbe = sb("c_nbe", (128, 128), BF16)
    bd2q = sb("c_bd2q", (128, 64), BF16); bd2k = sb("c_bd2k", (128, 64), BF16)
    bd4 = sb("c_bd4", (128, 128), BF16)
    l2oA = sb("c_l2oA", (33, 64), BF16); l2oB = sb("c_l2oB", (97, 64), BF16)
    qlb = sb("c_qlb", (128, 1)); kvlb = sb("c_kvlb", (128, 1))
    wqkb = sb("c_wqkb", (128, 1))
    cpbrow = sb("c_cpbrow", (1, D), BF16)
    cabb = sb("c_cabb", (128, 24), BF16)

    add("sync", lambda e: e.dma_start(
        out=qm[:, :].rearrange("p (j q) -> p j q", j=4),
        in_=dr["qmask"].rearrange("j p q -> p j q")), (), ("qm",))
    prog.dma(nbe[:, :], dr["negbigeye"][:, :], writes=("nbe",))
    prog.dma(qlb[:, :], dr["qlb_t"][:, :], writes=("qlb",))
    prog.dma(kvlb[:, :], dr["kvlb_t"][:, :], writes=("kvlb",))
    prog.dma(wqkb[:, :], dr["wqkb_t"][:, :], writes=("wqkb",))
    prog.dma(lnw[:, 0:24], dr["cab_cols"][:, :], writes=("lnw",))
    add(V, lambda e: e.tensor_copy(out=cabb[:, :], in_=lnw[:, 0:24]),
        ("lnw",), ("cabb",))
    for name, dst, dk, nn in (("q2l_bd2", bd2q, "bd2q", 64),
                              ("kv2l_bd2", bd2k, "bd2k", 64),
                              ("wqk_bd4", bd4, "bd4", 128)):
        prog.dma(lnw[:, 0:nn], dr[name][:, :], writes=("lnw",))
        add(V, lambda e, dst=dst, nn=nn: e.tensor_copy(out=dst[:, :],
                                                       in_=lnw[:, 0:nn]),
            ("lnw",), (dk,))
    for name, dsl, dk in (("l2o_extA", l2oA[:, :], "l2oA"),
                          ("l2o_extB", l2oB[64:97, :], "l2oB")):
        prog.dma(lnw[0:33, 0:64], dr[name][:, :], writes=("lnw",))
        add(V, lambda e, dsl=dsl: e.tensor_copy(out=dsl, in_=lnw[0:33, 0:64]),
            ("lnw",), (dk,))
    prog.dma(lnw[0:1, 0:D], dr["cprojb_row"][:, :], writes=("lnw",))
    add(V, lambda e: e.tensor_copy(out=cpbrow[:, :], in_=lnw[0:1, 0:D]),
        ("lnw",), ("cpbrow",))
    for k in range(8):
        prog.dma(lnw[:, :], dr["cprojT"][k * 128:(k + 1) * 128, :], writes=("lnw",))
        add(V, lambda e, k=k: e.tensor_copy(out=cpw[:, k * D:(k + 1) * D],
                                            in_=lnw[:, :]), ("lnw",), ("cpw",))
    for ty in range(3):
        bd = bd2q if ty == 0 else bd2k
        bdk = "bd2q" if ty == 0 else "bd2k"
        for pr in range(8):
            prog.dma(lnw[:, :], dr["c_attn_w"][ty * 1024 + pr * 128:
                                               ty * 1024 + (pr + 1) * 128, :],
                     writes=("lnw",))
            add(V, lambda e: e.tensor_copy(out=shuf[:, 0:1024], in_=lnw[:, :]),
                ("lnw",), ("shuf",))
            for k in range(8):
                pk = f"c_pB{k % 2}"
                add(Tn, lambda e, k=k, bd=bd: e.matmul(
                    out=pB[:, (k % 2) * 512:(k % 2) * 512 + 64],
                    lhsT=shuf[:, k * 128:(k + 1) * 128],
                    rhs=bd[:, :], start=True, stop=True),
                    ("shuf", bdk), (pk,))
                gidx, hpar = pr // 2, pr % 2
                cbase = ((ty * 4 + gidx) * 8 + k) * 128 + hpar * 64
                add(V, lambda e, cbase=cbase, k=k: e.tensor_copy(
                    out=merg[:, cbase:cbase + 64],
                    in_=pB[:, (k % 2) * 512:(k % 2) * 512 + 64]),
                    (pk,), ("merg",))
            add(Tn, lambda e, ty=ty, pr=pr, bd=bd: e.matmul(
                out=pB[0:64, 512:513], lhsT=bd[:, :],
                rhs=cabb[:, ty * 8 + pr: ty * 8 + pr + 1],
                start=True, stop=True), (bdk, "cabb"), ("c_pB1",))
            add(V, lambda e, ty=ty, pr=pr: e.tensor_copy(
                out=latb[(pr % 2) * 64:(pr % 2) * 64 + 64,
                         ty * 4 + pr // 2: ty * 4 + pr // 2 + 1],
                in_=pB[0:64, 512:513]), ("c_pB1",), ("latb",))
    add(V, lambda e: e.tensor_tensor(
        out=latb[:, 0:4], in0=latb[:, 0:4],
        in1=qlb[:, 0:1].broadcast_to((128, 4)), op=OP.add),
        ("latb", "qlb"), ("latb",))
    add(V, lambda e: e.tensor_tensor(
        out=latb[:, 4:12], in0=latb[:, 4:12],
        in1=kvlb[:, 0:1].broadcast_to((128, 8)), op=OP.add),
        ("latb", "kvlb"), ("latb",))
    for kt in range(32):
        add(V, lambda e, kt=kt: e.memset(
            vT[:, kt * 528:(kt + 1) * 528].rearrange("p (pr c) -> p pr c", pr=8)
            [:, :, 32:34], 1.0), (), ("vT",))
    prog.barrier()

    def rope(dstfn, dstkeys):
        add(V, lambda e: e.stream_shuffle(out=shuf[:, :], in_=qlat[:, :],
                                          mask=_ROT), ("qlat",), ("shuf",))
        add(V, lambda e: e.tensor_tensor(out=qlat[:, :], in0=qlat[:, :],
                                         in1=cosb[:, :], op=OP.mult),
            ("qlat", "cosb"), ("qlat",))
        add(V, lambda e: e.tensor_tensor(out=shuf[:, :], in0=shuf[:, :],
                                         in1=sinb[:, :], op=OP.mult),
            ("shuf", "sinb"), ("shuf",))
        add(V, dstfn, ("qlat", "shuf", "cosb", "sinb"), dstkeys)

    for half in range(2):
        prog.dma(cosb[:, :], dr["rope_cos"][:, half * 2048:(half + 1) * 2048],
                 writes=("cosb",))
        prog.dma(sinb[:, :], dr["rope_sin"][:, half * 2048:(half + 1) * 2048],
                 writes=("sinb",))
        for it in range(16):
            i = half * 16 + it
            prog.dma(xa[:, :], dr["xio"][i * 128:(i + 1) * 128, :], writes=("xa0",))
            add(V, lambda e: e.bn_stats(out=st6[:, 0:6], in_=xa[:, 0:512]),
                ("xa0",), ("st6",))
            add(V, lambda e: e.bn_stats(out=st6[:, 8:14], in_=xa[:, 512:1024]),
                ("xa0",), ("st6",))
            add(V, lambda e: e.bn_aggr(out=agg[:, 0:2], in_=st6[:, 0:6]),
                ("st6",), ("agg",))
            add(V, lambda e: e.bn_aggr(out=agg[:, 4:6], in_=st6[:, 8:14]),
                ("st6",), ("agg",))
            add(V, lambda e: e.tensor_tensor(out=comb[:, 0:1], in0=agg[:, 0:1],
                                             in1=agg[:, 4:5], op=OP.add),
                ("agg",), ("comb",))
            add(V, lambda e: e.tensor_scalar(out=comb[:, 0:1], in0=comb[:, 0:1],
                                             scalar1=0.5, scalar2=None,
                                             op0=OP.mult), ("comb",), ("comb",))
            add(V, lambda e: e.tensor_tensor(out=comb[:, 1:2], in0=agg[:, 1:2],
                                             in1=agg[:, 5:6], op=OP.add),
                ("agg",), ("comb",))
            add(V, lambda e: e.tensor_tensor(out=comb[:, 2:3], in0=agg[:, 0:1],
                                             in1=agg[:, 4:5], op=OP.subtract),
                ("agg",), ("comb",))
            add(V, lambda e: e.tensor_tensor(out=comb[:, 2:3], in0=comb[:, 2:3],
                                             in1=comb[:, 2:3], op=OP.mult),
                ("comb",), ("comb",))
            add(V, lambda e: e.tensor_scalar(out=comb[:, 1:2], in0=comb[:, 1:2],
                                             scalar1=0.5, scalar2=None,
                                             op0=OP.mult), ("comb",), ("comb",))
            add(V, lambda e: e.scalar_tensor_tensor(
                out=comb[:, 3:4], in0=comb[:, 2:3], scalar=0.25, in1=comb[:, 1:2],
                op0=OP.mult, op1=OP.add), ("comb",), ("comb",))
            add(V, lambda e: e.tensor_scalar(out=comb[:, 3:4], in0=comb[:, 3:4],
                                             scalar1=EPS, scalar2=None,
                                             op0=OP.add), ("comb",), ("comb",))
            add(V, lambda e: e.reciprocal(out=comb[:, 3:4], in_=comb[:, 3:4]),
                ("comb",), ("comb",))
            add(S, lambda e: e.activation(out=comb[:, 4:5], in_=comb[:, 3:4],
                                          func=AF.Sqrt), ("comb",), ("comb",))
            add(V, lambda e: e.tensor_scalar(
                out=lnw[:, :], in0=xa[:, :], scalar1=comb[:, 0:1],
                scalar2=comb[:, 4:5], op0=OP.subtract, op1=OP.mult),
                ("xa0", "comb"), ("lnw",))
            for hk in range(2):
                for kkk in range(4):
                    k = hk * 4 + kkk
                    add(Tn, lambda e, k=k, kkk=kkk: e.transpose(
                        out=pB[:, kkk * 128:(kkk + 1) * 128],
                        in_=lnw[:, k * 128:(k + 1) * 128],
                        identity=g["ident"][:, :]),
                        ("lnw", "ident"), ("c_pB0", "c_pB1"))
                for kkk in range(4):
                    k = hk * 4 + kkk
                    add(V, lambda e, k=k, kkk=kkk, it=it: e.tensor_copy(
                        out=ln2t[:, k * 2048 + it * 128: k * 2048 + (it + 1) * 128],
                        in_=pB[:, kkk * 128:(kkk + 1) * 128]),
                        ("c_pB0", "c_pB1"), ("ln2t",))
        for ty in (1, 2):
            for gi in range(4):
                for qb in range(4):
                    pk = f"c_pB{qb % 2}"
                    for k in range(8):
                        add(Tn, lambda e, ty=ty, gi=gi, qb=qb, k=k: e.matmul(
                            out=pB[:, (qb % 2) * 512:(qb % 2) * 512 + 512],
                            lhsT=merg[:, ((ty * 4 + gi) * 8 + k) * 128:
                                      ((ty * 4 + gi) * 8 + k + 1) * 128],
                            rhs=ln2t[:, k * 2048 + qb * 512:
                                     k * 2048 + (qb + 1) * 512],
                            start=(k == 0), stop=(k == 7)),
                            ("merg", "ln2t"), (pk,))
                    bcol = (4 + gi) if ty == 1 else (8 + gi)
                    dstb = qlat if ty == 1 else shuf
                    dk = "qlat" if ty == 1 else "shuf"
                    add(V, lambda e, qb=qb, bcol=bcol, dstb=dstb: e.tensor_scalar(
                        out=dstb[:, qb * 512:(qb + 1) * 512],
                        in0=pB[:, (qb % 2) * 512:(qb % 2) * 512 + 512],
                        scalar1=latb[:, bcol:bcol + 1], scalar2=None,
                        op0=OP.add), (pk, "latb"), (dk,))
                if ty == 1:
                    add(V, lambda e: e.stream_shuffle(out=shuf[:, :],
                                                      in_=qlat[:, :], mask=_ROT),
                        ("qlat",), ("shuf",))
                    add(V, lambda e: e.tensor_tensor(out=qlat[:, :], in0=qlat[:, :],
                                                     in1=cosb[:, :], op=OP.mult),
                        ("qlat", "cosb"), ("qlat",))
                    add(V, lambda e: e.tensor_tensor(out=shuf[:, :], in0=shuf[:, :],
                                                     in1=sinb[:, :], op=OP.mult),
                        ("shuf", "sinb"), ("shuf",))
                    add(V, lambda e, gi=gi, half=half: e.tensor_tensor(
                        out=krope[:, gi * 4096 + half * 2048:
                                  gi * 4096 + (half + 1) * 2048],
                        in0=qlat[:, :], in1=shuf[:, :], op=OP.add),
                        ("qlat", "shuf"), ("krope",))
                else:
                    for ts_ in range(16):
                        add(Tn, lambda e, ts_=ts_: e.transpose(
                            out=pC[:, ts_ * 128:(ts_ + 1) * 128],
                            in_=shuf[:, ts_ * 128:(ts_ + 1) * 128],
                            identity=g["identb"][:, :]),
                            ("shuf", "identb"), ("c_pC0", "c_pC1"))
                    for ts_ in range(16):
                        kt = half * 16 + ts_
                        vv = vT[:, kt * 528 + gi * 132: kt * 528 + (gi + 1) * 132]
                        add(V, lambda e, ts_=ts_, vv=vv: e.tensor_copy(
                            out=vv.rearrange("p (pr c) -> p pr c", pr=2)[:, :, 0:32],
                            in_=pC[:, ts_ * 128:(ts_ + 1) * 128]
                            .rearrange("p (h l) -> p h l", h=4)[:, 0::2, :]),
                            ("c_pC0", "c_pC1"), ("vT",))
                        add(V, lambda e, ts_=ts_, vv=vv: e.tensor_copy(
                            out=vv.rearrange("p (pr c) -> p pr c", pr=2)[:, :, 34:66],
                            in_=pC[:, ts_ * 128:(ts_ + 1) * 128]
                            .rearrange("p (h l) -> p h l", h=4)[:, 1::2, :]),
                            ("c_pC0", "c_pC1"), ("vT",))
        for gi in range(4):
            for qb in range(4):
                pk = f"c_pB{qb % 2}"
                for k in range(8):
                    add(Tn, lambda e, gi=gi, qb=qb, k=k: e.matmul(
                        out=pB[:, (qb % 2) * 512:(qb % 2) * 512 + 512],
                        lhsT=merg[:, (gi * 8 + k) * 128:(gi * 8 + k + 1) * 128],
                        rhs=ln2t[:, k * 2048 + qb * 512: k * 2048 + (qb + 1) * 512],
                        start=(k == 0), stop=(k == 7)),
                        ("merg", "ln2t"), (pk,))
                add(V, lambda e, gi=gi, qb=qb: e.tensor_scalar(
                    out=qlat[:, qb * 512:(qb + 1) * 512],
                    in0=pB[:, (qb % 2) * 512:(qb % 2) * 512 + 512],
                    scalar1=latb[:, gi:gi + 1], scalar2=None, op0=OP.add),
                    (pk, "latb"), ("qlat",))
            rope(lambda e: e.tensor_tensor(out=qlat[:, :], in0=qlat[:, :],
                                           in1=shuf[:, :], op=OP.add), ("qlat",))
            for qb in range(4):
                pk = f"c_pB{qb % 2}"
                add(Tn, lambda e, qb=qb: e.matmul(
                    out=pB[:, (qb % 2) * 512:(qb % 2) * 512 + 512],
                    lhsT=bd4[:, :], rhs=qlat[:, qb * 512:(qb + 1) * 512],
                    start=True, stop=True), ("bd4", "qlat"), (pk,))
                add(V, lambda e, gi=gi, qb=qb: e.tensor_scalar(
                    out=qfk[:, gi * 2048 + qb * 512:
                            gi * 2048 + (qb + 1) * 512],
                    in0=pB[:, (qb % 2) * 512:(qb % 2) * 512 + 512],
                    scalar1=wqkb[:, 0:1], scalar2=0.1767766952966369,
                    op0=OP.add, op1=OP.mult), (pk, "wqkb"), ("qfk",))
        for qb in range(4):
            qg = half * 4 + qb
            for gi in range(4):
                nkt = (qg + 1) * 4
                for kt in range(nkt):
                    pp = 0
                    j = kt - qg * 4
                    for hl in range(4):
                        add(Tn, lambda e, gi=gi, kt=kt, hl=hl, qb=qb: e.matmul(
                            out=pA[:, hl * 512:(hl + 1) * 512],
                            lhsT=krope[32 * hl:32 * hl + 32,
                                       gi * 4096 + kt * 128:
                                       gi * 4096 + (kt + 1) * 128],
                            rhs=qfk[32 * hl:32 * hl + 32,
                                    gi * 2048 + qb * 512:
                                    gi * 2048 + (qb + 1) * 512],
                            start=True, stop=(j < 0),
                            tile_position=(32 * hl, 0)),
                            ("krope", "qfk"), (f"c_pA{hl}",))
                        if j >= 0:
                            add(Tn, lambda e, hl=hl, j=j: e.matmul(
                                out=pA[:, hl * 512:(hl + 1) * 512],
                                lhsT=nbe[:, 0:128],
                                rhs=qm[:, j * 512:(j + 1) * 512],
                                start=False, stop=True),
                                ("qm", "nbe"), (f"c_pA{hl}",))
                    add(S, lambda e: e.activation(
                        out=pts[:, :], in_=pA[:, :], func=AF.Exp),
                        ("c_pA0", "c_pA1", "c_pA2", "c_pA3"), ("qlat",))
                    for pr in range(2):
                        pglob = gi * 2 + pr
                        for ab in range(2):
                            out_sl = (pB[0:33, pr * 512:(pr + 1) * 512] if ab == 0
                                      else pB[64:97, pr * 512:(pr + 1) * 512])
                            add(Tn, lambda e, kt=kt, pglob=pglob, pr=pr, ab=ab,
                                pp=pp, out_sl=out_sl, nkt=nkt: e.matmul(
                                out=out_sl,
                                lhsT=vT[:, kt * 528 + pglob * 66 + ab * 33:
                                        kt * 528 + pglob * 66 + (ab + 1) * 33],
                                rhs=pts[:, (pr * 2 + ab) * 512:
                                        (pr * 2 + ab + 1) * 512],
                                start=(kt == 0 and ab == 0),
                                stop=(kt == nkt - 1 and ab == 1),
                                tile_position=(0, 64 * ab)),
                                ("qlat", "vT"), (f"c_pB{pr}",))
                for pr in range(2):
                    add(V, lambda e, pr=pr: e.tensor_copy(
                        out=ylat[:, 0:512],
                        in_=pB[:, pr * 512:(pr + 1) * 512]),
                        (f"c_pB{pr}",), ("ylat",))
                    add(Tn, lambda e, pr=pr: e.matmul(
                        out=pA[0:64, pr * 512:(pr + 1) * 512],
                        lhsT=l2oA[:, :], rhs=ylat[0:33, 0:512],
                        start=True, stop=True),
                        ("l2oA", "ylat"), (f"c_pA{pr}",))
                    add(Tn, lambda e, pr=pr: e.matmul(
                        out=pA[64:128, pr * 512:(pr + 1) * 512],
                        lhsT=l2oB[64:97, :],
                        rhs=ylat[64:97, 0:512],
                        start=True, stop=True, tile_position=(64, 64)),
                        ("l2oB", "ylat"), (f"c_pA{pr}",))
                    add(V, lambda e, pr=pr: e.reciprocal(
                        out=invd[0:1, 0:512],
                        in_=ylat[32:33, 0:512]),
                        ("ylat",), ("invd",))
                    add(Tn, lambda e, pr=pr: e.matmul(
                        out=pB[0:64, pr * 512:(pr + 1) * 512],
                        lhsT=g["ones_row"][0:1, 0:64],
                        rhs=invd[0:1, 0:512],
                        start=True, stop=True),
                        ("invd", "ones_row"), (f"c_pB{pr}",))
                    add(V, lambda e, pr=pr: e.reciprocal(
                        out=invd[0:1, 0:512],
                        in_=ylat[64:65, 0:512]),
                        ("ylat",), ("invd",))
                    add(Tn, lambda e, pr=pr: e.matmul(
                        out=pB[64:128, pr * 512:(pr + 1) * 512],
                        lhsT=g["ones_row"][0:1, 0:64],
                        rhs=invd[0:1, 0:512],
                        start=True, stop=True, tile_position=(0, 64)),
                        ("invd", "ones_row"), (f"c_pB{pr}",))
                    add(S, lambda e, pr=pr: e.copy(
                        out=invdbc[:, :],
                        in_=pB[:, pr * 512:(pr + 1) * 512]),
                        (f"c_pB{pr}",), ("qlat",))
                    add(V, lambda e, pr=pr, gi=gi: e.tensor_tensor(
                        out=Yq[:, (gi * 2 + pr) * 512:(gi * 2 + pr + 1) * 512],
                        in0=pA[:, pr * 512:(pr + 1) * 512],
                        in1=invdbc[:, :], op=OP.mult),
                        (f"c_pA{pr}", "qlat"), ("Yq",))
            for qs in range(4):
                i = qg * 4 + qs
                prog.dma(xa[:, :], dr["xio"][i * 128:(i + 1) * 128, :],
                         writes=("xa0",))
                for dc in range(8):
                    for jj in range(2):
                        add(Tn, lambda e, dc=dc, jj=jj, qs=qs: e.matmul(
                            out=pA[:, 1024 + jj * 512: 1024 + (jj + 1) * 512],
                            lhsT=Yq[:, dc * 512 + qs * 128:
                                    dc * 512 + (qs + 1) * 128],
                            rhs=cpw[:, dc * D + jj * 512: dc * D + (jj + 1) * 512],
                            start=(dc == 0), stop=False),
                            ("Yq", "cpw"), (f"c_pA{2 + jj}",))
                for jj in range(2):
                    add(Tn, lambda e, jj=jj: e.matmul(
                        out=pA[:, 1024 + jj * 512: 1024 + (jj + 1) * 512],
                        lhsT=g["onesb_row"][:, :],
                        rhs=cpbrow[:, jj * 512:(jj + 1) * 512],
                        start=False, stop=True),
                        ("cpbrow", "onesb_row"), (f"c_pA{2 + jj}",))
                add(V, lambda e: e.tensor_tensor(
                    out=xnew[:, :], in0=pA[:, 1024:2048], in1=xa[:, :],
                    op=OP.add), ("c_pA2", "c_pA3", "xa0"), ("lnw",))
                prog.dma(dr["xio"][i * 128:(i + 1) * 128, :], xnew[:, :],
                         reads=("lnw",), writes=(), eng="gpsimd")


def kernel(**inputs):
    sh, per_core = host_prep(inputs)
    nc = build_nc()
    in_maps = []
    for b in range(NCORE):
        m = dict(sh)
        m.update(per_core[b])
        in_maps.append(m)
    res = run_bass_kernel_spmd(nc, in_maps, core_ids=list(range(NCORE)))
    out = np.stack([np.asarray(r["xio"]) for r in res.results], axis=0)
    return out.astype(np.float32)


# version 11 (final)
# speedup vs baseline: 1.0267x; 1.0267x over previous
import numpy as np
import ml_dtypes
from contextlib import ExitStack

import concourse.bass as bass
import concourse.mybir as mybir
from concourse.bass_utils import run_bass_kernel_spmd
from concourse.alu_op_type import AluOpType as OP

F32 = mybir.dt.float32
BF16 = mybir.dt.bfloat16
I32 = mybir.dt.int32
AF = mybir.ActivationFunctionType
AX = mybir.AxisListType

T = 4096
D = 1024
NCORE = 8
EPS = 1e-5

D_IN = 2048
D_ST = 128
NH = 32
HD = 64
CH = 256
NCH = T // CH
CONVD = D_IN + 2 * D_ST
D_PROJ = 2 * D_IN + 2 * D_ST + NH

AH = 16
AHD = 64
LAT = 32

STAGES = ("A", "B", "C", "D")
DEBUG = False
FFN_CHUNKS = None
_DBG = {}

BIG = 1e30
_ROT = [(i + 1 if i % 2 == 0 else i - 1) for i in range(32)]


ENGINES = ("sync", "tensor", "vector", "scalar", "gpsimd")


PARANOID = False
SCOPES = False
DMA_PIN = ("",)
SCHED = False


class Prog:
    def __init__(self, nc):
        self.nc = nc
        self.ops = []
        self.last_w = {}
        self.readers = {}
        self._last_q = {}
        self._open = set()
        self._pin = None
        self.scope = None

    def add(self, eng, fn, reads=(), writes=(), step=None):
        idx = len(self.ops)
        deps = set()
        for k in reads:
            j = self.last_w.get(k, self._pin)
            if j is not None:
                deps.add(j)
        for k in writes:
            j = self.last_w.get(k, self._pin)
            if j is not None:
                deps.add(j)
            for r in self.readers.get(k, ()):
                deps.add(r)
        if step is None:
            step = 16 if eng in ("sync", "gpsimd") else 1
        self.ops.append(dict(idx=idx, eng=eng, fn=fn, deps=deps,
                             step=step, scope=self.scope, w=tuple(writes)))
        self._open.difference_update(deps)
        self._open.add(idx)
        for k in reads:
            self.readers.setdefault(k, []).append(idx)
        for k in writes:
            self.last_w[k] = idx
            self.readers[k] = []
        return idx

    def dma(self, out, in_, reads=(), writes=(), eng="sync"):
        return self.add(eng, lambda e, o=out, i=in_: e.dma_start(out=o, in_=i),
                        reads=reads, writes=writes, step=16)

    def barrier(self):
        deps = set(self._open)
        pin = None
        for e in ENGINES:
            self.add(e, lambda eng: eng.nop(), reads=(), writes=(), step=1)
            self.ops[-1]["deps"] = deps - {self.ops[-1]["idx"]}
            self._open.difference_update(deps)
            if e == "sync":
                pin = self.ops[-1]["idx"]
        for k in list(self.last_w):
            self.last_w[k] = pin
            self.readers[k] = []
        self._pin = pin

    def emit(self, block, sems):
        ops = self.ops
        import heapq
        n = len(ops)
        succ = [[] for _ in range(n)]
        for op in ops:
            for dd in op["deps"]:
                succ[dd].append(op["idx"])

        def est(op):
            if op["step"] == 16:
                return 1400
            return {"tensor": 350, "vector": 500, "scalar": 700,
                    "sync": 1400, "gpsimd": 1400}[op["eng"]]
        prio = [0] * n
        for op in reversed(ops):
            i = op["idx"]
            mx = 0
            for j in succ[i]:
                if prio[j] > mx:
                    mx = prio[j]
            prio[i] = est(op) + mx
        if SCHED:
            if DMA_PIN:
                for op in ops:
                    if (op["step"] == 16 and
                            any(k.startswith(p) for p in DMA_PIN
                                for k in (op["w"] or ("?",)))):
                        prio[op["idx"]] = -op["idx"]
            indeg = [len(op["deps"]) for op in ops]
            ready = [(-prio[op["idx"]], op["idx"]) for op in ops
                     if indeg[op["idx"]] == 0]
            heapq.heapify(ready)
            order = []
            while ready:
                _, i = heapq.heappop(ready)
                order.append(i)
                for j in succ[i]:
                    indeg[j] -= 1
                    if indeg[j] == 0:
                        heapq.heappush(ready, (-prio[j], j))
            assert len(order) == n
        else:
            order = list(range(n))
        lastq = {}
        for i in order:
            e = ops[i]["eng"]
            if e in ("sync", "gpsimd"):
                j = lastq.get(e)
                if j is not None:
                    ops[i]["deps"].add(j)
                lastq[e] = i
        byidx = ops
        ops = [ops[i] for i in order]
        need_inc = set()
        for op in ops:
            if op["step"] == 16:
                need_inc.add(op["idx"])
            for d in op["deps"]:
                if (byidx[d]["eng"] != op["eng"] or byidx[d]["step"] == 16
                        or op["eng"] != "tensor" or PARANOID):
                    need_inc.add(d)
        inc_val = {}
        cum = {e: 0 for e in ENGINES}
        for op in ops:
            if op["idx"] in need_inc:
                cum[op["eng"]] += op["step"]
                inc_val[op["idx"]] = cum[op["eng"]]
        per_eng = {e: [] for e in ENGINES}
        for op in ops:
            per_eng[op["eng"]].append(op)

        def make(e):
            eng_ops = per_eng[e]

            def body(eng):
                hw = {p: 0 for p in ENGINES}
                cur = [None, None]

                def set_scope(sc):
                    if not SCOPES or sc == cur[0]:
                        return
                    if cur[0] is not None:
                        self.nc.leave_named_scope(cur[0], cur[1], False)
                    cur[0], cur[1] = sc, None
                    if sc is not None:
                        cur[1], _ = self.nc.enter_named_scope(sc, False)
                for op in eng_ops:
                    set_scope(op["scope"])
                    for d in sorted(op["deps"]):
                        dop = byidx[d]
                        if (dop["eng"] == e and dop["step"] != 16
                                and e == "tensor" and not PARANOID):
                            continue
                        v = inc_val[d]
                        if v > hw[dop["eng"]]:
                            eng.wait_ge(sems[dop["eng"]], v)
                            hw[dop["eng"]] = v
                    inst = op["fn"](eng)
                    if op["idx"] in need_inc:
                        inst.then_inc(sems[op["eng"]], op["step"])
                set_scope(None)
            return body

        for e in ENGINES:
            if per_eng[e]:
                getattr(block, e)(make(e))


def _rope_tables():
    inv_freq = 1.0 / 10000.0 ** (np.arange(0, LAT, 2, dtype=np.float32) / LAT)
    t = np.arange(T, dtype=np.float32)
    freqs = np.outer(t, inv_freq)
    cos = np.repeat(np.cos(freqs), 2, axis=-1)[:, :LAT]
    sin = np.repeat(np.sin(freqs), 2, axis=-1)[:, :LAT]
    sinp = sin.copy()
    sinp[:, 0::2] = -sin[:, 0::2]
    cos4 = np.tile(cos.T, (4, 1))
    sin4 = np.tile(sinp.T, (4, 1))
    return (cos4.astype(ml_dtypes.bfloat16), sin4.astype(ml_dtypes.bfloat16))


def host_prep(I):
    f32 = np.float32
    sh = {}
    W_inT = np.ascontiguousarray(I["L0_attn_a_W_in"].T)
    sh["WzdtT"] = np.ascontiguousarray(
        np.concatenate([W_inT[:, :D_IN], W_inT[:, D_IN + CONVD:]], axis=1))
    sh["WxbcT"] = np.ascontiguousarray(W_inT[:, D_IN:D_IN + CONVD])
    sh["WoutT"] = np.ascontiguousarray(I["L0_attn_a_W_out"].T)
    cw = I["L0_attn_a_conv_w"]
    diag = np.zeros((18 * 4, 128, 128), f32)
    idx = np.arange(128)
    for n in range(18):
        for k in range(4):
            diag[n * 4 + k, idx, idx] = cw[n * 128:(n + 1) * 128, k]
    sh["convdiag"] = diag
    sh["convbcols"] = np.ascontiguousarray(I["L0_attn_a_conv_b"].reshape(18, 128).T)
    sh["dtbias_col"] = I["L0_attn_a_dt_bias"].reshape(NH, 1).copy()
    sh["Alog_col"] = I["L0_attn_a_A_log"].reshape(NH, 1).copy()
    sh["D_col"] = I["L0_attn_a_D"].reshape(NH, 1).copy()
    sh["normw_cols"] = np.ascontiguousarray(I["L0_attn_a_norm_w"].reshape(16, 128).T)
    sh["fc1_q"] = I["L0_ffn_fc1_q"]
    sh["fc2_q"] = I["L0_ffn_fc2_q"]
    sh["fc1_scales"] = I["L0_ffn_fc1_scales"]
    sh["fc2_scales"] = I["L0_ffn_fc2_scales"]
    sh["b1cols"] = np.ascontiguousarray(I["L0_ffn_fc1_bias"].reshape(32, 128).T)
    sh["b2row"] = I["L0_ffn_fc2_bias"].reshape(1, D).copy()
    sh["gcols"] = np.ascontiguousarray(I["L0_ffn_ln_gamma"].reshape(8, 128).T)
    sh["bcols"] = np.ascontiguousarray(I["L0_ffn_ln_beta"].reshape(8, 128).T)
    sh["c_attn_w"] = I["L0_attn_b_c_attn_w"]
    sh["cab_cols"] = np.ascontiguousarray(I["L0_attn_b_c_attn_b"].reshape(24, 128).T)
    q2l = I["L0_attn_b_q2l_w"]
    kv2l = I["L0_attn_b_kv2l_w"]
    bd2 = np.zeros((128, 64), f32)
    bd2[0:64, 0:32] = q2l.T
    bd2[64:128, 32:64] = q2l.T
    sh["q2l_bd2"] = bd2
    bd2k = np.zeros((128, 64), f32)
    bd2k[0:64, 0:32] = kv2l.T
    bd2k[64:128, 32:64] = kv2l.T
    sh["kv2l_bd2"] = bd2k
    wqk = I["L0_attn_b_wqk_w"]
    bd4 = np.zeros((128, 128), f32)
    for i in range(4):
        bd4[32 * i:32 * i + 32, 32 * i:32 * i + 32] = wqk.T
    sh["wqk_bd4"] = bd4
    sh["qlb_t"] = np.tile(I["L0_attn_b_q2l_b"], 4).reshape(128, 1).copy()
    sh["kvlb_t"] = np.tile(I["L0_attn_b_kv2l_b"], 4).reshape(128, 1).copy()
    sh["wqkb_t"] = np.tile(I["L0_attn_b_wqk_b"], 4).reshape(128, 1).copy()
    l2o = I["L0_attn_b_l2o_w"]
    sh["cprojT"] = np.ascontiguousarray(I["L0_attn_b_cproj_w"].T)
    sh["cprojb_row"] = I["L0_attn_b_cproj_b"].reshape(1, D).copy()
    sh["ident"] = np.eye(128, dtype=f32)
    sh["identb"] = np.eye(128, dtype=f32).astype(ml_dtypes.bfloat16)
    step = (np.arange(128)[:, None] < np.arange(128)[None, :]).astype(f32)
    sh["stepb"] = step.astype(ml_dtypes.bfloat16)
    sh["negbigeye"] = (-BIG * np.eye(128)).astype(ml_dtypes.bfloat16)
    sel = np.zeros((32, 32 * 256), f32)
    for h in range(32):
        sel[h, 256 * h:256 * (h + 1)] = 1.0
    sh["selb"] = sel.astype(ml_dtypes.bfloat16)
    sh["ones_row"] = np.ones((1, 128), f32)
    selc0 = np.zeros((128, 8), f32)
    for k in range(24):
        selc0[k, k % 8] = 1.0
    sh["selc0_h"] = selc0.astype(ml_dtypes.bfloat16)
    selbc = np.zeros((128, 1024), f32)
    for k in range(24):
        hl = k % 8
        selbc[k, 128 * hl:128 * (hl + 1)] = -1.0
    sh["selbc_h"] = selbc.astype(ml_dtypes.bfloat16)
    sh["negones_rowb"] = (-np.ones((1, 128))).astype(ml_dtypes.bfloat16)
    cos4, sin4 = _rope_tables()
    sh["rope_cos"] = cos4
    sh["rope_sin"] = sin4
    qm = np.zeros((4, 128, 512), f32)
    kk = np.arange(128)[:, None]; qq = np.arange(512)[None, :]
    for j in range(4):
        qm[j] = (kk > qq - 128 * j)
    sh["qmask"] = qm.astype(ml_dtypes.bfloat16)
    sh["negones_sq"] = (-np.ones((128, 128))).astype(ml_dtypes.bfloat16)
    l2o = I["L0_attn_b_l2o_w"]
    l2ob = I["L0_attn_b_l2o_b"].reshape(1, 64)
    sh["l2o_extA"] = np.concatenate([l2o.T, l2ob], axis=0).copy()
    sh["l2o_extB"] = np.concatenate([l2ob, l2o.T], axis=0).copy()
    per_core = []
    for b in range(NCORE):
        per_core.append({"x_in": np.ascontiguousarray(I["L0_attn_a_inputs"][b])})
    return sh, per_core


def build_nc():
    nc = bass.Bass("TRN2", target_bir_lowering=False)
    dr = {}

    def din(name, shape, dt=F32):
        dr[name] = nc.dram_tensor(name, list(shape), dt, kind="ExternalInput").ap()

    din("x_in", (T, D))
    din("WzdtT", (D, 2080)); din("WxbcT", (D, CONVD)); din("WoutT", (D_IN, D))
    din("convdiag", (72, 128, 128)); din("convbcols", (128, 18))
    din("dtbias_col", (NH, 1)); din("Alog_col", (NH, 1)); din("D_col", (NH, 1))
    din("normw_cols", (128, 16))
    din("fc1_q", (D, 4096), I32); din("fc2_q", (4096, D), I32)
    din("fc1_scales", (8, 4096)); din("fc2_scales", (32, D))
    din("b1cols", (128, 32)); din("b2row", (1, D))
    din("gcols", (128, 8)); din("bcols", (128, 8))
    din("c_attn_w", (3072, D)); din("cab_cols", (128, 24))
    din("q2l_bd2", (128, 64)); din("kv2l_bd2", (128, 64)); din("wqk_bd4", (128, 128))
    din("qlb_t", (128, 1)); din("kvlb_t", (128, 1)); din("wqkb_t", (128, 1))
    din("l2o_extA", (33, 64)); din("l2o_extB", (33, 64))
    din("cprojT", (D, D)); din("cprojb_row", (1, D))
    din("qmask", (4, 128, 512), BF16); din("negones_sq", (128, 128), BF16)
    din("ident", (128, 128)); din("identb", (128, 128), BF16)
    din("stepb", (128, 128), BF16); din("negbigeye", (128, 128), BF16)
    din("selb", (32, 32 * 256), BF16)
    din("ones_row", (1, 128)); din("negones_rowb", (1, 128), BF16)
    din("selc0_h", (128, 8), BF16); din("selbc_h", (128, 1024), BF16)
    din("rope_cos", (128, T), BF16); din("rope_sin", (128, T), BF16)

    dr["xio"] = nc.dram_tensor("xio", [T, D], F32, kind="ExternalOutput").ap()
    dr["w1bf"] = nc.dram_tensor("w1bf", [128, 8 * 4096], BF16, kind="Internal").ap()
    dr["w2bf"] = nc.dram_tensor("w2bf", [128, 32 * D], BF16, kind="Internal").ap()

    prog = Prog(nc)
    with ExitStack() as ctx:
        sems = {e: ctx.enter_context(nc.semaphore(f"sem_{e}")) for e in ENGINES}
        g = {}

        def gsb(name, shape, dt=F32):
            t_ = ctx.enter_context(nc.sbuf_tensor("g_" + name, list(shape), dt))
            g[name] = t_
            return t_

        gsb("ident", (128, 128)); gsb("identb", (128, 128), BF16)
        gsb("ones_row", (1, 128))
        gsb("onesb_row", (1, 128), BF16)
        gsb("negones_rowb", (1, 128), BF16)
        gsb("ones_col", (128, 1))
        for n in ("ident", "identb", "ones_row"):
            prog.dma(g[n][:, :], dr[n][:, :], writes=(n,))
        prog.add("vector", lambda e: e.memset(g["ones_col"][:, :], 1.0),
                 writes=("ones_col",))
        prog.add("vector", lambda e: e.tensor_copy(out=g["onesb_row"][:, :],
                                                   in_=g["ones_row"][:, :]),
                 reads=("ones_row",), writes=("onesb_row",))
        prog.dma(g["negones_rowb"][:, :], dr["negones_rowb"][:, :],
                 writes=("negones_rowb",))

        if "A" in STAGES:
            prog.scope = "stA"
            with ExitStack() as sctx:
                stage_a(nc, prog, dr, g, sctx)
        else:
            stage_copy(nc, prog, dr, g, ctx)
        prog.barrier()
        if "B" in STAGES:
            prog.scope = "stB"
            with ExitStack() as sctx:
                stage_ffn(nc, prog, dr, g, sctx, spill=("D" in STAGES))
            prog.barrier()
        if "C" in STAGES:
            prog.scope = "stC"
            with ExitStack() as sctx:
                stage_attn(nc, prog, dr, g, sctx)
            prog.barrier()
        if "D" in STAGES:
            prog.scope = "stD"
            with ExitStack() as sctx:
                stage_ffn(nc, prog, dr, g, sctx, reload=True)
            prog.barrier()

        blk = ctx.enter_context(nc.Block())
        prog.emit(blk, sems)
    return nc


def stage_copy(nc, prog, dr, g, ctx):
    buf = ctx.enter_context(nc.sbuf_tensor("cp", [128, 2 * D], F32))
    for i in range(T // 128):
        p = i % 2
        key = f"cp{p}"
        prog.dma(buf[:, p * D:(p + 1) * D], dr["x_in"][i * 128:(i + 1) * 128, :],
                 writes=(key,))
        prog.dma(dr["xio"][i * 128:(i + 1) * 128, :], buf[:, p * D:(p + 1) * D],
                 reads=(key,), writes=(), eng="gpsimd")


def dbg_dump(nc, prog, dr, name, ap, shape, dt, keys):
    if not DEBUG or name in _DBG:
        return
    _DBG[name] = True
    t = nc.dram_tensor("dbg_" + name, list(shape), dt, kind="ExternalOutput").ap()
    prog.dma(t[:, :] if len(shape) == 2 else t[:], ap, reads=tuple(keys), writes=())


def emit_ln_stats(prog, st6, agg, comb, xts, xkeys, kp, fused):
    K6, KA, KC = kp + "st6", kp + "agg", kp + "comb"
    for tt in range(4):
        prog.add("vector", lambda e, tt=tt: e.bn_stats(
            out=st6[:, 16 * tt:16 * tt + 6], in_=xts[tt][:, 0:512]),
            reads=(xkeys[tt],), writes=(K6,))
        prog.add("vector", lambda e, tt=tt: e.bn_stats(
            out=st6[:, 16 * tt + 8:16 * tt + 14], in_=xts[tt][:, 512:1024]),
            reads=(xkeys[tt],), writes=(K6,))
    for tt in range(4):
        prog.add("vector", lambda e, tt=tt: e.bn_aggr(
            out=agg[:, 8 * tt:8 * tt + 2], in_=st6[:, 16 * tt:16 * tt + 6]),
            reads=(K6,), writes=(KA,))
        prog.add("vector", lambda e, tt=tt: e.bn_aggr(
            out=agg[:, 8 * tt + 4:8 * tt + 6], in_=st6[:, 16 * tt + 8:16 * tt + 14]),
            reads=(K6,), writes=(KA,))
    a4 = agg[:, 0:32].rearrange("p (t k) -> p t k", k=8)
    m0 = a4[:, :, 0:1].squeeze(2); v0 = a4[:, :, 1:2].squeeze(2)
    m1 = a4[:, :, 4:5].squeeze(2); v1 = a4[:, :, 5:6].squeeze(2)
    prog.add("vector", lambda e: e.tensor_tensor(out=comb[:, 0:4], in0=m0, in1=m1,
                                                 op=OP.add),
             reads=(KA,), writes=(KC,))
    prog.add("vector", lambda e: e.tensor_scalar(out=comb[:, 0:4], in0=comb[:, 0:4],
                                                 scalar1=0.5, scalar2=None, op0=OP.mult),
             reads=(KC,), writes=(KC,))
    prog.add("vector", lambda e: e.tensor_tensor(out=comb[:, 4:8], in0=v0, in1=v1,
                                                 op=OP.add),
             reads=(KA,), writes=(KC,))
    prog.add("vector", lambda e: e.tensor_scalar(out=comb[:, 4:8], in0=comb[:, 4:8],
                                                 scalar1=0.5, scalar2=None, op0=OP.mult),
             reads=(KC,), writes=(KC,))
    prog.add("vector", lambda e: e.tensor_tensor(out=comb[:, 12:16], in0=m0, in1=m1,
                                                 op=OP.subtract),
             reads=(KA,), writes=(KC,))
    prog.add("vector", lambda e: e.tensor_tensor(out=comb[:, 8:12], in0=comb[:, 12:16],
                                                 in1=comb[:, 12:16], op=OP.mult),
             reads=(KC,), writes=(KC,))
    prog.add("vector", lambda e: e.scalar_tensor_tensor(
        out=comb[:, 16:20], in0=comb[:, 8:12], scalar=0.25, in1=comb[:, 4:8],
        op0=OP.mult, op1=OP.add),
        reads=(KC,), writes=(KC,))
    if fused:
        prog.add("vector", lambda e: e.tensor_scalar(
            out=comb[:, 12:16], in0=comb[:, 16:20],
            scalar1=EPS, scalar2=None, op0=OP.add),
            reads=(KC,), writes=(KC,))
        prog.add("vector", lambda e: e.reciprocal(
            out=comb[:, 12:16], in_=comb[:, 12:16]),
            reads=(KC,), writes=(KC,))
        prog.add("scalar", lambda e: e.activation(
            out=comb[:, 8:12], in_=comb[:, 12:16], func=AF.Sqrt),
            reads=(KC,), writes=(KC,))
        prog.add("vector", lambda e: e.tensor_tensor(
            out=comb[:, 4:8], in0=comb[:, 0:4], in1=comb[:, 8:12],
            op=OP.mult), reads=(KC,), writes=(KC,))
        prog.add("vector", lambda e: e.tensor_scalar(
            out=comb[:, 12:16], in0=comb[:, 8:12],
            scalar1=1.0, scalar2=None, op0=OP.add),
            reads=(KC,), writes=(KC,))
        prog.add("vector", lambda e: e.tensor_scalar(
            out=comb[:, 16:20], in0=comb[:, 16:20],
            scalar1=1.0 + EPS, scalar2=EPS * EPS, op0=OP.mult, op1=OP.add),
            reads=(KC,), writes=(KC,))
    else:
        prog.add("vector", lambda e: e.tensor_scalar(
            out=comb[:, 16:20], in0=comb[:, 16:20],
            scalar1=EPS, scalar2=None, op0=OP.add),
            reads=(KC,), writes=(KC,))
    prog.add("vector", lambda e: e.reciprocal(out=comb[:, 16:20], in_=comb[:, 16:20]),
             reads=(KC,), writes=(KC,))
    prog.add("scalar", lambda e: e.activation(out=comb[:, 20:24], in_=comb[:, 16:20],
                                              func=AF.Sqrt),
             reads=(KC,), writes=(KC,))


def stage_ffn(nc, prog, dr, g, ctx, spill=False, reload=False):
    sb = lambda n, s, d=F32: ctx.enter_context(nc.sbuf_tensor(n, list(s), d))
    ps = lambda n, s, d=F32: ctx.enter_context(nc.psum_tensor(n, list(s), d))
    tag = "ffnD" if reload else "ffnB"

    w1 = sb(tag + "w1", (128, 8 * 4096), BF16)
    w2 = sb(tag + "w2", (128, 32 * D), BF16)
    b1c = sb(tag + "b1c", (128, 32))
    b2r = sb(tag + "b2r", (1, D))
    gc = sb(tag + "gc", (128, 8))
    bc = sb(tag + "bc", (128, 8))
    prog.dma(b1c[:, :], dr["b1cols"][:, :], writes=("b1c",))
    prog.dma(b2r[:, :], dr["b2row"][:, :], writes=("b2r",))
    prog.dma(gc[:, :], dr["gcols"][:, :], writes=("gc",))
    prog.dma(bc[:, :], dr["bcols"][:, :], writes=("bc",))

    if reload:
        prog.dma(w1[:, :], dr["w1bf"][:, :], writes=("w1",))
        prog.dma(w2[:, :], dr["w2bf"][:, :], writes=("w2",))
    else:
      with ExitStack() as dctx:
        sbq = lambda n, s, d=F32: dctx.enter_context(nc.sbuf_tensor(n, list(s), d))
        psq = lambda n, s, d=F32: dctx.enter_context(nc.psum_tensor(n, list(s), d))
        qb = sbq(tag + "qb", (128, 2 * 4096), I32)
        srow = sbq(tag + "srow", (1, 4096))
        psE = psq(tag + "psE", (128, 2048))
        for gi in range(8):
            pq = gi % 2
            qk = f"qb{pq}"
            prog.dma(qb[:, pq * 4096:(pq + 1) * 4096],
                     dr["fc1_q"][gi * 128:(gi + 1) * 128, :],
                     writes=(qk,))
            prog.dma(srow[:, :], dr["fc1_scales"][gi:gi + 1, :],
                     writes=("srow",))
            for half in range(2):
                for j in range(4):
                    c0 = half * 2048 + j * 512
                    prog.add("tensor",
                             lambda e, c0=c0, j=j: e.matmul(
                                 out=psE[:, j * 512:(j + 1) * 512],
                                 lhsT=g["ones_row"][:, :],
                                 rhs=srow[:, c0:c0 + 512], start=True, stop=True),
                             reads=("srow", "ones_row"), writes=("psE",))
                prog.add("vector",
                         lambda e, gi=gi, pq=pq, half=half: e.scalar_tensor_tensor(
                             out=w1[:, gi * 4096 + half * 2048: gi * 4096 + half * 2048 + 2048],
                             in0=qb[:, pq * 4096 + half * 2048: pq * 4096 + half * 2048 + 2048],
                             scalar=-128.0, in1=psE[:, :],
                             op0=OP.add, op1=OP.mult),
                         reads=(qk, "psE", "srow"), writes=("w1",))
        srow2 = sbq(tag + "srow2", (1, D))
        for gi in range(32):
            pq = gi % 2
            qk = f"qb{pq}"
            prog.dma(qb[:, pq * 4096:pq * 4096 + D],
                     dr["fc2_q"][gi * 128:(gi + 1) * 128, :],
                     writes=(qk,))
            prog.dma(srow2[:, :], dr["fc2_scales"][gi:gi + 1, :],
                     writes=("srow2",))
            for j in range(2):
                prog.add("tensor",
                         lambda e, j=j: e.matmul(
                             out=psE[:, j * 512:(j + 1) * 512],
                             lhsT=g["ones_row"][:, :],
                             rhs=srow2[:, j * 512:(j + 1) * 512], start=True, stop=True),
                         reads=("srow2", "ones_row"), writes=("psE",))
            prog.add("vector",
                     lambda e, gi=gi, pq=pq: e.scalar_tensor_tensor(
                         out=w2[:, gi * D:(gi + 1) * D],
                         in0=qb[:, pq * 4096:pq * 4096 + D],
                         scalar=-128.0, in1=psE[:, 0:D],
                         op0=OP.add, op1=OP.mult),
                     reads=(qk, "psE", "srow2"), writes=("w2",))
        if spill:
            prog.dma(dr["w1bf"][:, :], w1[:, :], reads=("w1",),
                     writes=("w1sp",), eng="gpsimd")
            prog.dma(dr["w2bf"][:, :], w2[:, :], reads=("w2",),
                     writes=("w2sp",), eng="gpsimd")
        prog.barrier()

    xb = sb(tag + "xb", (128, 4 * D))
    lnt = sb(tag + "lnt", (128, 8 * 512), BF16)
    silu = sb(tag + "silu", (128, 32 * 512), BF16)
    st6 = sb(tag + "st6", (128, 64))
    agg = sb(tag + "agg", (128, 32))
    comb = sb(tag + "comb", (128, 24))
    lnw = sb(tag + "lnw", (128, D))
    xnew = sb(tag + "xnew", (128, D))
    psA = ps(tag + "psA", (128, 512))
    psB = ps(tag + "psB", (128, 512))
    psC = ps(tag + "psC", (128, 512))
    psD = ps(tag + "psD", (128, 2 * 1024))

    nchunks = FFN_CHUNKS or (T // 512)
    for c in range(nchunks):
        cp = c % 2
        for tt in range(4):
            i = c * 4 + tt
            xk = f"x{tt}"
            prog.dma(xb[:, tt * D:(tt + 1) * D],
                     dr["xio"][i * 128:(i + 1) * 128, :],
                     writes=(xk,))
        xts = [xb[:, tt * D:(tt + 1) * D] for tt in range(4)]
        emit_ln_stats(prog, st6, agg, comb, xts, [f"x{t_}" for t_ in range(4)],
                      tag, fused=True)
        for tt in range(4):
            xk = f"x{tt}"
            prog.add("vector", lambda e, tt=tt: e.tensor_scalar(
                out=lnw[:, :], in0=xts[tt],
                scalar1=comb[:, tt:tt + 1],
                scalar2=comb[:, 20 + tt:21 + tt],
                op0=OP.subtract, op1=OP.mult),
                reads=(xk, tag + "comb"), writes=("lnw",))
            prog.add("vector", lambda e, tt=tt: e.tensor_scalar(
                out=xts[tt], in0=xts[tt],
                scalar1=comb[:, 12 + tt:13 + tt],
                scalar2=comb[:, 4 + tt:5 + tt],
                op0=OP.mult, op1=OP.subtract),
                reads=(xk, tag + "comb"), writes=(xk,))
            for half in range(2):
                pst = psA if half == 0 else psB
                pk = "psA" if half == 0 else "psB"
                for kk in range(4):
                    k = half * 4 + kk
                    prog.add("tensor", lambda e, k=k, kk=kk, pst=pst: e.transpose(
                        out=pst[:, kk * 128:(kk + 1) * 128],
                        in_=lnw[:, k * 128:(k + 1) * 128],
                        identity=g["ident"][:, :]),
                        reads=("lnw", "ident"), writes=(pk,))
                for kk in range(4):
                    k = half * 4 + kk
                    prog.add("vector", lambda e, k=k, kk=kk, tt=tt, pst=pst: e.tensor_scalar(
                        out=lnt[:, k * 512 + tt * 128: k * 512 + (tt + 1) * 128],
                        in0=pst[:, kk * 128:(kk + 1) * 128],
                        scalar1=gc[:, k:k + 1], scalar2=bc[:, k:k + 1],
                        op0=OP.mult, op1=OP.add),
                        reads=(pk, "gc", "bc"), writes=("lnt",))
        if c == 0:
            dbg_dump(nc, prog, dr, "lnt", lnt[:, :], (128, 8 * 512), BF16, ("lnt",))
            dbg_dump(nc, prog, dr, "w1", w1[:, :], (128, 8 * 4096), BF16, ("w1",))
            dbg_dump(nc, prog, dr, "stats", comb[:, :], (128, 24), F32, (tag + "comb",))
            dbg_dump(nc, prog, dr, "agg", agg[:, :], (128, 32), F32, (tag + "agg",))
            dbg_dump(nc, prog, dr, "st6", st6[:, :], (128, 64), F32, (tag + "st6",))
            dbg_dump(nc, prog, dr, "x0", xb[:, 0:D], (128, D), F32, ("x0",))
            dbg_dump(nc, prog, dr, "lnw", lnw[:, :], (128, D), F32, ("lnw",))
        for n in range(32):
            pst, pk = (psB, "psB") if n % 2 == 0 else (psC, "psC")
            for k in range(8):
                prog.add("tensor", lambda e, n=n, k=k, pst=pst: e.matmul(
                    out=pst[:, :],
                    lhsT=w1[:, k * 4096 + n * 128: k * 4096 + (n + 1) * 128],
                    rhs=lnt[:, k * 512:(k + 1) * 512],
                    start=(k == 0), stop=(k == 7)),
                    reads=("w1", "lnt"), writes=(pk,))
            prog.add("scalar", lambda e, n=n, pst=pst: e.activation(
                out=silu[:, n * 512:(n + 1) * 512], in_=pst[:, :],
                func=AF.Silu, bias=b1c[:, n:n + 1]),
                reads=(pk, "b1c"), writes=("silu",))
        if c == 0:
            dbg_dump(nc, prog, dr, "silu", silu[:, :], (128, 32 * 512), BF16, ("silu",))
            dbg_dump(nc, prog, dr, "w2", w2[:, :], (128, 32 * D), BF16, ("w2",))
        for tt in range(4):
            dp = tt % 2
            dk = f"psD{dp}"
            for F in range(32):
                for j in range(2):
                    prog.add("tensor", lambda e, F=F, j=j, tt=tt, dp=dp: e.matmul(
                        out=psD[:, dp * 1024 + j * 512: dp * 1024 + (j + 1) * 512],
                        lhsT=silu[:, F * 512 + tt * 128: F * 512 + (tt + 1) * 128],
                        rhs=w2[:, F * D + j * 512: F * D + (j + 1) * 512],
                        start=(F == 0), stop=False),
                        reads=("silu", "w2"), writes=(dk,))
            for j in range(2):
                prog.add("tensor", lambda e, j=j, dp=dp: e.matmul(
                    out=psD[:, dp * 1024 + j * 512: dp * 1024 + (j + 1) * 512],
                    lhsT=g["ones_row"][:, :],
                    rhs=b2r[:, j * 512:(j + 1) * 512],
                    start=False, stop=True),
                    reads=("b2r", "ones_row"), writes=(dk,))
            xk = f"x{tt}"
            xt = xb[:, tt * D:(tt + 1) * D]
            prog.add("vector", lambda e, xt=xt, dp=dp: e.scalar_tensor_tensor(
                out=xnew[:, :],
                in0=psD[:, dp * 1024: dp * 1024 + 1024],
                scalar=0.5, in1=xt, op0=OP.mult, op1=OP.add),
                reads=(dk, xk), writes=("xnew",))
            i = c * 4 + tt
            prog.dma(dr["xio"][i * 128:(i + 1) * 128, :],
                     xnew[:, :],
                     reads=("xnew",), writes=(), eng="gpsimd")


A_PARTS = 3


def stage_a(nc, prog, dr, g, ctx):
    sb = lambda n, s_, d=F32: ctx.enter_context(nc.sbuf_tensor(n, list(s_), d))
    ps = lambda n, s_, d=F32: ctx.enter_context(nc.psum_tensor(n, list(s_), d))
    pA = ps("a_pA", (128, 2048))
    pB = ps("a_pB", (128, 1024))
    pC = ps("a_pC", (128, 2048), BF16)

    wz = sb("a_wz", (128, 8 * 2080), BF16)
    wxbc = sb("a_wxbc", (128, 8 * 2304), BF16)
    wout = sb("a_wout", (128, 16 * 1024), BF16)
    convd = sb("a_convd", (128, 72 * 128), BF16)
    convb = sb("a_convb", (128, 18))
    selc0 = sb("a_selc0", (128, 8), BF16)
    selbc = sb("a_selbc", (128, 1024), BF16)
    stepb = sb("a_stepb", (128, 128), BF16)
    nbe = sb("a_nbe", (128, 128), BF16)
    dtbc = sb("a_dtbc", (32, 1))
    Acol = sb("a_Acol", (32, 1))
    Dcol = sb("a_Dcol", (32, 1))
    lnst = sb("a_lnst", (128, 64))
    prev = sb("a_prev", (128, 2048))
    xa = sb("a_xa", (128, 2 * D))
    ln0t = sb("a_ln0t", (128, 8 * 256), BF16)
    xbcfm = sb("a_xbcfm", (128, 18 * 260), BF16)
    siluz = sb("a_siluz", (128, 2 * 2048), BF16)
    dtc = sb("a_dtc", (128, 64))
    xssm = sb("a_xssm", (128, 16 * 256), BF16)
    Bfm = sb("a_Bfm", (128, 256), BF16)
    Cfm = sb("a_Cfm", (128, 256), BF16)
    Btm = sb("a_Btm", (128, 256), BF16)
    fm32 = sb("a_fm32", (32, 1024))
    hml = sb("a_hml", (128, 256), BF16)
    grp = sb("a_grp", (128, 256), BF16)
    one256 = sb("a_one256", (32, 256), BF16)
    cdrow = sb("a_cdrow", (1, 32), BF16)
    cstm = sb("a_cstm", (128, 64))
    ecs = sb("a_ecs", (128, 64))
    dtptm = sb("a_dtptm", (128, 64), BF16)
    dectm = sb("a_dectm", (128, 64), BF16)
    Ddtm = sb("a_Ddtm", (128, 64), BF16)
    cdb = sb("a_cdb", (128, 32))
    CBT0 = sb("a_CBT0", (128, 256), BF16)
    CBT1 = sb("a_CBT1", (128, 128), BF16)
    mtg = sb("a_mtg", (128, 2048), BF16)
    mt1g = sb("a_mt1g", (128, 1024), BF16)
    xd = sb("a_xd", (128, 4096), BF16)
    xdds = sb("a_xdds", (128, 512), BF16)
    tg = sb("a_tg", (128, 2048), BF16)
    yscr = sb("a_yscr", (128, 512))
    x4s = sb("a_x4s", (128, 512), BF16)
    rms = sb("a_rms", (128, 16))
    ynt = sb("a_ynt", (128, 2048), BF16)
    xnew = sb("a_xnew", (128, D))

    for name, buf in (("selc0_h", selc0), ("selbc_h", selbc),
                      ("stepb", stepb), ("negbigeye", nbe)):
        prog.dma(buf[:, :], dr[name][:, :], writes=(name,))
    prog.dma(convb[:, :], dr["convbcols"][:, :], writes=("convb",))
    prog.dma(dtbc[:, :], dr["dtbias_col"][:, :], writes=("dtbc",))
    prog.dma(Dcol[:, :], dr["D_col"][:, :], writes=("Dcol",))
    prog.dma(Acol[:, :], dr["Alog_col"][:, :], writes=("Acol",))
    prog.add("scalar", lambda e: e.activation(out=Acol[:, :], in_=Acol[:, :],
                                              func=AF.Exp),
             reads=("Acol",), writes=("Acol",))
    prog.add("vector", lambda e: e.tensor_scalar(out=Acol[:, :], in0=Acol[:, :],
                                                 scalar1=-1.0, scalar2=None,
                                                 op0=OP.mult),
             reads=("Acol",), writes=("Acol",))
    prog.add("vector", lambda e: e.memset(xbcfm[:, :], 0.0), writes=("xbcfm",))
    prog.add("vector", lambda e: e.memset(grp[:, :], 0.0), writes=("grp",))
    prog.add("vector", lambda e: e.memset(hml[:, :], 0.0), writes=("hml",))
    prog.add("vector", lambda e: e.memset(one256[:, :], 1.0), writes=("one256",))

    xd_f = xd.bitcast(F32)
    tg_f = tg.bitcast(F32)
    mtg_f = mtg.bitcast(F32)
    stgs = [xd_f[:, 0:1024], xd_f[:, 1024:2048]]
    stgk = ["xd", "xd2"]

    def stream_cast(dst, dst_key, dram, rows, cols, fold_col=None, nwc=None):
        for (r0, c0, n, dcol) in _pieces(rows, cols):
            si = stream_cast.si % 2
            stream_cast.si += 1
            prog.dma(stgs[si][:, 0:n], dram[r0:r0 + 128, c0:c0 + n],
                     writes=(stgk[si],))
            if fold_col is None:
                prog.add("vector", lambda e, si=si, n=n, dcol=dcol: e.tensor_copy(
                    out=dst[:, dcol:dcol + n], in_=stgs[si][:, 0:n]),
                    reads=(stgk[si],), writes=(dst_key,))
            else:
                prog.add("vector", lambda e, si=si, n=n, dcol=dcol: e.tensor_scalar(
                    out=dst[:, dcol:dcol + n], in0=stgs[si][:, 0:n],
                    scalar1=nwc[:, fold_col:fold_col + 1], scalar2=None,
                    op0=OP.mult),
                    reads=(stgk[si], "nwc"), writes=(dst_key,))
    stream_cast.si = 0

    def _pieces(r0, spec):
        c0, tot, d0 = spec
        out = []
        off = 0
        while off < tot:
            n = min(1024, tot - off)
            out.append((r0, c0 + off, n, d0 + off))
            off += n
        return out

    with ExitStack() as wctx:
        nwc = wctx.enter_context(nc.sbuf_tensor("a_nwc", [128, 16], F32))
        pvf = prev
        st6 = pvf[:, 0:64]; agg = pvf[:, 64:96]; comb = pvf[:, 96:120]
        prog.dma(nwc[:, :], dr["normw_cols"][:, :], writes=("nwc",))
        for k in range(8):
            stream_cast(wz, "wz", dr["WzdtT"], k * 128, (0, 2080, k * 2080))
        for k in range(8):
            stream_cast(wxbc, "wxbc", dr["WxbcT"], k * 128, (0, 2304, k * 2304))
        for k in range(16):
            stream_cast(wout, "wout", dr["WoutT"], k * 128, (0, 1024, k * 1024),
                        fold_col=k, nwc=nwc)
        for q in range(9):
            si = stream_cast.si % 2
            stream_cast.si += 1
            prog.dma(stgs[si][:, 0:1024].rearrange("p (n q) -> p n q", n=8),
                     dr["convdiag"][8 * q:8 * (q + 1), :, :]
                     .rearrange("n p q -> p n q"),
                     writes=(stgk[si],))
            prog.add("vector", lambda e, si=si, q=q: e.tensor_copy(
                out=convd[:, q * 1024:(q + 1) * 1024], in_=stgs[si][:, 0:1024]),
                reads=(stgk[si],), writes=("convd",))
        pstg = [xd_f[:, 0:1024], xd_f[:, 1024:2048], tg_f[:, 0:1024],
                mtg_f[:, 0:1024]]
        pkey = ["xd", "xd2", "tg", "mtg"]
        for c8 in range(8):
            for tt in range(4):
                i = c8 * 4 + tt
                prog.dma(pstg[tt][:, :], dr["x_in"][i * 128:(i + 1) * 128, :],
                         writes=(pkey[tt],))
            emit_ln_stats(prog, st6, agg, comb, pstg, pkey, "a0", fused=False)
            prog.add("vector", lambda e, c8=c8: e.tensor_copy(
                out=lnst[:, c8 * 4:(c8 + 1) * 4], in_=comb[:, 0:4]),
                reads=("a0comb",), writes=("lnst",))
            prog.add("vector", lambda e, c8=c8: e.tensor_copy(
                out=lnst[:, 32 + c8 * 4:32 + (c8 + 1) * 4], in_=comb[:, 20:24]),
                reads=("a0comb",), writes=("lnst",))
    prog.add("vector", lambda e: e.memset(prev[:, :], 0.0),
             reads=("a0st6", "a0agg", "a0comb", "lnst"), writes=("prev",))
    prog.barrier()

    for c in range(NCH):
        stage_a_chunk(nc, prog, dr, g, c, locals())


def stage_a_chunk(nc, prog, dr, g, c, B):
    pA, pB, pC = B["pA"], B["pB"], B["pC"]
    wz, wxbc, wout = B["wz"], B["wxbc"], B["wout"]
    convd, convb = B["convd"], B["convb"]
    xa, ln0t, lnst = B["xa"], B["ln0t"], B["lnst"]
    lnw = B["ynt"].bitcast(F32)
    xbcfm, siluz, dtc = B["xbcfm"], B["siluz"], B["dtc"]
    xssm, Bfm, Cfm, Btm = B["xssm"], B["Bfm"], B["Cfm"], B["Btm"]
    fm32, hml, grp = B["fm32"], B["hml"], B["grp"]
    one256, cdrow, selbc = B["one256"], B["cdrow"], B["selbc"]
    cstm, ecs, dtptm, dectm, Ddtm, cdb = (B["cstm"], B["ecs"], B["dtptm"],
                                          B["dectm"], B["Ddtm"], B["cdb"])
    CBT0, CBT1, mtg, mt1g = B["CBT0"], B["CBT1"], B["mtg"], B["mt1g"]
    xd, xdds, tg = B["xd"], B["xdds"], B["tg"]
    yscr, x4s, rms, ynt, xnew = (
        B["yscr"], B["x4s"], B["rms"], B["ynt"], B["xnew"])
    selc0, stepb, nbe = (B["selc0"], B["stepb"], B["nbe"])
    dtbc, Acol, Dcol = B["dtbc"], B["Acol"], B["Dcol"]
    prev = B["prev"]

    for th in range(2):
        i = c * 2 + th
        prog.dma(xa[:, th * D:(th + 1) * D], dr["x_in"][i * 128:(i + 1) * 128, :],
                 writes=(f"ax{th}",))
    for th in range(2):
        i = c * 2 + th
        prog.add("vector", lambda e, th=th, i=i: e.tensor_scalar(
            out=lnw[:, :], in0=xa[:, th * D:(th + 1) * D],
            scalar1=lnst[:, i:i + 1], scalar2=lnst[:, 32 + i:33 + i],
            op0=OP.subtract, op1=OP.mult),
            reads=(f"ax{th}", "lnst"), writes=("ynt",))
        for half in range(2):
            for kk in range(4):
                k = half * 4 + kk
                prog.add("tensor", lambda e, k=k, kk=kk: e.transpose(
                    out=pB[:, kk * 128:(kk + 1) * 128],
                    in_=lnw[:, k * 128:(k + 1) * 128],
                    identity=g["ident"][:, :]),
                    reads=("ynt", "ident"), writes=("a_pB0", "a_pB1"))
            prog.add("vector", lambda e, half=half, th=th: e.tensor_copy(
                out=ln0t[:, :].rearrange("p (k t) -> p k t", k=8)
                    [:, half * 4:(half + 1) * 4, th * 128:(th + 1) * 128],
                in_=pB[:, 0:512].rearrange("p (k t) -> p k t", k=4)),
                reads=("a_pB0", "a_pB1"), writes=("ln0t",))
    for th in range(2):
        for ncol in range(4):
            for k in range(8):
                prog.add("tensor", lambda e, th=th, ncol=ncol, k=k: e.matmul(
                    out=pA[:, ncol * 512:(ncol + 1) * 512],
                    lhsT=ln0t[:, k * 256 + th * 128: k * 256 + (th + 1) * 128],
                    rhs=wz[:, k * 2080 + ncol * 512: k * 2080 + (ncol + 1) * 512],
                    start=(k == 0), stop=(k == 7)),
                    reads=("ln0t", "wz"), writes=(f"a_pA{ncol}",))
        for k in range(8):
            prog.add("tensor", lambda e, th=th, k=k: e.matmul(
                out=pB[:, th * 512:th * 512 + 32],
                lhsT=ln0t[:, k * 256 + th * 128: k * 256 + (th + 1) * 128],
                rhs=wz[:, k * 2080 + 2048: k * 2080 + 2080],
                start=(k == 0), stop=(k == 7)),
                reads=("ln0t", "wz"), writes=(f"a_pB{th}",))
        prog.add("scalar", lambda e, th=th: e.activation(
            out=siluz[:, th * 2048:(th + 1) * 2048], in_=pA[:, :], func=AF.Silu),
            reads=("a_pA0", "a_pA1", "a_pA2", "a_pA3"), writes=("siluz",))
        prog.add("vector", lambda e, th=th: e.tensor_copy(
            out=dtc[:, th * 32:(th + 1) * 32], in_=pB[:, th * 512:th * 512 + 32]),
            reads=(f"a_pB{th}",), writes=("dtc",))
    for n in range(18):
        pk = "a_pB0" if n % 2 == 0 else "a_pB1"
        off = 0 if n % 2 == 0 else 512
        for k in range(8):
            prog.add("tensor", lambda e, n=n, k=k, off=off: e.matmul(
                out=pB[:, off:off + 256],
                lhsT=wxbc[:, k * 2304 + n * 128: k * 2304 + (n + 1) * 128],
                rhs=ln0t[:, k * 256:(k + 1) * 256],
                start=(k == 0), stop=(k == 7)),
                reads=("wxbc", "ln0t"), writes=(pk,))
        prog.add("vector", lambda e, n=n, off=off: e.tensor_copy(
            out=xbcfm[:, n * 260 + 3: n * 260 + 259], in_=pB[:, off:off + 256]),
            reads=(pk,), writes=("xbcfm", "halo")) 
        cpk = "a_pB1" if n % 2 == 0 else "a_pB0"
        coff = 512 if n % 2 == 0 else 0
        for tap in range(4):
            prog.add("tensor", lambda e, n=n, tap=tap, coff=coff: e.matmul(
                out=pB[:, coff:coff + 256],
                lhsT=convd[:, (4 * n + tap) * 128:(4 * n + tap + 1) * 128],
                rhs=xbcfm[:, n * 260 + tap: n * 260 + tap + 256],
                start=(tap == 0), stop=(tap == 3)),
                reads=("convd", "xbcfm"), writes=(cpk,))
        if n < 16:
            dst = lambda e, n=n, coff=coff: e.activation(
                out=xssm[:, n * 256:(n + 1) * 256], in_=pB[:, coff:coff + 256],
                func=AF.Silu, bias=convb[:, n:n + 1])
            wkey = "xssm"
        elif n == 16:
            dst = lambda e, coff=coff: e.activation(
                out=Bfm[:, :], in_=pB[:, coff:coff + 256],
                func=AF.Silu, bias=convb[:, 16:17])
            wkey = "Bfm"
        else:
            dst = lambda e, coff=coff: e.activation(
                out=Cfm[:, :], in_=pB[:, coff:coff + 256],
                func=AF.Silu, bias=convb[:, 17:18])
            wkey = "Cfm"
        prog.add("scalar", dst, reads=(cpk, "convb"), writes=(wkey,))
    for hseg in range(3):
        v = xbcfm[:, hseg * 6 * 260:(hseg + 1) * 6 * 260].rearrange(
            "p (n q) -> p n q", n=6)
        prog.add("vector", lambda e, v=v: e.tensor_copy(
            out=v[:, :, 0:3], in_=v[:, :, 256:259]),
            reads=("xbcfm",), writes=("xbcfm",))
    for half in range(2):
        prog.add("tensor", lambda e, half=half: e.transpose(
            out=pC[:, half * 128:(half + 1) * 128],
            in_=Bfm[:, half * 128:(half + 1) * 128], identity=g["identb"][:, :]),
            reads=("Bfm", "identb"), writes=("a_pC0",))
    prog.add("vector", lambda e: e.tensor_copy(out=Btm[:, :], in_=pC[:, 0:256]),
             reads=("a_pC0",), writes=("Btm",))
    if c == 0:
        dbg_dump(nc, prog, dr, "a_siluz", B["siluz"][:, :], (128, 4096), BF16, ("siluz",))
        dbg_dump(nc, prog, dr, "a_xssm", B["xssm"][:, :], (128, 4096), BF16, ("xssm",))
        dbg_dump(nc, prog, dr, "a_Bfm", B["Bfm"][:, :], (128, 256), BF16, ("Bfm",))
        dbg_dump(nc, prog, dr, "a_Cfm", B["Cfm"][:, :], (128, 256), BF16, ("Cfm",))
        dbg_dump(nc, prog, dr, "a_dtc", B["dtc"][:, :], (128, 64), F32, ("dtc",))
        dbg_dump(nc, prog, dr, "a_ln0t", B["ln0t"][:, :], (128, 2048), BF16, ("ln0t",))
        dbg_dump(nc, prog, dr, "a_xbcfm", B["xbcfm"][:, :], (128, 18 * 260), BF16, ("xbcfm",))
        dbg_dump(nc, prog, dr, "a_convd", B["convd"][:, :], (128, 72 * 128), BF16, ("convd",))
    if A_PARTS < 2:
        return
    stage_a_chunk_p2(nc, prog, dr, g, c, B)


def stage_a_chunk_p2(nc, prog, dr, g, c, B):
    pA, pB, pC = B["pA"], B["pB"], B["pC"]
    fm32, hml, grp = B["fm32"], B["hml"], B["grp"]
    one256, cdrow, selbc = B["one256"], B["cdrow"], B["selbc"]
    selc0, stepb, nbe = B["selc0"], B["stepb"], B["nbe"]
    dtbc, Acol, Dcol = B["dtbc"], B["Acol"], B["Dcol"]
    dtc, xssm, Bfm, Cfm, Btm = B["dtc"], B["xssm"], B["Bfm"], B["Cfm"], B["Btm"]
    cstm, ecs, dtptm, dectm, Ddtm, cdb = (B["cstm"], B["ecs"], B["dtptm"],
                                          B["dectm"], B["Ddtm"], B["cdb"])
    CBT0, CBT1, mtg, mt1g = B["CBT0"], B["CBT1"], B["mtg"], B["mt1g"]
    xd, xdds, siluz = B["xd"], B["xdds"], B["siluz"]
    yscr, x4s, rms, ynt, xnew = (B["yscr"], B["x4s"], B["rms"], B["ynt"],
                                 B["xnew"])
    prev, xa, wout = B["prev"], B["xa"], B["wout"]
    f0, f1, f2, f3 = (fm32[:, 0:256], fm32[:, 256:512], fm32[:, 512:768],
                      fm32[:, 768:1024])
    FK = "a_fm"
    V, S, Tn = "vector", "scalar", "tensor"

    def add(eng, fn, r, w):
        prog.add(eng, fn, reads=r, writes=w)

    for th in range(2):
        add(Tn, lambda e, th=th: e.transpose(
            out=pB[0:32, th * 128:(th + 1) * 128],
            in_=dtc[:, th * 32:(th + 1) * 32], identity=g["ident"][:, :]),
            ("dtc", "ident"), ("a_pB0",))
    add(V, lambda e: e.tensor_copy(out=f0[:, :], in_=pB[0:32, 0:256]),
        ("a_pB0",), (FK,))
    add(V, lambda e: e.tensor_scalar(out=f1[:, :], in0=f0[:, :],
                                     scalar1=dtbc[:, 0:1], scalar2=None,
                                     op0=OP.add), (FK, "dtbc"), (FK,))
    add(V, lambda e: e.tensor_scalar(out=f3[:, :], in0=f1[:, :], scalar1=-1.0,
                                     scalar2=None, op0=OP.mult), (FK,), (FK,))
    add(V, lambda e: e.tensor_tensor(out=f2[:, :], in0=f1[:, :], in1=f3[:, :],
                                     op=OP.max), (FK,), (FK,))
    add(S, lambda e: e.activation(out=f3[:, :], in_=f2[:, :], func=AF.Exp,
                                  scale=-1.0), (FK,), (FK,))
    add(S, lambda e: e.activation(out=f2[:, :], in_=f3[:, :], func=AF.Ln,
                                  bias=g["ones_col"][0:32, 0:1]),
        (FK, "ones_col"), (FK,))
    add(S, lambda e: e.activation(out=f0[:, :], in_=f2[:, :], func=AF.Exp,
                                  scale=-1.0), (FK,), (FK,))
    add(V, lambda e: e.scalar_tensor_tensor(out=f0[:, :], in0=f3[:, :],
                                            scalar=1.0, in1=f0[:, :],
                                            op0=OP.add, op1=OP.mult),
        (FK,), (FK,))
    add(V, lambda e: e.scalar_tensor_tensor(out=f2[:, :], in0=f0[:, :],
                                            scalar=-1.0, in1=f2[:, :],
                                            op0=OP.add, op1=OP.add),
        (FK,), (FK,))
    add(V, lambda e: e.tensor_scalar(out=f3[:, :], in0=f1[:, :], scalar1=0.0,
                                     scalar2=None, op0=OP.max), (FK,), (FK,))
    add(V, lambda e: e.tensor_tensor(out=f1[:, :], in0=f2[:, :], in1=f3[:, :],
                                     op=OP.add), (FK,), (FK,))
    add(V, lambda e: e.tensor_scalar(out=f0[:, :], in0=f1[:, :],
                                     scalar1=Acol[:, 0:1], scalar2=None,
                                     op0=OP.mult), (FK, "Acol"), (FK,))
    add(V, lambda e: e.tensor_tensor_scan(out=f2[:, :], data0=one256[:, :],
                                          data1=f0[:, :], initial=0.0,
                                          op0=OP.mult, op1=OP.add),
        (FK, "one256"), (FK,))
    add(S, lambda e: e.activation(out=f3[:, :], in_=f2[:, :], func=AF.Exp,
                                  scale=-1.0, bias=f2[:, 255:256]),
        (FK,), (FK,))
    add(S, lambda e: e.activation(out=f0[:, 0:1], in_=f2[:, 255:256],
                                  func=AF.Exp), (FK,), (FK,))
    add(Tn, lambda e: e.transpose(out=pB[0:1, 0:32], in_=f0[:, 0:1],
                                  identity=g["ident"][0:32, 0:32]),
        (FK, "ident"), ("a_pB0",))
    add(V, lambda e: e.tensor_copy(out=cdrow[:, :], in_=pB[0:1, 0:32]),
        ("a_pB0",), ("cdrow",))
    add(Tn, lambda e: e.matmul(out=pB[:, 0:32], lhsT=g["onesb_row"][:, :],
                               rhs=cdrow[:, :], start=True, stop=True),
        ("cdrow", "onesb_row"), ("a_pB0",))
    add(V, lambda e: e.tensor_copy(out=cdb[:, :], in_=pB[:, 0:32]),
        ("a_pB0",), ("cdb",))
    for th in range(2):
        add(Tn, lambda e, th=th: e.transpose(
            out=pB[:, th * 32:(th + 1) * 32],
            in_=f2[:, th * 128:(th + 1) * 128],
            identity=g["ident"][0:32, 0:32]), (FK, "ident"), ("a_pB0",))
        add(Tn, lambda e, th=th: e.transpose(
            out=pB[:, 64 + th * 32: 64 + (th + 1) * 32],
            in_=f1[:, th * 128:(th + 1) * 128],
            identity=g["ident"][0:32, 0:32]), (FK, "ident"), ("a_pB0",))
        add(Tn, lambda e, th=th: e.transpose(
            out=pB[:, 128 + th * 32: 128 + (th + 1) * 32],
            in_=f3[:, th * 128:(th + 1) * 128],
            identity=g["ident"][0:32, 0:32]), (FK, "ident"), ("a_pB0",))
    add(V, lambda e: e.tensor_copy(out=cstm[:, :], in_=pB[:, 0:64]),
        ("a_pB0",), ("cstm",))
    add(V, lambda e: e.tensor_copy(out=dtptm[:, :], in_=pB[:, 64:128]),
        ("a_pB0",), ("dtptm",))
    add(V, lambda e: e.tensor_copy(out=dectm[:, :], in_=pB[:, 128:192]),
        ("a_pB0",), ("dectm",))
    add(S, lambda e: e.activation(out=ecs[:, :], in_=cstm[:, :], func=AF.Exp),
        ("cstm",), ("ecs",))
    add(V, lambda e: e.reciprocal(out=f3[:, :], in_=f1[:, :]), (FK,), (FK,))
    add(V, lambda e: e.tensor_scalar(out=f3[:, :], in0=f3[:, :],
                                     scalar1=Dcol[:, 0:1], scalar2=None,
                                     op0=OP.mult), (FK, "Dcol"), (FK,))
    for th in range(2):
        add(Tn, lambda e, th=th: e.transpose(
            out=pB[:, 192 + th * 32: 192 + (th + 1) * 32],
            in_=f3[:, th * 128:(th + 1) * 128],
            identity=g["ident"][0:32, 0:32]), (FK, "ident"), ("a_pB0",))
    add(V, lambda e: e.tensor_copy(out=Ddtm[:, :], in_=pB[:, 192:256]),
        ("a_pB0",), ("Ddtm",))
    hscr = B["x4s"][0:32, 0:256].bitcast(BF16)[:, 0:256]
    add(V, lambda e: e.tensor_scalar(out=f0[:, :], in0=f2[:, :], scalar1=-1.0,
                                     scalar2=None, op0=OP.mult), (FK,), (FK,))
    add(V, lambda e, hscr=hscr: e.tensor_copy(out=hscr, in_=f0[:, :]),
        (FK,), ("x4s",))
    add(V, lambda e, hscr=hscr: e.tensor_copy(out=hml[0:32, :], in_=hscr),
        ("x4s",), ("hml",))
    add(V, lambda e, hscr=hscr: e.tensor_tensor(out=f1[:, :], in0=f0[:, :],
                                                in1=hscr, op=OP.subtract),
        (FK, "x4s"), (FK,))
    add(V, lambda e, hscr=hscr: e.tensor_copy(out=hscr, in_=f1[:, :]),
        (FK,), ("x4s",))
    add(V, lambda e, hscr=hscr: e.tensor_copy(out=hml[32:64, :], in_=hscr),
        ("x4s",), ("hml",))
    add(V, lambda e, hscr=hscr: e.tensor_tensor(out=f3[:, :], in0=f1[:, :],
                                                in1=hscr, op=OP.subtract),
        (FK, "x4s"), (FK,))
    add(V, lambda e: e.tensor_copy(out=hml[64:96, :], in_=f3[:, :]),
        (FK,), ("hml",))
    add(Tn, lambda e: e.matmul(out=pB[:, 0:256], lhsT=Bfm[:, 0:128],
                               rhs=Cfm[:, :], start=True, stop=True),
        ("Bfm", "Cfm"), ("a_pB0",))
    add(Tn, lambda e: e.matmul(out=pB[:, 512:640], lhsT=Bfm[:, 128:256],
                               rhs=Cfm[:, 128:256], start=True, stop=True),
        ("Bfm", "Cfm"), ("a_pB1",))
    add(V, lambda e: e.tensor_copy(out=CBT0[:, :], in_=pB[:, 0:256]),
        ("a_pB0",), ("CBT0",))
    add(V, lambda e: e.tensor_copy(out=CBT1[:, :], in_=pB[:, 512:640]),
        ("a_pB1",), ("CBT1",))
    for th in range(2):
        for i in range(16):
            add(Tn, lambda e, th=th, i=i: e.transpose(
                out=pC[:, i * 128:(i + 1) * 128],
                in_=xssm[:, i * 256 + th * 128: i * 256 + (th + 1) * 128],
                identity=g["identb"][:, :]),
                ("xssm", "identb"), ("a_pC0", "a_pC1"))
        add(V, lambda e, th=th: e.tensor_tensor(
            out=xd[:, th * 2048:(th + 1) * 2048].rearrange(
                "p (h q) -> p h q", h=32),
            in0=pC[:, :].rearrange("p (h q) -> p h q", h=32),
            in1=dtptm[:, th * 32:(th + 1) * 32].unsqueeze(2)
                .broadcast_to((128, 32, 64)),
            op=OP.mult), ("a_pC0", "a_pC1", "dtptm"), ("xd",))
    for gi in range(4):
        GK = "grp"
        for lvl in range(3):
            prog.dma(grp[8 * lvl:8 * lvl + 8, :],
                     hml[32 * lvl + 8 * gi: 32 * lvl + 8 * gi + 8, :],
                     reads=("hml",), writes=(GK,))
        for hl in range(8):
            add(Tn, lambda e, hl=hl: e.matmul(
                out=pA[:, hl * 256:(hl + 1) * 256],
                lhsT=selbc[:, hl * 128:(hl + 1) * 128],
                rhs=grp[:, 0:256], start=True, stop=False),
                (GK, "selbc"), ("a_pA0", "a_pA1", "a_pA2", "a_pA3"))
            add(Tn, lambda e, hl=hl: e.matmul(
                out=pA[:, hl * 256:(hl + 1) * 256],
                lhsT=grp[:, 0:128],
                rhs=selc0[:, hl:hl + 1].broadcast_to((128, 256)),
                start=False, stop=False),
                (GK, "selc0"), ("a_pA0", "a_pA1", "a_pA2", "a_pA3"))
            add(Tn, lambda e, hl=hl: e.matmul(
                out=pA[:, hl * 256: hl * 256 + 128],
                lhsT=stepb[:, :], rhs=nbe[:, :], start=False, stop=True),
                ("stepb", "negbigeye"), ("a_pA0", "a_pA1", "a_pA2", "a_pA3"))
        add(S, lambda e: e.activation(out=mtg[:, :], in_=pA[:, :], func=AF.Exp),
            ("a_pA0", "a_pA1", "a_pA2", "a_pA3"), ("mtg",))
        for hl in range(8):
            add(Tn, lambda e, hl=hl: e.matmul(
                out=pB[:, hl * 128:(hl + 1) * 128],
                lhsT=selbc[:, hl * 128:(hl + 1) * 128],
                rhs=grp[:, 128:256], start=True, stop=False),
                (GK, "selbc"), ("a_pB0", "a_pB1"))
            add(Tn, lambda e, hl=hl: e.matmul(
                out=pB[:, hl * 128:(hl + 1) * 128],
                lhsT=grp[:, 128:256],
                rhs=selc0[:, hl:hl + 1].broadcast_to((128, 128)),
                start=False, stop=False),
                (GK, "selc0"), ("a_pB0", "a_pB1"))
            add(Tn, lambda e, hl=hl: e.matmul(
                out=pB[:, hl * 128:(hl + 1) * 128],
                lhsT=stepb[:, :], rhs=nbe[:, :], start=False, stop=True),
                ("stepb", "negbigeye"), ("a_pB0", "a_pB1"))
        add(S, lambda e: e.activation(out=mt1g[:, :], in_=pB[:, :], func=AF.Exp),
            ("a_pB0", "a_pB1"), ("mt1g",))
        add(V, lambda e: e.tensor_tensor(
            out=mtg[:, :].rearrange("p (h q) -> p h q", h=8),
            in0=mtg[:, :].rearrange("p (h q) -> p h q", h=8),
            in1=CBT0[:, :].unsqueeze(1).broadcast_to((128, 8, 256)),
            op=OP.mult), ("mtg", "CBT0"), ("mtg",))
        add(V, lambda e: e.tensor_tensor(
            out=mt1g[:, :].rearrange("p (h q) -> p h q", h=8),
            in0=mt1g[:, :].rearrange("p (h q) -> p h q", h=8),
            in1=CBT1[:, :].unsqueeze(1).broadcast_to((128, 8, 128)),
            op=OP.mult), ("mt1g", "CBT1"), ("mt1g",))
        add(V, lambda e, gi=gi: e.tensor_copy(
            out=xdds[:, :], in_=prev[:, gi * 512:(gi + 1) * 512]),
            ("prev",), ("xdds",))
        for lt in range(2):
            add(V, lambda e, lt=lt, gi=gi: e.tensor_tensor(
                out=x4s[:, :].rearrange("p (h q) -> p h q", h=8),
                in0=xd[:, lt * 2048 + gi * 512: lt * 2048 + (gi + 1) * 512]
                    .rearrange("p (h q) -> p h q", h=8),
                in1=Ddtm[:, lt * 32 + gi * 8: lt * 32 + (gi + 1) * 8]
                    .unsqueeze(2).broadcast_to((128, 8, 64)),
                op=OP.mult), ("xd", "Ddtm"), ("x4s",))
            for hl in range(8):
                if lt == 0:
                    add(Tn, lambda e, hl=hl, gi=gi: e.matmul(
                        out=pA[:, hl * 64:(hl + 1) * 64],
                        lhsT=mtg[:, hl * 256: hl * 256 + 128],
                        rhs=xd[:, gi * 512 + hl * 64: gi * 512 + (hl + 1) * 64],
                        start=True, stop=True),
                        ("mtg", "xd"), ("a_pA0",))
                else:
                    add(Tn, lambda e, hl=hl, gi=gi: e.matmul(
                        out=pA[:, hl * 64:(hl + 1) * 64],
                        lhsT=mtg[:, hl * 256 + 128: hl * 256 + 256],
                        rhs=xd[:, gi * 512 + hl * 64: gi * 512 + (hl + 1) * 64],
                        start=True, stop=False),
                        ("mtg", "xd"), ("a_pA0",))
                    add(Tn, lambda e, hl=hl, gi=gi: e.matmul(
                        out=pA[:, hl * 64:(hl + 1) * 64],
                        lhsT=mt1g[:, hl * 128:(hl + 1) * 128],
                        rhs=xd[:, 2048 + gi * 512 + hl * 64:
                                2048 + gi * 512 + (hl + 1) * 64],
                        start=False, stop=True),
                        ("mt1g", "xd"), ("a_pA0",))
                add(Tn, lambda e, hl=hl, lt=lt: e.matmul(
                    out=pA[:, 512 + hl * 64: 512 + (hl + 1) * 64],
                    lhsT=Cfm[:, lt * 128:(lt + 1) * 128],
                    rhs=xdds[:, hl * 64:(hl + 1) * 64],
                    start=True, stop=True),
                    ("Cfm", "xdds"), ("a_pA1",))
            add(V, lambda e, lt=lt, gi=gi: e.tensor_tensor(
                out=yscr[:, :].rearrange("p (h q) -> p h q", h=8),
                in0=pA[:, 512:1024].rearrange("p (h q) -> p h q", h=8),
                in1=ecs[:, lt * 32 + gi * 8: lt * 32 + (gi + 1) * 8]
                    .unsqueeze(2).broadcast_to((128, 8, 64)),
                op=OP.mult), ("a_pA1", "ecs"), ("yscr",))
            add(V, lambda e: e.tensor_tensor(out=yscr[:, :], in0=yscr[:, :],
                                             in1=x4s[:, :], op=OP.add),
                ("yscr", "x4s"), ("yscr",))
            add(V, lambda e: e.scalar_tensor_tensor(
                out=yscr[:, :], in0=yscr[:, :], scalar=0.0, in1=pA[:, 0:512],
                op0=OP.add, op1=OP.add), ("a_pA0", "yscr"), ("yscr",))
            add(V, lambda e, lt=lt, gi=gi: e.tensor_tensor(
                out=xssm[:, lt * 2048 + gi * 512: lt * 2048 + (gi + 1) * 512],
                in0=yscr[:, :],
                in1=siluz[:, lt * 2048 + gi * 512: lt * 2048 + (gi + 1) * 512],
                op=OP.mult), ("yscr", "siluz"), ("xssm",))
            add(V, lambda e, lt=lt, gi=gi: e.scalar_tensor_tensor(
                out=x4s[:, :],
                in0=xssm[:, lt * 2048 + gi * 512: lt * 2048 + (gi + 1) * 512],
                scalar=1.0,
                in1=xssm[:, lt * 2048 + gi * 512: lt * 2048 + (gi + 1) * 512],
                op0=OP.mult, op1=OP.mult,
                accum_out=rms[:, lt * 4 + gi: lt * 4 + gi + 1]),
                ("xssm",), ("x4s", "rms"))
        for lh in range(2):
            add(V, lambda e, lh=lh, gi=gi: e.tensor_tensor(
                out=xdds[:, :].rearrange("p (h q) -> p h q", h=8),
                in0=xd[:, lh * 2048 + gi * 512: lh * 2048 + (gi + 1) * 512]
                    .rearrange("p (h q) -> p h q", h=8),
                in1=dectm[:, lh * 32 + gi * 8: lh * 32 + (gi + 1) * 8]
                    .unsqueeze(2).broadcast_to((128, 8, 64)),
                op=OP.mult), ("xd", "dectm"), ("xdds",)) 
            if c == 0 and gi == 0:
                dbg_dump(nc, prog, dr, f"a_xdds{lh}", xdds[:, :], (128, 512),
                         BF16, ("xdds",))
            for hl in range(8):
                add(Tn, lambda e, hl=hl, lh=lh: e.matmul(
                    out=pA[:, 1536 + hl * 64: 1536 + (hl + 1) * 64],
                    lhsT=Btm[:, lh * 128:(lh + 1) * 128],
                    rhs=xdds[:, hl * 64:(hl + 1) * 64],
                    start=(lh == 0 and hl == 0), stop=(lh == 1 and hl == 7)),
                    ("Btm", "xdds"), ("a_pA3",))
            if lh == 0:
                pass
        add(V, lambda e, gi=gi: e.tensor_tensor(
            out=yscr[:, :].rearrange("p (h q) -> p h q", h=8),
            in0=prev[:, gi * 512:(gi + 1) * 512]
                .rearrange("p (h q) -> p h q", h=8),
            in1=cdb[:, gi * 8:(gi + 1) * 8].unsqueeze(2)
                .broadcast_to((128, 8, 64)),
            op=OP.mult), ("prev", "cdb"), ("yscr",))
        add(V, lambda e, gi=gi: e.scalar_tensor_tensor(
            out=prev[:, gi * 512:(gi + 1) * 512],
            in0=pA[:, 1536:2048], scalar=0.0, in1=yscr[:, :],
            op0=OP.add, op1=OP.add), ("a_pA3", "yscr"), ("prev",))
    for lt in range(2):
        add(V, lambda e, lt=lt: e.tensor_reduce(
            out=rms[:, 8 + lt: 9 + lt], in_=rms[:, lt * 4:(lt + 1) * 4],
            axis=AX.X, op=OP.add), ("rms",), ("rms",))
        add(V, lambda e, lt=lt: e.tensor_scalar(
            out=rms[:, 10 + lt: 11 + lt], in0=rms[:, 8 + lt: 9 + lt],
            scalar1=1.0 / 2048.0, scalar2=EPS, op0=OP.mult, op1=OP.add),
            ("rms",), ("rms",))
        add(S, lambda e, lt=lt: e.activation(
            out=rms[:, 12 + lt: 13 + lt], in_=rms[:, 10 + lt: 11 + lt],
            func=AF.Ln), ("rms",), ("rms",))
        add(S, lambda e, lt=lt: e.activation(
            out=rms[:, 14 + lt: 15 + lt], in_=rms[:, 12 + lt: 13 + lt],
            func=AF.Exp, scale=-0.5), ("rms",), ("rms",))
        add(V, lambda e, lt=lt: e.tensor_scalar(
            out=xssm[:, lt * 2048:(lt + 1) * 2048],
            in0=xssm[:, lt * 2048:(lt + 1) * 2048],
            scalar1=rms[:, 14 + lt: 15 + lt], scalar2=None, op0=OP.mult),
            ("xssm", "rms"), ("xssm",))
        for i in range(16):
            add(Tn, lambda e, i=i, lt=lt: e.transpose(
                out=pC[:, i * 128:(i + 1) * 128],
                in_=xssm[:, lt * 2048 + i * 128: lt * 2048 + (i + 1) * 128],
                identity=g["identb"][:, :]),
                ("xssm", "identb"), ("a_pC0", "a_pC1"))
        add(V, lambda e: e.tensor_copy(out=ynt[:, 0:2048],
                                       in_=pC[:, :]),
            ("a_pC0", "a_pC1"), ("ynt",))
        for k in range(16):
            for j in range(2):
                add(Tn, lambda e, k=k, j=j: e.matmul(
                    out=pB[:, j * 512:(j + 1) * 512],
                    lhsT=ynt[:, k * 128:(k + 1) * 128],
                    rhs=wout[:, k * 1024 + j * 512: k * 1024 + (j + 1) * 512],
                    start=(k == 0), stop=(k == 15)),
                    ("ynt", "wout"), (f"a_pB{j}",))
        add(V, lambda e, lt=lt: e.tensor_tensor(
            out=xnew[:, :], in0=pB[:, 0:1024],
            in1=xa[:, lt * D:(lt + 1) * D], op=OP.add),
            ("a_pB0", "a_pB1", f"ax{lt}"), ("xnew",))
        i = c * 2 + lt
        prog.dma(dr["xio"][i * 128:(i + 1) * 128, :], xnew[:, :],
                 reads=("xnew",), writes=(), eng="gpsimd")
    if c == 1:
        dbg_dump(nc, prog, dr, "a_prev1", B["prev"][:, :], (128, 2048), F32, ("prev",))
        dbg_dump(nc, prog, dr, "a_cdb1", B["cdb"][:, :], (128, 32), F32, ("cdb",))
        dbg_dump(nc, prog, dr, "a_yn1", B["xssm"][:, :], (128, 4096), BF16, ("xssm",))
    if c == 0:
        dbg_dump(nc, prog, dr, "a_yn", B["xssm"][:, :], (128, 4096), BF16, ("xssm",))
        dbg_dump(nc, prog, dr, "a_mtg", B["mtg"][:, :], (128, 2048), BF16, ("mtg",))
        dbg_dump(nc, prog, dr, "a_prevb", B["prev"][:, :], (128, 2048), F32, ("prev",))
        dbg_dump(nc, prog, dr, "a_cstm", B["cstm"][:, :], (128, 64), F32, ("cstm",))
        dbg_dump(nc, prog, dr, "a_dectm", B["dectm"][:, :], (128, 64), BF16, ("dectm",))
        dbg_dump(nc, prog, dr, "a_dtptm", B["dtptm"][:, :], (128, 64), BF16, ("dtptm",))
        dbg_dump(nc, prog, dr, "a_xd", B["xd"][:, :], (128, 4096), BF16, ("xd",))
        dbg_dump(nc, prog, dr, "a_Btm", B["Btm"][:, :], (128, 256), BF16, ("Btm",))



def stage_attn(nc, prog, dr, g, ctx):
    sb = lambda n, s_, d=F32: ctx.enter_context(nc.sbuf_tensor(n, list(s_), d))
    ps = lambda n, s_, d=F32: ctx.enter_context(nc.psum_tensor(n, list(s_), d))
    V, S, Tn = "vector", "scalar", "tensor"

    def add(eng, fn, r, w):
        prog.add(eng, fn, reads=r, writes=w)

    pA = ps("c_pA", (128, 2048))
    pB = ps("c_pB", (128, 1024))
    pC = ps("c_pC", (128, 2048), BF16)

    merg = sb("c_merg", (128, 12 * 8 * 128), BF16)
    ln2t = sb("c_ln2t", (128, 8 * 2048), BF16)
    krope = sb("c_krope", (128, 4 * 4096), BF16)
    vT = sb("c_vT", (128, 32 * 528), BF16)
    cpw = sb("c_cpw", (128, 8 * 1024), BF16)
    qlat = sb("c_qlat", (128, 2048), BF16)
    shuf = sb("c_shuf", (128, 2048), BF16)
    qfk = sb("c_qfk", (128, 4 * 2048), BF16)
    pts = qlat[:, :]
    Yq = sb("c_Yq", (128, 8 * 512), BF16)
    ylat = sb("c_ylat", (128, 512), BF16)

    invd = sb("c_invd", (1, 512))
    cosb = sb("c_cosb", (128, 2048), BF16)
    sinb = sb("c_sinb", (128, 2048), BF16)
    xa = sb("c_xa", (128, D))

    qsq = pts[:, :].bitcast(F32)
    latb = sb("c_latb", (128, 12))
    lnw = sb("c_lnw", (128, D))
    invdbc = qlat[:, 0:512]
    xnew = lnw
    st6 = sb("c_st6", (128, 64)); agg = sb("c_agg", (128, 32))
    comb = sb("c_comb", (128, 24))
    qm = sb("c_qm", (128, 4 * 512), BF16)
    nbe = sb("c_nbe", (128, 128), BF16)
    bd2q = sb("c_bd2q", (128, 64), BF16); bd2k = sb("c_bd2k", (128, 64), BF16)
    bd4 = sb("c_bd4", (128, 128), BF16)
    l2oA = sb("c_l2oA", (33, 64), BF16); l2oB = sb("c_l2oB", (97, 64), BF16)
    qlb = sb("c_qlb", (128, 1)); kvlb = sb("c_kvlb", (128, 1))
    wqkb = sb("c_wqkb", (128, 1))
    cpbrow = sb("c_cpbrow", (1, D), BF16)
    cabb = sb("c_cabb", (128, 24), BF16)

    add("sync", lambda e: e.dma_start(
        out=qm[:, :].rearrange("p (j q) -> p j q", j=4),
        in_=dr["qmask"].rearrange("j p q -> p j q")), (), ("qm",))
    prog.dma(nbe[:, :], dr["negbigeye"][:, :], writes=("nbe",))
    prog.dma(qlb[:, :], dr["qlb_t"][:, :], writes=("qlb",))
    prog.dma(kvlb[:, :], dr["kvlb_t"][:, :], writes=("kvlb",))
    prog.dma(wqkb[:, :], dr["wqkb_t"][:, :], writes=("wqkb",))
    prog.dma(lnw[:, 0:24], dr["cab_cols"][:, :], writes=("lnw",))
    add(V, lambda e: e.tensor_copy(out=cabb[:, :], in_=lnw[:, 0:24]),
        ("lnw",), ("cabb",))
    for name, dst, dk, nn in (("q2l_bd2", bd2q, "bd2q", 64),
                              ("kv2l_bd2", bd2k, "bd2k", 64),
                              ("wqk_bd4", bd4, "bd4", 128)):
        prog.dma(lnw[:, 0:nn], dr[name][:, :], writes=("lnw",))
        add(V, lambda e, dst=dst, nn=nn: e.tensor_copy(out=dst[:, :],
                                                       in_=lnw[:, 0:nn]),
            ("lnw",), (dk,))
    for name, dsl, dk in (("l2o_extA", l2oA[:, :], "l2oA"),
                          ("l2o_extB", l2oB[64:97, :], "l2oB")):
        prog.dma(lnw[0:33, 0:64], dr[name][:, :], writes=("lnw",))
        add(V, lambda e, dsl=dsl: e.tensor_copy(out=dsl, in_=lnw[0:33, 0:64]),
            ("lnw",), (dk,))
    prog.dma(lnw[0:1, 0:D], dr["cprojb_row"][:, :], writes=("lnw",))
    add(V, lambda e: e.tensor_copy(out=cpbrow[:, :], in_=lnw[0:1, 0:D]),
        ("lnw",), ("cpbrow",))
    for k in range(8):
        prog.dma(lnw[:, :], dr["cprojT"][k * 128:(k + 1) * 128, :], writes=("lnw",))
        add(V, lambda e, k=k: e.tensor_copy(out=cpw[:, k * D:(k + 1) * D],
                                            in_=lnw[:, :]), ("lnw",), ("cpw",))
    for ty in range(3):
        bd = bd2q if ty == 0 else bd2k
        bdk = "bd2q" if ty == 0 else "bd2k"
        for pr in range(8):
            prog.dma(lnw[:, :], dr["c_attn_w"][ty * 1024 + pr * 128:
                                               ty * 1024 + (pr + 1) * 128, :],
                     writes=("lnw",))
            add(V, lambda e: e.tensor_copy(out=shuf[:, 0:1024], in_=lnw[:, :]),
                ("lnw",), ("shuf",))
            for k in range(8):
                pk = f"c_pB{k % 2}"
                add(Tn, lambda e, k=k, bd=bd: e.matmul(
                    out=pB[:, (k % 2) * 512:(k % 2) * 512 + 64],
                    lhsT=shuf[:, k * 128:(k + 1) * 128],
                    rhs=bd[:, :], start=True, stop=True),
                    ("shuf", bdk), (pk,))
                gidx, hpar = pr // 2, pr % 2
                cbase = ((ty * 4 + gidx) * 8 + k) * 128 + hpar * 64
                add(V, lambda e, cbase=cbase, k=k: e.tensor_copy(
                    out=merg[:, cbase:cbase + 64],
                    in_=pB[:, (k % 2) * 512:(k % 2) * 512 + 64]),
                    (pk,), ("merg",))
            add(Tn, lambda e, ty=ty, pr=pr, bd=bd: e.matmul(
                out=pB[0:64, 512:513], lhsT=bd[:, :],
                rhs=cabb[:, ty * 8 + pr: ty * 8 + pr + 1],
                start=True, stop=True), (bdk, "cabb"), ("c_pB1",))
            add(V, lambda e, ty=ty, pr=pr: e.tensor_copy(
                out=latb[(pr % 2) * 64:(pr % 2) * 64 + 64,
                         ty * 4 + pr // 2: ty * 4 + pr // 2 + 1],
                in_=pB[0:64, 512:513]), ("c_pB1",), ("latb",))
    add(V, lambda e: e.tensor_tensor(
        out=latb[:, 0:4], in0=latb[:, 0:4],
        in1=qlb[:, 0:1].broadcast_to((128, 4)), op=OP.add),
        ("latb", "qlb"), ("latb",))
    add(V, lambda e: e.tensor_tensor(
        out=latb[:, 4:12], in0=latb[:, 4:12],
        in1=kvlb[:, 0:1].broadcast_to((128, 8)), op=OP.add),
        ("latb", "kvlb"), ("latb",))
    for kt in range(32):
        add(V, lambda e, kt=kt: e.memset(
            vT[:, kt * 528:(kt + 1) * 528].rearrange("p (pr c) -> p pr c", pr=8)
            [:, :, 32:34], 1.0), (), ("vT",))
    prog.barrier()

    def rope(dstfn, dstkeys):
        add(V, lambda e: e.stream_shuffle(out=shuf[:, :], in_=qlat[:, :],
                                          mask=_ROT), ("qlat",), ("shuf",))
        add(V, lambda e: e.tensor_tensor(out=qlat[:, :], in0=qlat[:, :],
                                         in1=cosb[:, :], op=OP.mult),
            ("qlat", "cosb"), ("qlat",))
        add(V, lambda e: e.tensor_tensor(out=shuf[:, :], in0=shuf[:, :],
                                         in1=sinb[:, :], op=OP.mult),
            ("shuf", "sinb"), ("shuf",))
        add(V, dstfn, ("qlat", "shuf", "cosb", "sinb"), dstkeys)

    for half in range(2):
        prog.dma(cosb[:, :], dr["rope_cos"][:, half * 2048:(half + 1) * 2048],
                 writes=("cosb",))
        prog.dma(sinb[:, :], dr["rope_sin"][:, half * 2048:(half + 1) * 2048],
                 writes=("sinb",))
        for it in range(16):
            i = half * 16 + it
            prog.dma(xa[:, :], dr["xio"][i * 128:(i + 1) * 128, :], writes=("xa0",))
            add(V, lambda e: e.bn_stats(out=st6[:, 0:6], in_=xa[:, 0:512]),
                ("xa0",), ("st6",))
            add(V, lambda e: e.bn_stats(out=st6[:, 8:14], in_=xa[:, 512:1024]),
                ("xa0",), ("st6",))
            add(V, lambda e: e.bn_aggr(out=agg[:, 0:2], in_=st6[:, 0:6]),
                ("st6",), ("agg",))
            add(V, lambda e: e.bn_aggr(out=agg[:, 4:6], in_=st6[:, 8:14]),
                ("st6",), ("agg",))
            add(V, lambda e: e.tensor_tensor(out=comb[:, 0:1], in0=agg[:, 0:1],
                                             in1=agg[:, 4:5], op=OP.add),
                ("agg",), ("comb",))
            add(V, lambda e: e.tensor_scalar(out=comb[:, 0:1], in0=comb[:, 0:1],
                                             scalar1=0.5, scalar2=None,
                                             op0=OP.mult), ("comb",), ("comb",))
            add(V, lambda e: e.tensor_tensor(out=comb[:, 1:2], in0=agg[:, 1:2],
                                             in1=agg[:, 5:6], op=OP.add),
                ("agg",), ("comb",))
            add(V, lambda e: e.tensor_tensor(out=comb[:, 2:3], in0=agg[:, 0:1],
                                             in1=agg[:, 4:5], op=OP.subtract),
                ("agg",), ("comb",))
            add(V, lambda e: e.tensor_tensor(out=comb[:, 2:3], in0=comb[:, 2:3],
                                             in1=comb[:, 2:3], op=OP.mult),
                ("comb",), ("comb",))
            add(V, lambda e: e.tensor_scalar(out=comb[:, 1:2], in0=comb[:, 1:2],
                                             scalar1=0.5, scalar2=None,
                                             op0=OP.mult), ("comb",), ("comb",))
            add(V, lambda e: e.scalar_tensor_tensor(
                out=comb[:, 3:4], in0=comb[:, 2:3], scalar=0.25, in1=comb[:, 1:2],
                op0=OP.mult, op1=OP.add), ("comb",), ("comb",))
            add(V, lambda e: e.tensor_scalar(out=comb[:, 3:4], in0=comb[:, 3:4],
                                             scalar1=EPS, scalar2=None,
                                             op0=OP.add), ("comb",), ("comb",))
            add(V, lambda e: e.reciprocal(out=comb[:, 3:4], in_=comb[:, 3:4]),
                ("comb",), ("comb",))
            add(S, lambda e: e.activation(out=comb[:, 4:5], in_=comb[:, 3:4],
                                          func=AF.Sqrt), ("comb",), ("comb",))
            add(V, lambda e: e.tensor_scalar(
                out=lnw[:, :], in0=xa[:, :], scalar1=comb[:, 0:1],
                scalar2=comb[:, 4:5], op0=OP.subtract, op1=OP.mult),
                ("xa0", "comb"), ("lnw",))
            for hk in range(2):
                for kkk in range(4):
                    k = hk * 4 + kkk
                    add(Tn, lambda e, k=k, kkk=kkk: e.transpose(
                        out=pB[:, kkk * 128:(kkk + 1) * 128],
                        in_=lnw[:, k * 128:(k + 1) * 128],
                        identity=g["ident"][:, :]),
                        ("lnw", "ident"), ("c_pB0", "c_pB1"))
                for kkk in range(4):
                    k = hk * 4 + kkk
                    add(V, lambda e, k=k, kkk=kkk, it=it: e.tensor_copy(
                        out=ln2t[:, k * 2048 + it * 128: k * 2048 + (it + 1) * 128],
                        in_=pB[:, kkk * 128:(kkk + 1) * 128]),
                        ("c_pB0", "c_pB1"), ("ln2t",))
        for ty in (1, 2):
            for gi in range(4):
                for qb in range(4):
                    pk = f"c_pB{qb % 2}"
                    for k in range(8):
                        add(Tn, lambda e, ty=ty, gi=gi, qb=qb, k=k: e.matmul(
                            out=pB[:, (qb % 2) * 512:(qb % 2) * 512 + 512],
                            lhsT=merg[:, ((ty * 4 + gi) * 8 + k) * 128:
                                      ((ty * 4 + gi) * 8 + k + 1) * 128],
                            rhs=ln2t[:, k * 2048 + qb * 512:
                                     k * 2048 + (qb + 1) * 512],
                            start=(k == 0), stop=(k == 7)),
                            ("merg", "ln2t"), (pk,))
                    bcol = (4 + gi) if ty == 1 else (8 + gi)
                    dstb = qlat if ty == 1 else shuf
                    dk = "qlat" if ty == 1 else "shuf"
                    add(V, lambda e, qb=qb, bcol=bcol, dstb=dstb: e.tensor_scalar(
                        out=dstb[:, qb * 512:(qb + 1) * 512],
                        in0=pB[:, (qb % 2) * 512:(qb % 2) * 512 + 512],
                        scalar1=latb[:, bcol:bcol + 1], scalar2=None,
                        op0=OP.add), (pk, "latb"), (dk,))
                if ty == 1:
                    add(V, lambda e: e.stream_shuffle(out=shuf[:, :],
                                                      in_=qlat[:, :], mask=_ROT),
                        ("qlat",), ("shuf",))
                    add(V, lambda e: e.tensor_tensor(out=qlat[:, :], in0=qlat[:, :],
                                                     in1=cosb[:, :], op=OP.mult),
                        ("qlat", "cosb"), ("qlat",))
                    add(V, lambda e: e.tensor_tensor(out=shuf[:, :], in0=shuf[:, :],
                                                     in1=sinb[:, :], op=OP.mult),
                        ("shuf", "sinb"), ("shuf",))
                    add(V, lambda e, gi=gi, half=half: e.tensor_tensor(
                        out=krope[:, gi * 4096 + half * 2048:
                                  gi * 4096 + (half + 1) * 2048],
                        in0=qlat[:, :], in1=shuf[:, :], op=OP.add),
                        ("qlat", "shuf"), ("krope",))
                else:
                    for ts_ in range(16):
                        add(Tn, lambda e, ts_=ts_: e.transpose(
                            out=pC[:, ts_ * 128:(ts_ + 1) * 128],
                            in_=shuf[:, ts_ * 128:(ts_ + 1) * 128],
                            identity=g["identb"][:, :]),
                            ("shuf", "identb"), ("c_pC0", "c_pC1"))
                    for ts_ in range(16):
                        kt = half * 16 + ts_
                        vv = vT[:, kt * 528 + gi * 132: kt * 528 + (gi + 1) * 132]
                        add(V, lambda e, ts_=ts_, vv=vv: e.tensor_copy(
                            out=vv.rearrange("p (pr c) -> p pr c", pr=2)[:, :, 0:32],
                            in_=pC[:, ts_ * 128:(ts_ + 1) * 128]
                            .rearrange("p (h l) -> p h l", h=4)[:, 0::2, :]),
                            ("c_pC0", "c_pC1"), ("vT",))
                        add(V, lambda e, ts_=ts_, vv=vv: e.tensor_copy(
                            out=vv.rearrange("p (pr c) -> p pr c", pr=2)[:, :, 34:66],
                            in_=pC[:, ts_ * 128:(ts_ + 1) * 128]
                            .rearrange("p (h l) -> p h l", h=4)[:, 1::2, :]),
                            ("c_pC0", "c_pC1"), ("vT",))
        for gi in range(4):
            for qb in range(4):
                pk = f"c_pB{qb % 2}"
                for k in range(8):
                    add(Tn, lambda e, gi=gi, qb=qb, k=k: e.matmul(
                        out=pB[:, (qb % 2) * 512:(qb % 2) * 512 + 512],
                        lhsT=merg[:, (gi * 8 + k) * 128:(gi * 8 + k + 1) * 128],
                        rhs=ln2t[:, k * 2048 + qb * 512: k * 2048 + (qb + 1) * 512],
                        start=(k == 0), stop=(k == 7)),
                        ("merg", "ln2t"), (pk,))
                add(V, lambda e, gi=gi, qb=qb: e.tensor_scalar(
                    out=qlat[:, qb * 512:(qb + 1) * 512],
                    in0=pB[:, (qb % 2) * 512:(qb % 2) * 512 + 512],
                    scalar1=latb[:, gi:gi + 1], scalar2=None, op0=OP.add),
                    (pk, "latb"), ("qlat",))
            rope(lambda e: e.tensor_tensor(out=qlat[:, :], in0=qlat[:, :],
                                           in1=shuf[:, :], op=OP.add), ("qlat",))
            for qb in range(4):
                pk = f"c_pB{qb % 2}"
                add(Tn, lambda e, qb=qb: e.matmul(
                    out=pB[:, (qb % 2) * 512:(qb % 2) * 512 + 512],
                    lhsT=bd4[:, :], rhs=qlat[:, qb * 512:(qb + 1) * 512],
                    start=True, stop=True), ("bd4", "qlat"), (pk,))
                add(V, lambda e, gi=gi, qb=qb: e.tensor_scalar(
                    out=qfk[:, gi * 2048 + qb * 512:
                            gi * 2048 + (qb + 1) * 512],
                    in0=pB[:, (qb % 2) * 512:(qb % 2) * 512 + 512],
                    scalar1=wqkb[:, 0:1], scalar2=0.1767766952966369,
                    op0=OP.add, op1=OP.mult), (pk, "wqkb"), ("qfk",))
        for qb in range(4):
            qg = half * 4 + qb
            for gi in range(4):
                nkt = (qg + 1) * 4
                for kt in range(nkt):
                    pp = 0
                    j = kt - qg * 4
                    for hl in range(4):
                        add(Tn, lambda e, gi=gi, kt=kt, hl=hl, qb=qb: e.matmul(
                            out=pA[:, hl * 512:(hl + 1) * 512],
                            lhsT=krope[32 * hl:32 * hl + 32,
                                       gi * 4096 + kt * 128:
                                       gi * 4096 + (kt + 1) * 128],
                            rhs=qfk[32 * hl:32 * hl + 32,
                                    gi * 2048 + qb * 512:
                                    gi * 2048 + (qb + 1) * 512],
                            start=True, stop=(j < 0),
                            tile_position=(32 * hl, 0)),
                            ("krope", "qfk"), (f"c_pA{hl}",))
                        if j >= 0:
                            add(Tn, lambda e, hl=hl, j=j: e.matmul(
                                out=pA[:, hl * 512:(hl + 1) * 512],
                                lhsT=nbe[:, 0:128],
                                rhs=qm[:, j * 512:(j + 1) * 512],
                                start=False, stop=True),
                                ("qm", "nbe"), (f"c_pA{hl}",))
                    add(S, lambda e: e.activation(
                        out=pts[:, :], in_=pA[:, :], func=AF.Exp),
                        ("c_pA0", "c_pA1", "c_pA2", "c_pA3"), ("qlat",))
                    for pr in range(2):
                        pglob = gi * 2 + pr
                        for ab in range(2):
                            out_sl = (pB[0:33, pr * 512:(pr + 1) * 512] if ab == 0
                                      else pB[64:97, pr * 512:(pr + 1) * 512])
                            add(Tn, lambda e, kt=kt, pglob=pglob, pr=pr, ab=ab,
                                pp=pp, out_sl=out_sl, nkt=nkt: e.matmul(
                                out=out_sl,
                                lhsT=vT[:, kt * 528 + pglob * 66 + ab * 33:
                                        kt * 528 + pglob * 66 + (ab + 1) * 33],
                                rhs=pts[:, (pr * 2 + ab) * 512:
                                        (pr * 2 + ab + 1) * 512],
                                start=(kt == 0 and ab == 0),
                                stop=(kt == nkt - 1 and ab == 1),
                                tile_position=(0, 64 * ab)),
                                ("qlat", "vT"), (f"c_pB{pr}",))
                for pr in range(2):
                    add(V, lambda e, pr=pr: e.tensor_copy(
                        out=ylat[:, 0:512],
                        in_=pB[:, pr * 512:(pr + 1) * 512]),
                        (f"c_pB{pr}",), ("ylat",))
                    add(Tn, lambda e, pr=pr: e.matmul(
                        out=pA[0:64, pr * 512:(pr + 1) * 512],
                        lhsT=l2oA[:, :], rhs=ylat[0:33, 0:512],
                        start=True, stop=True),
                        ("l2oA", "ylat"), (f"c_pA{pr}",))
                    add(Tn, lambda e, pr=pr: e.matmul(
                        out=pA[64:128, pr * 512:(pr + 1) * 512],
                        lhsT=l2oB[64:97, :],
                        rhs=ylat[64:97, 0:512],
                        start=True, stop=True, tile_position=(64, 64)),
                        ("l2oB", "ylat"), (f"c_pA{pr}",))
                    add(V, lambda e, pr=pr: e.reciprocal(
                        out=invd[0:1, 0:512],
                        in_=ylat[32:33, 0:512]),
                        ("ylat",), ("invd",))
                    add(Tn, lambda e, pr=pr: e.matmul(
                        out=pB[0:64, pr * 512:(pr + 1) * 512],
                        lhsT=g["ones_row"][0:1, 0:64],
                        rhs=invd[0:1, 0:512],
                        start=True, stop=True),
                        ("invd", "ones_row"), (f"c_pB{pr}",))
                    add(V, lambda e, pr=pr: e.reciprocal(
                        out=invd[0:1, 0:512],
                        in_=ylat[64:65, 0:512]),
                        ("ylat",), ("invd",))
                    add(Tn, lambda e, pr=pr: e.matmul(
                        out=pB[64:128, pr * 512:(pr + 1) * 512],
                        lhsT=g["ones_row"][0:1, 0:64],
                        rhs=invd[0:1, 0:512],
                        start=True, stop=True, tile_position=(0, 64)),
                        ("invd", "ones_row"), (f"c_pB{pr}",))
                    add(S, lambda e, pr=pr: e.copy(
                        out=invdbc[:, :],
                        in_=pB[:, pr * 512:(pr + 1) * 512]),
                        (f"c_pB{pr}",), ("qlat",))
                    add(V, lambda e, pr=pr, gi=gi: e.tensor_tensor(
                        out=Yq[:, (gi * 2 + pr) * 512:(gi * 2 + pr + 1) * 512],
                        in0=pA[:, pr * 512:(pr + 1) * 512],
                        in1=invdbc[:, :], op=OP.mult),
                        (f"c_pA{pr}", "qlat"), ("Yq",))
            for qs in range(4):
                i = qg * 4 + qs
                prog.dma(xa[:, :], dr["xio"][i * 128:(i + 1) * 128, :],
                         writes=("xa0",))
                for dc in range(8):
                    for jj in range(2):
                        add(Tn, lambda e, dc=dc, jj=jj, qs=qs: e.matmul(
                            out=pA[:, 1024 + jj * 512: 1024 + (jj + 1) * 512],
                            lhsT=Yq[:, dc * 512 + qs * 128:
                                    dc * 512 + (qs + 1) * 128],
                            rhs=cpw[:, dc * D + jj * 512: dc * D + (jj + 1) * 512],
                            start=(dc == 0), stop=False),
                            ("Yq", "cpw"), (f"c_pA{2 + jj}",))
                for jj in range(2):
                    add(Tn, lambda e, jj=jj: e.matmul(
                        out=pA[:, 1024 + jj * 512: 1024 + (jj + 1) * 512],
                        lhsT=g["onesb_row"][:, :],
                        rhs=cpbrow[:, jj * 512:(jj + 1) * 512],
                        start=False, stop=True),
                        ("cpbrow", "onesb_row"), (f"c_pA{2 + jj}",))
                add(V, lambda e: e.tensor_tensor(
                    out=xnew[:, :], in0=pA[:, 1024:2048], in1=xa[:, :],
                    op=OP.add), ("c_pA2", "c_pA3", "xa0"), ("lnw",))
                prog.dma(dr["xio"][i * 128:(i + 1) * 128, :], xnew[:, :],
                         reads=("lnw",), writes=(), eng="gpsimd")


def kernel(**inputs):
    sh, per_core = host_prep(inputs)
    nc = build_nc()
    in_maps = []
    for b in range(NCORE):
        m = dict(sh)
        m.update(per_core[b])
        in_maps.append(m)
    res = run_bass_kernel_spmd(nc, in_maps, core_ids=list(range(NCORE)))
    out = np.stack([np.asarray(r["xio"]) for r in res.results], axis=0)
    return out.astype(np.float32)
